# version 07
# baseline (speedup 1.0000x reference)
import numpy as np
import contextlib
import concourse.bass as bass
import concourse.mybir as mybir
from concourse.alu_op_type import AluOpType as OP
from concourse.bass_utils import run_bass_kernel_spmd

F32 = mybir.dt.float32
BF16 = mybir.dt.bfloat16
AF = mybir.ActivationFunctionType
AXX = mybir.AxisListType.X

H = 4
NCHIP = 8
T2 = 2048
TW = 256
WIN = 8
NWIN = 16
PP = 512
NPP = 8
D = 1024
FFI = 2816

ENGS = ("sync", "vector", "scalar", "tensor", "gpsimd")


class Prog:
    def __init__(self, nc, es):
        self.nc = nc
        self.q = {e: [] for e in ENGS}
        self.cnt = {}
        self.sem = {}
        for e in ENGS:
            for sfx in ("_t", "_d"):
                name = e + sfx
                self.sem[name] = es.enter_context(nc.semaphore(name))
                self.cnt[name] = 0
        self.seq = 0
        self.auto_drain = True
        self.fence_pending = {}
        self.fenced = {}

    def ins(self, eng, fn, inc=False, dma=False):
        tok = None
        if dma:
            name = eng + "_d"
            pend = self.fence_pending.get(name, 0)
            if pend > self.fenced.get(name, 0):
                self.q[eng].append(lambda E, name=name, v=pend: E.wait_ge(self.sem[name], v))
                self.fenced[name] = pend
            self.cnt[name] += 16
            tok = (name, self.cnt[name])
            def wrapped(E, fn=fn, name=name):
                fn(E).then_inc(self.sem[name], 16)
        elif inc:
            name = eng + "_t"
            self.cnt[name] += 1
            tok = (name, self.cnt[name])
            if eng in ("vector", "scalar", "gpsimd"):
                def wrapped(E, fn=fn, name=name):
                    fn(E)
                    E.drain().then_inc(self.sem[name], 1)
            else:
                def wrapped(E, fn=fn, name=name):
                    fn(E).then_inc(self.sem[name], 1)
        elif eng == "vector" and self.auto_drain:
            def wrapped(E, fn=fn):
                fn(E)
                E.drain()
        else:
            wrapped = fn
        self.q[eng].append(wrapped)
        self.seq += 1
        return tok

    def wait(self, eng, tok):
        if tok is None:
            return
        name, val = tok
        if name.endswith("_d"):
            val = self.cnt[name]
            self.fence_pending[name] = max(self.fence_pending.get(name, 0), val)
        self.q[eng].append(lambda E, name=name, val=val: E.wait_ge(self.sem[name], val))

    def waits(self, eng, toks):
        best = {}
        for t in toks:
            if t is None:
                continue
            n, v = t
            if n.endswith("_d"):
                v = self.cnt[n]
                self.fence_pending[n] = max(self.fence_pending.get(n, 0), v)
            best[n] = max(best.get(n, 0), v)
        for n, v in best.items():
            self.q[eng].append(lambda E, n=n, v=v: E.wait_ge(self.sem[n], v))

    def token(self, eng):
        if eng in ("vector", "scalar", "gpsimd"):
            name = eng + "_t"
            self.cnt[name] += 1
            tok = (name, self.cnt[name])
            self.q[eng].append(lambda E, name=name: E.drain().then_inc(self.sem[name], 1))
            self.seq += 1
            return tok
        return self.ins(eng, lambda E: E.nop(), inc=True)

    def barrier(self, engs=ENGS):
        toks = [self.token(e) for e in engs]
        for e in engs:
            self.waits(e, [t for t in toks])

    def finalize(self, block):
        for e in ENGS:
            fns = self.q[e]
            def runner(E, fns=fns):
                for f in fns:
                    f(E)
            getattr(block, e)(runner)


def bcast_ap(ap, inner):
    return bass.AP(tensor=ap.tensor, offset=ap.offset,
                   ap=[list(q) for q in ap.ap] + [[0, inner]])


def rep_ap(ap, n, step=0):
    a = [list(q) for q in ap.ap]
    return bass.AP(tensor=ap.tensor, offset=ap.offset, ap=[a[0]] + [[step, n]] + a[1:])



def make_consts():
    ident = np.eye(128, dtype=np.float32)
    ones128 = np.ones((128, 128), np.float32)
    sel4 = np.zeros((4, 512), np.float32)
    for h in range(4):
        sel4[h, h * 128:(h + 1) * 128] = 1.0
    r = np.arange(32)
    maskL32 = np.tile((r[:, None] > r[None, :]).astype(np.float32), (4, 1))
    maskU32 = np.tile((r[:, None] < r[None, :]).astype(np.float32), (4, 1))
    i32 = np.tile(np.eye(32, dtype=np.float32), (4, 1))
    a = np.arange(256)
    ca, cb = a[:, None] // 32, a[None, :] // 32
    strict = (ca < cb).astype(np.float32)
    diag = ((ca == cb) & (a[:, None] % 32 <= a[None, :] % 32)).astype(np.float32)
    return dict(ident=ident, ones128=ones128, sel4=sel4, maskl32=maskL32, masku32=maskU32,
                i32=i32, maskl32r=np.tile(maskL32, (1, 8)), masku32r=np.tile(maskU32, (1, 8)),
                i32r=np.tile(i32, (1, 8)),
                gstrict0=strict[:128], gstrict1=strict[128:],
                gdiag0=diag[:128], gdiag1=diag[128:])


def shard_inputs(I):
    x = np.asarray(I["L0_attn_hidden_states"], np.float32)
    consts = make_consts()
    maps = []
    for c in range(NCHIP):
        b, half = c // 2, c % 2
        x12 = np.concatenate([x[b, 0:2048], x[b, 2048 * half: 2048 * half + 2048]], 0)
        m = {
            "x12": x12,
            "wq": I["L0_attn_Wq"], "wk": I["L0_attn_Wk"], "wv": I["L0_attn_Wv"],
            "wb": I["L0_attn_Wb"],
            "convq": I["L0_attn_conv_q"], "convk": I["L0_attn_conv_k"],
            "convv": I["L0_attn_conv_v"],
            "fir_s": np.asarray(I["L0_attn_fir_short_w"], np.float32).reshape(1024, 3),
            "fir_l": np.asarray(I["L0_attn_fir_long_w"], np.float32).reshape(1024, 63),
            "fparam": np.asarray(I["L0_attn_forget_param"], np.float32).reshape(4, 1),
            "ltemp": np.asarray(I["L0_attn_log_temp"], np.float32).reshape(4, 1),
            "floorp": np.asarray(I["L0_attn_floor_param"], np.float32).reshape(1, 16),
            "w1": I["L0_attn_W1"],
            "b1": np.asarray(I["L0_attn_b1"], np.float32).reshape(1, 2048),
            "b1c": np.asarray(I["L0_attn_b1"], np.float32).reshape(16, 128).T,
            "w2": I["L0_attn_W2"],
            "b2": np.asarray(I["L0_attn_b2"], np.float32).reshape(1, 16),
            "normw_t": np.tile(np.asarray(I["L0_attn_norm_w"], np.float32).reshape(1, 256), (128, 4)),
            "b2t": np.tile(np.asarray(I["L0_attn_b2"], np.float32).reshape(1, 16), (128, 1)),
            "wo": I["L0_attn_Wo"],
            "wg": I["L0_ffn_w_gate"], "wu": I["L0_ffn_w_up"], "wd": I["L0_ffn_w_down"],
            "flagv": np.full((128, 1), float(half), np.float32),
        }
        m.update(consts)
        maps.append({k: np.ascontiguousarray(np.asarray(v, np.float32)) for k, v in m.items()})
    return maps



def build_nc(upto="ALL", dbg=(), skip=(), p3stop=99):
    nc = bass.Bass("TRN2", target_bir_lowering=False, debug=False)
    es = contextlib.ExitStack()
    P = Prog(nc, es)

    def dram(name, shape, kind="Internal"):
        k = "ExternalOutput" if (name in dbg and kind == "Internal") else kind
        return nc.dram_tensor(name, shape, F32, kind=k).ap()

    IN = {}
    for name, shape in [
        ("x12", [4096, 1024]), ("wq", [1024, 1024]), ("wk", [1024, 1024]),
        ("wv", [1024, 1024]), ("wb", [1024, 4]),
        ("convq", [1024, 4]), ("convk", [1024, 4]), ("convv", [1024, 4]),
        ("fir_s", [1024, 3]), ("fir_l", [1024, 63]),
        ("fparam", [4, 1]), ("ltemp", [4, 1]), ("floorp", [1, 16]),
        ("w1", [1088, 2048]), ("b1", [1, 2048]), ("b1c", [128, 16]), ("w2", [2048, 16]), ("b2", [1, 16]),
        ("normw_t", [128, 1024]), ("b2t", [128, 16]), ("wo", [1024, 1024]),
        ("wg", [2816, 1024]), ("wu", [2816, 1024]), ("wd", [1024, 2816]),
        ("flagv", [128, 1]), ("ident", [128, 128]), ("ones128", [128, 128]), ("sel4", [4, 512]),
        ("maskl32", [128, 32]), ("masku32", [128, 32]), ("i32", [128, 32]),
        ("maskl32r", [128, 256]), ("masku32r", [128, 256]), ("i32r", [128, 256]),
        ("gstrict0", [128, 256]), ("gstrict1", [128, 256]),
        ("gdiag0", [128, 256]), ("gdiag1", [128, 256]),
    ]:
        IN[name] = dram(name, shape, "ExternalInput")

    out_h = dram("out", [2048, 1024], "ExternalOutput")

    khat_s = dram("khat_s", [64, 128, 512])
    v_s = dram("v_s", [64, 128, 512])
    qhat_s = dram("qhat_s", [32, 128, 512])
    xnT_s = dram("xnT_s", [32, 128, 512])
    xqT_s = dram("xqT_s", [32, 128, 512])
    beta_s = dram("beta_s", [8, 4, 512])
    xn_dbg = dram("xn_dbg", [128, 4096])
    ln_dbg = dram("ln_dbg", [128, 16])
    xtm_dbg = dram("xtm_dbg", [128, 1024])
    xnT_dbg = dram("xnT_dbg", [128, 4096])
    delta_s = dram("delta_s", [2048, 1024])
    y1_s = dram("y1_s", [2048, 1024])
    wgT_s = dram("wgT_s", [8, 128, 2816])
    wuT_s = dram("wuT_s", [8, 128, 2816])
    wdT_s = dram("wdT_s", [22, 128, 1024])
    hqT_s = dram("hqT_s", [22, 128, 2048])

    sb = lambda name, p, f, dt=F32: es.enter_context(nc.sbuf_tensor(name, [p, f], dt))

    ident = sb("identt", 128, 128)
    ones128 = sb("ones128t", 128, 128)
    sel4 = sb("sel4t", 4, 512)
    flag = sb("flagt", 128, 1)
    maskl32 = sb("maskl32t", 128, 32)
    masku32 = sb("masku32t", 128, 32)
    i32 = sb("i32t", 128, 32)
    maskl32r = sb("maskl32rt", 128, 256)
    masku32r = sb("masku32rt", 128, 256)
    i32r = sb("i32rt", 128, 256)
    convw = {t: sb("convw" + t, 128, 8 * 4) for t in "qkv"}
    firw_s = sb("firw_s", 128, 8 * 3)
    firw_l = sb("firw_l", 128, 8 * 63)
    lam_t = sb("lam", 4, 64)
    lamR = sb("lamR", 4, 512)
    rampPT = sb("rampPT", 128, 8)
    rampNT = sb("rampNT", 128, 8)
    lamcol = sb("lamcol", 128, 16)
    beta_f = sb("beta_f", 4, 512)
    wbt = sb("wbt", 128, 8 * 4)
    misc = sb("misct", 128, 512)
    row1 = sb("row1", 1, 2048)

    psum = lambda name, p, f: es.enter_context(nc.psum_tensor(name, [p, f], F32))

    block = es.enter_context(nc.Block())

    def ld(dst, src, eng="sync"):
        return P.ins(eng, lambda E, d=dst, s=src: E.dma_start(out=d, in_=s), dma=True)

    def st(dst, src, eng="gpsimd"):
        return P.ins(eng, lambda E, d=dst, s=src: E.dma_start(out=d, in_=s), dma=True)

    tl = []
    for n, t in [("ident", ident), ("ones128", ones128), ("sel4", sel4), ("flagv", flag),
                 ("maskl32", maskl32), ("masku32", masku32), ("i32", i32),
                 ("maskl32r", maskl32r), ("masku32r", masku32r), ("i32r", i32r)]:
        tl.append(ld(t[:], IN[n]))
    for t in "qkv":
        for ci in range(8):
            tl.append(ld(convw[t][:, ci * 4:(ci + 1) * 4], IN["conv" + t][ci * 128:(ci + 1) * 128, :]))
    for ci in range(8):
        tl.append(ld(wbt[:, ci * 4:(ci + 1) * 4], IN["wb"][ci * 128:(ci + 1) * 128, :]))
        tl.append(ld(firw_s[:, ci * 3:(ci + 1) * 3], IN["fir_s"][ci * 128:(ci + 1) * 128, :]))
        tl.append(ld(firw_l[:, ci * 63:(ci + 1) * 63], IN["fir_l"][ci * 128:(ci + 1) * 128, :]))
    tl.append(ld(lam_t[0:4, 0:1], IN["fparam"]))
    gd_t = [sb(f"gdt{i}", 128, 256) for i in range(2)]
    for i in range(2):
        tl.append(ld(gd_t[i][:], IN["gdiag%d" % i]))

    P.waits("scalar", tl)
    P.ins("scalar", lambda E: E.activation(lam_t[0:4, 1:2], lam_t[0:4, 0:1], AF.Sigmoid))
    ts_ = P.token("scalar")
    P.waits("vector", tl + [ts_])
    V = lambda fn, inc=False: P.ins("vector", fn, inc=inc)
    V(lambda E: E.tensor_scalar(lam_t[0:4, 2:3], lam_t[0:4, 1:2], 0.45, 0.55, op0=OP.mult, op1=OP.add))
    V(lambda E: E.memset(lam_t[0:4, 8:9], 1.0))
    V(lambda E: E.drain())
    for j in range(1, 9):
        V(lambda E, j=j: E.tensor_tensor(lam_t[0:4, 8 + j:9 + j], lam_t[0:4, 7 + j:8 + j], lam_t[0:4, 2:3], op=OP.mult))
    V(lambda E: E.reciprocal(lam_t[0:4, 3:4], lam_t[0:4, 2:3]))
    V(lambda E: E.drain())
    V(lambda E: E.memset(lam_t[0:4, 24:25], 1.0))
    V(lambda E: E.drain())
    for l in range(1, 8):
        V(lambda E, l=l: E.tensor_tensor(lam_t[0:4, 24 + l:25 + l], lam_t[0:4, 23 + l:24 + l], lam_t[0:4, 3:4], op=OP.mult))
    V(lambda E: E.tensor_scalar(lam_t[0:4, 5:6], lam_t[0:4, 3:4], -1.0, None, op0=OP.mult))
    for j in range(8):
        V(lambda E, j=j: E.tensor_copy(lamR[0:4, 32 * j:32 * j + 32], bcast_ap(lam_t[0:4, 8 + j:9 + j], 32)))
        V(lambda E, j=j: E.tensor_copy(lamR[0:4, 256 + 32 * j:256 + 32 * j + 32], bcast_ap(lam_t[0:4, 24 + j:25 + j], 32)))
    tlam = P.token("vector")

    psB0 = psum("psB0", 128, 512)
    psB1 = psum("psB1", 128, 512)
    P.wait("tensor", tlam)
    TT = lambda fn, inc=False: P.ins("tensor", fn, inc=inc)
    for h in range(H):
        sl = sel4[:, h * 128:(h + 1) * 128]
        TT(lambda E, sl=sl: E.matmul(psB1[:, 0:1], sl, lam_t[0:4, 5:6], start=True, stop=True))
        TT(lambda E, sl=sl: E.matmul(psB1[:, 1:2], sl, lam_t[0:4, 2:3], start=True, stop=True))
        TT(lambda E, sl=sl: E.matmul(psB1[:, 2:3], sl, lam_t[0:4, 15:16], start=True, stop=True))
        TT(lambda E, sl=sl: E.matmul(psB1[:, 3:4], sl, lam_t[0:4, 3:4], start=True, stop=True), inc=True)
        tmm = ("tensor_t", P.cnt["tensor_t"])
        P.wait("vector", tmm)
        V(lambda E, h=h: E.tensor_copy(lamcol[:, 4 * h:4 * h + 4], psB1[:, 0:4]), inc=True)
        tev = ("vector_t", P.cnt["vector_t"])
        P.wait("tensor", tev)
    tlam2 = P.token("vector")
    P.wait("tensor", tlam2)
    for tj in range(2):
        TT(lambda E, tj=tj: E.transpose(psB0[:, 0:4], lamR[0:4, tj * 128:tj * 128 + 128], ident[0:4, 0:4]))
        TT(lambda E, tj=tj: E.transpose(psB0[:, 4:8], lamR[0:4, 256 + tj * 128:256 + tj * 128 + 128], ident[0:4, 0:4]), inc=True)
        tmm = ("tensor_t", P.cnt["tensor_t"])
        P.wait("vector", tmm)
        V(lambda E, tj=tj: E.tensor_copy(rampPT[:, tj * 4:tj * 4 + 4], psB0[:, 0:4]))
        V(lambda E, tj=tj: E.tensor_copy(rampNT[:, tj * 4:tj * 4 + 4], psB0[:, 4:8]), inc=True)
        P.wait("tensor", ("vector_t", P.cnt["vector_t"]))

    P.barrier()
    if upto == "P0":
        P.finalize(block)
        es.close()
        return nc

    es2 = contextlib.ExitStack()
    sb2 = lambda name, p, f, dt=F32: es2.enter_context(nc.sbuf_tensor(name, [p, f], dt))

    wk_t = sb2("wk_t", 128, 8192)
    wv_t = sb2("wv_t", 128, 8192)
    wq_t = sb2("wq_t", 128, 8192)
    twl = []
    for name, t in [("wk", wk_t), ("wv", wv_t), ("wq", wq_t)]:
        for ki in range(8):
            twl.append(ld(t[:, ki * 1024:(ki + 1) * 1024], IN[name][ki * 128:(ki + 1) * 128, :]))

    x_tm = sb2("x_tm", 128, 1024)
    xn_tm = sb2("xn_tm", 128, 4 * 1024)
    xq_tm = sb2("xq_tm", 128, 1024)
    xnT = sb2("xnT", 128, 8 * 512)
    xqT = sb2("xqT", 128, 8 * 512)
    pslice = sb2("pslice", 128, 515)
    tails = {t: sb2("tails" + t, 128, 24) for t in "qkv"}
    kbuf = sb2("kbuf", 128, 1024)
    qbuf = sb2("qbuf", 128, 1024)
    vbuf = sb2("vbuf", 128, 512)
    sqr = sb2("sqr", 128, 1024)
    lncol = sb2("lncol", 128, 16)
    qrow = sb2("qrow", 128, 64)
    uq = sb2("uq", 128, 1024)
    gq = sb2("gq", 128, 1024)
    sgq = sb2("sgq", 128, 1024)

    psP = [psB0, psB1]
    psT = es2.enter_context(nc.psum_tensor("psT", [128, 512], F32))
    psR = es2.enter_context(nc.psum_tensor("psR", [4, 512], F32))
    psL = es2.enter_context(nc.psum_tensor("psL", [1, 512], F32))

    V(lambda E: E.memset(tails["q"][:], 0.0))

    V(lambda E: E.drain())
    V(lambda E: E.memset(tails["k"][:], 0.0))
    V(lambda E: E.drain())
    V(lambda E: E.memset(tails["v"][:], 0.0))
    V(lambda E: E.drain())
    tz = P.token("vector")
    P.waits("tensor", twl + [tz])
    P.waits("scalar", twl + [tz])

    xtm_free = [None]
    evtok = [None, None]
    for pp in range(NPP):
        own = pp >= 4
        for s in range(4):
            P.wait("sync", xtm_free[0])
            tx = ld(x_tm[:], IN["x12"][pp * 512 + s * 128: pp * 512 + (s + 1) * 128, :])
            P.wait("vector", tx)
            V(lambda E: E.tensor_reduce(lncol[:, 0:1], x_tm[:], axis=AXX, op=OP.add))
            V(lambda E: E.tensor_tensor(sqr[:], x_tm[:], x_tm[:], op=OP.mult))
            V(lambda E: E.drain())
            V(lambda E: E.tensor_reduce(lncol[:, 1:2], sqr[:], axis=AXX, op=OP.add))
            V(lambda E: E.tensor_scalar(lncol[:, 2:3], lncol[:, 0:1], 1.0 / 1024, None, op0=OP.mult))
            V(lambda E: E.tensor_scalar(lncol[:, 3:4], lncol[:, 1:2], 1.0 / 1024, None, op0=OP.mult))
            V(lambda E: E.tensor_tensor(lncol[:, 4:5], lncol[:, 2:3], lncol[:, 2:3], op=OP.mult))
            V(lambda E: E.tensor_tensor(lncol[:, 5:6], lncol[:, 3:4], lncol[:, 4:5], op=OP.subtract))
            V(lambda E: E.tensor_scalar(lncol[:, 5:6], lncol[:, 5:6], 1e-5, None, op0=OP.add), inc=True)
            P.wait("scalar", ("vector_t", P.cnt["vector_t"]))
            P.ins("scalar", lambda E: E.activation(lncol[:, 6:7], lncol[:, 5:6], AF.Sqrt), inc=True)
            P.wait("vector", ("scalar_t", P.cnt["scalar_t"]))
            V(lambda E: E.reciprocal(lncol[:, 7:8], lncol[:, 6:7]))
            V(lambda E: E.drain())
            V(lambda E: E.tensor_tensor(lncol[:, 8:9], lncol[:, 2:3], lncol[:, 7:8], op=OP.mult))
            V(lambda E: E.tensor_scalar(lncol[:, 9:10], lncol[:, 8:9], -1.0, None, op0=OP.mult))
            V(lambda E, s=s: E.tensor_scalar(
                xn_tm[:, s * 1024:(s + 1) * 1024], x_tm[:], lncol[:, 7:8], None, op0=OP.mult))
            V(lambda E, s=s: E.tensor_scalar(
                xn_tm[:, s * 1024:(s + 1) * 1024], xn_tm[:, s * 1024:(s + 1) * 1024],
                lncol[:, 9:10], None, op0=OP.add), inc=True)
            tln = ("vector_t", P.cnt["vector_t"])
            xtm_free[0] = tln
            if own and "quant" not in skip:
                xnt = xn_tm[:, s * 1024:(s + 1) * 1024]
                xv = xnt.rearrange("p (nb b) -> p nb b", b=16)
                V(lambda E: E.drain())
                V(lambda E, xv=xv: E.tensor_reduce(qrow[:], xv, axis=AXX, op=OP.max, apply_absolute_value=True))
                V(lambda E: E.tensor_scalar(qrow[:], qrow[:], 1.0 / 6, 1e-12, op0=OP.mult, op1=OP.max))
                V(lambda E: E.reciprocal(misc[:, 0:64], qrow[:]))
                V(lambda E: E.drain())
                rb = bcast_ap(misc[:, 0:64], 16)
                V(lambda E, xv=xv, rb=rb: E.tensor_tensor(uq[:].rearrange("p (nb b) -> p nb b", b=16), xv, rb, op=OP.mult))
                V(lambda E: E.tensor_scalar(sgq[:], uq[:], 0.0, None, op0=OP.is_gt))
                V(lambda E: E.tensor_scalar(gq[:], uq[:], 0.0, None, op0=OP.is_lt))
                V(lambda E: E.tensor_tensor(sgq[:], sgq[:], gq[:], op=OP.subtract))
                V(lambda E: E.tensor_tensor(uq[:], uq[:], sgq[:], op=OP.mult))
                V(lambda E: E.tensor_scalar(gq[:], uq[:], 0.25, None, op0=OP.is_gt))
                for thr in (0.75, 1.25, 1.75):
                    V(lambda E, thr=thr: E.scalar_tensor_tensor(gq[:], uq[:], thr, gq[:], op0=OP.is_gt, op1=OP.add))
                V(lambda E: E.tensor_scalar(gq[:], gq[:], 0.5, None, op0=OP.mult))
                for thr in (2.5, 3.5, 5.0, 5.0):
                    V(lambda E, thr=thr: E.scalar_tensor_tensor(gq[:], uq[:], thr, gq[:], op0=OP.is_gt, op1=OP.add))
                V(lambda E: E.tensor_tensor(gq[:], gq[:], sgq[:], op=OP.mult))
                qb = bcast_ap(qrow[:], 16)
                V(lambda E, qb=qb: E.tensor_tensor(
                    xq_tm[:].rearrange("p (nb b) -> p nb b", b=16),
                    gq[:].rearrange("p (nb b) -> p nb b", b=16), qb, op=OP.mult), inc=True)
                tln = ("vector_t", P.cnt["vector_t"])
            P.wait("tensor", tln)
            if "trans" in skip:
                continue
            srcs = [(xn_tm[:, s * 1024:(s + 1) * 1024], xnT)] + ([(xq_tm[:], xqT)] if own else [])
            for srct, dstt in srcs:
                for cg in range(2):
                    for cj in range(4):
                        ci = cg * 4 + cj
                        TT(lambda E, ci=ci, cj=cj, srct=srct: E.transpose(
                            psT[:, cj * 128:(cj + 1) * 128],
                            srct[:, ci * 128:(ci + 1) * 128], ident[:]), inc=(cj == 3))
                    tmm = ("tensor_t", P.cnt["tensor_t"])
                    P.wait("scalar", tmm)
                    for cj in range(4):
                        ci = cg * 4 + cj
                        P.ins("scalar", lambda E, s=s, ci=ci, cj=cj, dstt=dstt: E.copy(
                            dstt[:, ci * 512 + s * 128: ci * 512 + (s + 1) * 128],
                            psT[:, cj * 128:(cj + 1) * 128]), inc=(cj == 3))
                    P.wait("tensor", ("scalar_t", P.cnt["scalar_t"]))
        ttr = P.token("scalar")
        if own:
            P.wait("gpsimd", ttr)
            for ci in range(8):
                st(xnT_s[(pp - 4) * 8 + ci], xnT[:, ci * 512:(ci + 1) * 512])
                st(xqT_s[(pp - 4) * 8 + ci], xqT[:, ci * 512:(ci + 1) * 512])
        if pp == 4 and "xn_dbg" in dbg:
            P.wait("gpsimd", ttr)
            P.wait("gpsimd", P.token("vector"))
            st(xn_dbg, xn_tm[:])
            st(xnT_dbg, xnT[:])
            st(ln_dbg, lncol[:])
            st(xtm_dbg, x_tm[:])
            P.wait("vector", ("gpsimd_d", P.cnt["gpsimd_d"]))
            P.wait("scalar", ("gpsimd_d", P.cnt["gpsimd_d"]))
            P.wait("sync", ("gpsimd_d", P.cnt["gpsimd_d"]))

        if "rest" in skip:
            continue
        P.wait("tensor", ttr)
        for ki in range(8):
            TT(lambda E, ki=ki: E.matmul(psR[:], wbt[:, ki * 4:(ki + 1) * 4],
                                         xnT[:, ki * 512:(ki + 1) * 512],
                                         start=(ki == 0), stop=(ki == 7)), inc=(ki == 7))
        P.wait("scalar", ("tensor_t", P.cnt["tensor_t"]))
        P.ins("scalar", lambda E: E.activation(beta_f[0:4, :], psR[:], AF.Sigmoid), inc=True)
        tbeta = ("scalar_t", P.cnt["scalar_t"])
        P.wait("gpsimd", tbeta)
        st(beta_s[pp], beta_f[0:4, :])
        P.wait("tensor", tbeta)
        P.wait("scalar", ("gpsimd_d", P.cnt["gpsimd_d"]))

        if pp == 3:
            P.wait("tensor", evtok[0])
            for mi in range(8):
                for ki in range(8):
                    TT(lambda E, ki=ki, mi=mi: E.matmul(
                        psP[0][:, 0:3], wq_t[:, ki * 1024 + mi * 128: ki * 1024 + (mi + 1) * 128],
                        xnT[:, ki * 512 + 509: ki * 512 + 512],
                        start=(ki == 0), stop=(ki == 7)), inc=(ki == 7))
                P.wait("vector", ("tensor_t", P.cnt["tensor_t"]))
                V(lambda E, mi=mi: E.tensor_copy(tails["q"][:, mi * 3:(mi + 1) * 3], psP[0][:, 0:3]), inc=True)
                P.wait("tensor", ("vector_t", P.cnt["vector_t"]))

        if pp == 4:
            for tt in "qkv":
                V(lambda E, tt=tt: E.tensor_scalar(tails[tt][:], tails[tt][:], flag[:, 0:1], None, op0=OP.mult))
            tfl = P.token("vector")
            P.wait("tensor", tfl)

        tensors = ["k", "v"] + (["q"] if own else [])
        for t in tensors:
            wt = {"q": wq_t, "k": wk_t, "v": wv_t}[t]
            buf = {"q": qbuf, "k": kbuf, "v": vbuf}[t]
            for mi in range(8):
                pb = mi % 2
                P.wait("tensor", evtok[pb])
                for ki in range(8):
                    TT(lambda E, wt=wt, ki=ki, mi=mi, pb=pb: E.matmul(
                        psP[pb][:], wt[:, ki * 1024 + mi * 128: ki * 1024 + (mi + 1) * 128],
                        xnT[:, ki * 512:(ki + 1) * 512],
                        start=(ki == 0), stop=(ki == 7)), inc=(ki == 7))
                tmm = ("tensor_t", P.cnt["tensor_t"])
                P.wait("vector", tmm)
                V(lambda E, t=t, mi=mi: E.tensor_copy(pslice[:, 0:3], tails[t][:, mi * 3:(mi + 1) * 3]))
                V(lambda E, pb=pb: E.tensor_copy(pslice[:, 3:515], psP[pb][:]), inc=True)
                evtok[pb] = ("vector_t", P.cnt["vector_t"])
                V(lambda E, t=t, mi=mi: E.tensor_copy(tails[t][:, mi * 3:(mi + 1) * 3], pslice[:, 512:515]))
                cw = convw[t]
                bo = (mi % 2) * 512 if t in "kq" else 0
                V(lambda E, buf=buf, bo=bo, cw=cw, mi=mi: E.tensor_scalar(
                    buf[:, bo:bo + 512], pslice[:, 0:512],
                    cw[:, mi * 4 + 0: mi * 4 + 1], None, op0=OP.mult))
                for j in (1, 2, 3):
                    V(lambda E, buf=buf, bo=bo, cw=cw, mi=mi, j=j: E.scalar_tensor_tensor(
                        buf[:, bo:bo + 512], pslice[:, j: 512 + j],
                        cw[:, mi * 4 + j: mi * 4 + j + 1], buf[:, bo:bo + 512],
                        op0=OP.mult, op1=OP.add), inc=(j == 3))
                P.wait("scalar", ("vector_t", P.cnt["vector_t"]))
                P.ins("scalar", lambda E, buf=buf, bo=bo: E.activation(
                    buf[:, bo:bo + 512], buf[:, bo:bo + 512], AF.Silu), inc=True)
                tsil = ("scalar_t", P.cnt["scalar_t"])
                if t == "v":
                    P.wait("gpsimd", tsil)
                    st(v_s[pp * 8 + mi], vbuf[:, 0:512])
                    P.wait("vector", ("gpsimd_d", P.cnt["gpsimd_d"]))
                    P.wait("scalar", ("gpsimd_d", P.cnt["gpsimd_d"]))
                elif mi % 2 == 1:
                    h = mi // 2
                    P.wait("vector", tsil)
                    for c2 in range(2):
                        V(lambda E, buf=buf, c2=c2: E.tensor_tensor(
                            sqr[:, c2 * 512:(c2 + 1) * 512], buf[:, c2 * 512:(c2 + 1) * 512],
                            buf[:, c2 * 512:(c2 + 1) * 512], op=OP.mult), inc=(c2 == 1))
                    P.wait("tensor", ("vector_t", P.cnt["vector_t"]))
                    for c2 in range(2):
                        TT(lambda E, c2=c2: E.matmul(psL[0:1, :], ones128[:, 0:1],
                                                     sqr[:, c2 * 512:(c2 + 1) * 512],
                                                     start=(c2 == 0), stop=(c2 == 1)), inc=(c2 == 1))
                    P.wait("vector", ("tensor_t", P.cnt["tensor_t"]))
                    V(lambda E: E.tensor_scalar(row1[0:1, 0:512], psL[0:1, :], 1e-6, None, op0=OP.add), inc=True)
                    P.wait("scalar", ("vector_t", P.cnt["vector_t"]))
                    P.ins("scalar", lambda E: E.activation(row1[0:1, 0:512], row1[0:1, 0:512], AF.Sqrt), inc=True)
                    P.wait("vector", ("scalar_t", P.cnt["scalar_t"]))
                    V(lambda E: E.reciprocal(row1[0:1, 0:512], row1[0:1, 0:512]))
                    V(lambda E: E.drain(), inc=True)
                    P.wait("tensor", ("vector_t", P.cnt["vector_t"]))
                    TT(lambda E: E.matmul(psT[:], ones128[0:1, :], row1[0:1, 0:512],
                                          start=True, stop=True), inc=True)
                    P.wait("vector", ("tensor_t", P.cnt["tensor_t"]))
                    for c2 in range(2):
                        V(lambda E, buf=buf, c2=c2: E.tensor_tensor(
                            buf[:, c2 * 512:(c2 + 1) * 512], buf[:, c2 * 512:(c2 + 1) * 512],
                            psT[:], op=OP.mult), inc=(c2 == 1))
                    tl2 = ("vector_t", P.cnt["vector_t"])
                    P.wait("tensor", tl2)
                    P.wait("gpsimd", tl2)
                    dst = khat_s if t == "k" else qhat_s
                    base = pp * 8 if t == "k" else (pp - 4) * 8
                    st(dst[base + mi - 1], buf[:, 0:512])
                    st(dst[base + mi], buf[:, 512:1024])
                    P.wait("vector", ("gpsimd_d", P.cnt["gpsimd_d"]))
                    P.wait("scalar", ("gpsimd_d", P.cnt["gpsimd_d"]))

    P.barrier()
    es2.close()
    if upto == "P2":
        P.finalize(block)
        es.close()
        return nc

    es3 = contextlib.ExitStack()
    sb3 = lambda name, p, f, dt=F32: es3.enter_context(nc.sbuf_tensor(name, [p, f], dt))
    ps3 = lambda name, p, f: es3.enter_context(nc.psum_tensor(name, [p, f], F32))

    ghmask = [[sb3(f"ghm{h}_{i}", 128, 256) for i in range(2)] for h in range(H)]
    lamPb = [sb3(f"lamPb{h}", 128, 256) for h in range(H)]
    lamNb = [sb3(f"lamNb{h}", 128, 256) for h in range(H)]
    kw = sb3("kw", 128, 8 * 256)
    vw = sb3("vw", 128, 8 * 256)
    qw = sb3("qw", 128, 8 * 256)
    kbf = sb3("kbf", 128, 8 * 256)
    knf = sb3("knf", 128, 8 * 256)
    qtf = sb3("qtf", 128, 8 * 256)
    kT = sb3("kT", 128, 2048)
    vT = sb3("vT", 128, 2048)
    kbT = sb3("kbT", 128, 2048)
    ktT = sb3("ktT", 128, 2048)
    vbT = sb3("vbT", 128, 2048)
    bw = sb3("bw", 4, 256)
    bT = sb3("bT", 128, 8)
    colsc = sb3("colsc", 128, 8)
    Lst = sb3("Lst", 128, 256)
    Ust = sb3("Ust", 128, 256)
    Dst = sb3("Dst", 128, 256)
    Lp = sb3("Lp", 128, 256)
    Up = sb3("Up", 128, 256)
    DfinPad = sb3("DfinPad", 128, 1024)
    ut = sb3("ut", 128, 2048)
    wf = sb3("wf", 128, 2048)
    WSt = sb3("WSt", 128, 2048)
    QSt = sb3("QSt", 128, 2048)
    PTt = sb3("PTt", 128, 512)
    GTt = sb3("GTt", 128, 512)
    upb = sb3("upb", 128, 2048)
    S_t = sb3("S_t", 128, 2048)
    osb = sb3("osb", 128, 256)
    psW0 = ps3("psW0", 128, 512)
    psW1 = ps3("psW1", 128, 512)
    psW2 = ps3("psW2", 128, 512)
    psW3 = ps3("psW3", 128, 1024)

    V(lambda E: E.memset(S_t[:], 0.0))
    V(lambda E: E.drain())
    for h in range(H):
        sl = sel4[:, h * 128:(h + 1) * 128]
        TT(lambda E, sl=sl: E.matmul(psW0[:, 0:256], sl, lamR[0:4, 0:256], start=True, stop=True))
        TT(lambda E, sl=sl: E.matmul(psW0[:, 256:512], sl, lamR[0:4, 256:512], start=True, stop=True), inc=True)
        P.wait("vector", ("tensor_t", P.cnt["tensor_t"]))
        V(lambda E, h=h: E.tensor_copy(lamPb[h][:], psW0[:, 0:256]))
        V(lambda E, h=h: E.tensor_copy(lamNb[h][:], psW0[:, 256:512]), inc=True)
        P.wait("tensor", ("vector_t", P.cnt["vector_t"]))
    tgl = []
    for h in range(H):
        for i in range(2):
            tgl.append(ld(ghmask[h][i][:], IN["gstrict%d" % i]))
    P.waits("vector", tgl)
    for h in range(H):
        for i in range(2):
            V(lambda E, h=h, i=i: E.scalar_tensor_tensor(
                ghmask[h][i][:], ghmask[h][i][:], lamcol[:, 4 * h + 3:4 * h + 4], gd_t[i][:],
                op0=OP.mult, op1=OP.add))
    tpre = P.token("vector")
    P.waits("tensor", [tpre])
    P.waits("sync", [tpre])

    nwin_run = 1 if upto == "P3w1" else NWIN
    wqbd = [sb3("wqbA", 128, 512), sb3("wqbB", 128, 512), sb3("wqbC", 128, 512)]
    uqad = [sb3("uqaA", 128, 512), sb3("uqaB", 128, 512)]
    gqa = sb3("gqa", 128, 512)
    sga = sb3("sga", 128, 512)
    qra = sb3("qra", 128, 32)
    rca = sb3("rca", 128, 32)
    psA5 = ps3("psA5", 128, 512)

    def emit_quant_a(buf, K):
        nb = K // 16
        bv = buf.rearrange("p (nb b) -> p nb b", b=16)
        V(lambda E: E.drain())
        V(lambda E, bv=bv, nb=nb: E.tensor_reduce(qra[:, 0:nb], bv, axis=AXX, op=OP.max, apply_absolute_value=True))
        V(lambda E, nb=nb: E.tensor_scalar(qra[:, 0:nb], qra[:, 0:nb], 1.0 / 6, 1e-12, op0=OP.mult, op1=OP.max))
        V(lambda E, nb=nb: E.reciprocal(rca[:, 0:nb], qra[:, 0:nb]))
        V(lambda E: E.drain())
        rb = bcast_ap(rca[:, 0:nb], 16)
        V(lambda E, buf=buf, bv=bv, rb=rb, K=K: E.tensor_tensor(buf.rearrange("p (nb b) -> p nb b", b=16), bv, rb, op=OP.mult))
        V(lambda E, buf=buf, K=K: E.tensor_scalar(sga[:, 0:K], buf[:, 0:K], 0.0, None, op0=OP.is_gt))
        V(lambda E, buf=buf, K=K: E.tensor_scalar(gqa[:, 0:K], buf[:, 0:K], 0.0, None, op0=OP.is_lt))
        V(lambda E, K=K: E.tensor_tensor(sga[:, 0:K], sga[:, 0:K], gqa[:, 0:K], op=OP.subtract))
        V(lambda E, buf=buf, K=K: E.tensor_tensor(buf[:, 0:K], buf[:, 0:K], sga[:, 0:K], op=OP.mult))
        V(lambda E, buf=buf, K=K: E.tensor_scalar(gqa[:, 0:K], buf[:, 0:K], 0.25, None, op0=OP.is_gt))
        for thr in (0.75, 1.25, 1.75):
            V(lambda E, buf=buf, K=K, thr=thr: E.scalar_tensor_tensor(gqa[:, 0:K], buf[:, 0:K], thr, gqa[:, 0:K], op0=OP.is_gt, op1=OP.add))
        V(lambda E, K=K: E.tensor_scalar(gqa[:, 0:K], gqa[:, 0:K], 0.5, None, op0=OP.mult))
        for thr in (2.5, 3.5, 5.0, 5.0):
            V(lambda E, buf=buf, K=K, thr=thr: E.scalar_tensor_tensor(gqa[:, 0:K], buf[:, 0:K], thr, gqa[:, 0:K], op0=OP.is_gt, op1=OP.add))
        V(lambda E, K=K: E.tensor_tensor(gqa[:, 0:K], gqa[:, 0:K], sga[:, 0:K], op=OP.mult))
        qb = bcast_ap(qra[:, 0:nb], 16)
        V(lambda E, buf=buf, K=K, qb=qb: E.tensor_tensor(
            buf.rearrange("p (nb b) -> p nb b", b=16),
            gqa[:, 0:K].rearrange("p (nb b) -> p nb b", b=16), qb, op=OP.mult), inc=True)
        return ("vector_t", P.cnt["vector_t"])

    p5a_jobs = []
    for (wname, wtgt, nrc, KT) in (("wg", wgT_s, 22, 1024), ("wu", wuT_s, 22, 1024), ("wd", wdT_s, 8, 2816)):
        for rc in range(nrc):
            c0 = 0
            while c0 < KT:
                K = min(512, KT - c0)
                p5a_jobs.append((wname, wtgt, rc, c0, K))
                c0 += K
    NJOB = len(p5a_jobs)
    p5c = {"ld": 0, "q": 0, "tr": 0}
    for _ps in ("p5at0", "p5at1", "w2s"):
        P.sem[_ps] = es.enter_context(nc.semaphore(_ps))
        P.cnt[_ps] = 0
    t_ld = {}
    t_q = {}
    t_tr = {}
    t_ev = {}
    t_sp = {}

    def p5a_slot(drain=False):
        if p5c["tr"] < p5c["q"] - (0 if drain else 1):
            n = p5c["tr"]
            (wname, wtgt, rc, c0, K) = p5a_jobs[n]
            P.wait("tensor", t_q[n])
            P.wait("tensor", t_ev.get(n - 1))
            sname = "p5at%d" % (n % 2)
            for kc in range(K // 128):
                last = kc == K // 128 - 1
                if last:
                    P.cnt[sname] += 1
                    P.ins("tensor", lambda E, kc=kc, n=n, sname=sname: E.transpose(
                        psA5[:, kc * 128:(kc + 1) * 128],
                        wqbd[n % 3][:, kc * 128:(kc + 1) * 128], ident[:]).then_inc(P.sem[sname], 1))
                else:
                    TT(lambda E, kc=kc, n=n: E.transpose(psA5[:, kc * 128:(kc + 1) * 128],
                                                         wqbd[n % 3][:, kc * 128:(kc + 1) * 128], ident[:]))
            t_tr[n] = (sname, P.cnt[sname])
            P.wait("scalar", t_tr[n])
            P.wait("scalar", t_sp.get(n - 2))
            for kc in range(K // 128):
                P.ins("scalar", lambda E, kc=kc, n=n: E.copy(
                    uqad[n % 2][:, kc * 128:(kc + 1) * 128], psA5[:, kc * 128:(kc + 1) * 128]),
                    inc=(kc == K // 128 - 1))
            t_ev[n] = ("scalar_t", P.cnt["scalar_t"])
            for kc in range(K // 128):
                t_sp[n] = st(wtgt[c0 // 128 + kc][:, rc * 128:(rc + 1) * 128],
                             uqad[n % 2][:, kc * 128:(kc + 1) * 128], eng="scalar")
            p5c["tr"] = n + 1
        if p5c["ld"] < NJOB and p5c["ld"] - p5c["tr"] < 3:
            m = p5c["ld"]
            (wname, wtgt, rc, c0, K) = p5a_jobs[m]
            t_ld[m] = ld(wqbd[m % 3][:, 0:K], IN[wname][rc * 128:(rc + 1) * 128, c0:c0 + K], eng="scalar")
            p5c["ld"] = m + 1
        if p5c["q"] < p5c["ld"]:
            p = p5c["q"]
            (wname, wtgt, rc, c0, K) = p5a_jobs[p]
            P.wait("vector", t_ld[p])
            t_q[p] = emit_quant_a(wqbd[p % 3][:, 0:K], K)
            p5c["q"] = p + 1

    rot_banks = [psB0, psB1, psW0]
    rot = {"i": 0, "ev": [None, None, None]}

    def rot_acquire():
        i = rot["i"]
        rot["i"] += 1
        P.wait("tensor", rot["ev"][i % 3])
        return rot_banks[i % 3]

    def rot_release(tok):
        rot["ev"][(rot["i"] - 1) % 3] = tok

    def rot_fence():
        P.waits("tensor", [t for t in rot["ev"] if t is not None])

    rot["ev"] = [tpre, tpre, tpre]
    win_free = [None]
    for wi in range(nwin_run):
        own = wi >= NWIN // 2
        p4 = wi // 2
        off = (wi % 2) * 256
        W0 = wi * 256

        if wi == NWIN // 2:
            V(lambda E: E.tensor_scalar(S_t[:], S_t[:], flag[:, 0:1], None, op0=OP.mult), inc=True)
            P.wait("tensor", ("vector_t", P.cnt["vector_t"]))

        P.wait("sync", win_free[0])
        tload = []
        for ci in range(8):
            tload.append(ld(kw[:, ci * 256:(ci + 1) * 256], khat_s[p4 * 8 + ci][:, off:off + 256]))
            tload.append(ld(vw[:, ci * 256:(ci + 1) * 256], v_s[p4 * 8 + ci][:, off:off + 256]))
            if own:
                tload.append(ld(qw[:, ci * 256:(ci + 1) * 256], qhat_s[(p4 - 4) * 8 + ci][:, off:off + 256]))
        tload.append(ld(bw[:], beta_s[p4][:, off:off + 256]))
        if p3stop < 1:
            continue
        P.waits("tensor", tload)
        P.waits("vector", tload)

        for tj in range(2):
            rps = rot_acquire()
            TT(lambda E, tj=tj, rps=rps: E.transpose(rps[:, 0:4], bw[0:4, tj * 128:(tj + 1) * 128], ident[0:4, 0:4]), inc=True)
            P.wait("vector", ("tensor_t", P.cnt["tensor_t"]))
            V(lambda E, tj=tj, rps=rps: E.tensor_copy(bT[:, tj * 4:tj * 4 + 4], rps[:, 0:4]), inc=True)
            rot_release(("vector_t", P.cnt["vector_t"]))

        if p3stop < 2:
            continue
        for h in range(H):
            sl = sel4[:, h * 128:(h + 1) * 128]
            rps = rot_acquire()
            TT(lambda E, sl=sl, rps=rps: E.matmul(rps[:, 0:256], sl, bw[0:4, :], start=True, stop=True), inc=True)
            P.wait("vector", ("tensor_t", P.cnt["tensor_t"]))
            for ci in (2 * h, 2 * h + 1):
                V(lambda E, ci=ci, rps=rps: E.tensor_tensor(kbf[:, ci * 256:(ci + 1) * 256],
                                                            kw[:, ci * 256:(ci + 1) * 256], rps[:, 0:256], op=OP.mult))
                V(lambda E, ci=ci, h=h: E.tensor_tensor(knf[:, ci * 256:(ci + 1) * 256],
                                                        kw[:, ci * 256:(ci + 1) * 256], lamNb[h][:], op=OP.mult))
                if own:
                    V(lambda E, ci=ci, h=h: E.tensor_tensor(qtf[:, ci * 256:(ci + 1) * 256],
                                                            qw[:, ci * 256:(ci + 1) * 256], lamPb[h][:], op=OP.mult))
            V(lambda E: E.nop(), inc=True)
            rot_release(("vector_t", P.cnt["vector_t"]))

        if p3stop < 3:
            continue
        for (srct, dstt) in ((kw, kT), (vw, vT)):
            for h in range(H):
                for tj in range(2):
                    rps = rot_acquire()
                    for c2 in range(2):
                        ci = 2 * h + c2
                        TT(lambda E, srct=srct, ci=ci, tj=tj, c2=c2, rps=rps: E.transpose(
                            rps[:, c2 * 128:c2 * 128 + 128],
                            srct[:, ci * 256 + tj * 128: ci * 256 + tj * 128 + 128], ident[:]), inc=(c2 == 1))
                    P.wait("scalar", ("tensor_t", P.cnt["tensor_t"]))
                    for c2 in range(2):
                        P.ins("scalar", lambda E, dstt=dstt, h=h, tj=tj, c2=c2, rps=rps: E.copy(
                            dstt[:, (h * 2 + tj) * 256 + c2 * 128: (h * 2 + tj) * 256 + c2 * 128 + 128],
                            rps[:, c2 * 128:c2 * 128 + 128]), inc=(c2 == 1))
                    rot_release(("scalar_t", P.cnt["scalar_t"]))
        ttr3 = P.token("scalar")
        P.wait("vector", ttr3)

        if p3stop < 4:
            continue
        for h in range(H):
            for tj in range(2):
                r = (h * 2 + tj) * 256
                V(lambda E, h=h, tj=tj: E.tensor_tensor(colsc[:, 0:1], bT[:, tj * 4 + h:tj * 4 + h + 1],
                                                        rampPT[:, tj * 4 + h:tj * 4 + h + 1], op=OP.mult))
                V(lambda E, r=r: E.tensor_scalar(kbT[:, r:r + 256], kT[:, r:r + 256], colsc[:, 0:1], None, op0=OP.mult))
                V(lambda E, r=r, h=h, tj=tj: E.tensor_scalar(ktT[:, r:r + 256], kT[:, r:r + 256],
                                                             rampNT[:, tj * 4 + h:tj * 4 + h + 1], None, op0=OP.mult))
                V(lambda E, r=r, h=h, tj=tj: E.tensor_scalar(vbT[:, r:r + 256], vT[:, r:r + 256],
                                                             bT[:, tj * 4 + h:tj * 4 + h + 1], None, op0=OP.mult))
        tsc = P.token("vector")
        P.wait("tensor", tsc)
        for _ in range(3):
            p5a_slot()

        if p3stop < 5:
            continue
        rot_fence()
        def slot(h, c):
            return (32 * (c % 4), h * 64 + (c // 4) * 32)
        for h in range(H):
            for c in range(8):
                rb, cb = slot(h, c)
                for c2 in range(2):
                    ci = 2 * h + c2
                    TT(lambda E, rb=rb, cb=cb, ci=ci, c=c, c2=c2: E.matmul(
                        psW1[rb:rb + 32, cb:cb + 32],
                        kbf[:, ci * 256 + c * 32: ci * 256 + (c + 1) * 32],
                        kw[:, ci * 256 + c * 32: ci * 256 + (c + 1) * 32],
                        start=(c2 == 0), stop=(c2 == 1), tile_position=(0, rb)))
                for c2 in range(2):
                    ci = 2 * h + c2
                    TT(lambda E, rb=rb, cb=cb, ci=ci, c=c, c2=c2: E.matmul(
                        psW1[rb:rb + 32, 256 + cb: 256 + cb + 32],
                        kw[:, ci * 256 + c * 32: ci * 256 + (c + 1) * 32],
                        kbf[:, ci * 256 + c * 32: ci * 256 + (c + 1) * 32],
                        start=(c2 == 0), stop=(c2 == 1), tile_position=(0, rb)),
                       inc=(h == H - 1 and c == 7 and c2 == 1))
        tA = ("tensor_t", P.cnt["tensor_t"])
        P.wait("vector", tA)
        V(lambda E: E.tensor_tensor(Lst[:], maskl32r[:], psW1[:, 0:256], op=OP.mult))
        V(lambda E: E.tensor_scalar(Lst[:], Lst[:], -1.0, None, op0=OP.mult))
        V(lambda E: E.tensor_tensor(Ust[:], masku32r[:], psW1[:, 256:512], op=OP.mult))
        V(lambda E: E.tensor_scalar(Ust[:], Ust[:], -1.0, None, op0=OP.mult))
        V(lambda E: E.tensor_tensor(Dst[:], Ust[:], i32r[:], op=OP.add))
        V(lambda E: E.tensor_copy(Up[:], Ust[:]))
        V(lambda E: E.tensor_copy(Lp[:], Lst[:]))
        V(lambda E: E.nop(), inc=True)
        P.wait("tensor", ("vector_t", P.cnt["vector_t"]))

        if p3stop < 6:
            continue
        tiles_by_cc = [[(h, c) for h in range(H) for c in (cc, cc + 4)] for cc in range(4)]
        for lvl in range(4):
            last = (lvl == 3)
            for cc in range(4):
                for (h, c) in tiles_by_cc[cc]:
                    rb, cb = slot(h, c)
                    L_t = Lp[rb:rb + 32, cb:cb + 32]
                    U_t = Up[rb:rb + 32, cb:cb + 32]
                    TT(lambda E, L_t=L_t, U_t=U_t, rb=rb, cb=cb: E.matmul(
                        psW2[rb:rb + 32, cb:cb + 32], U_t, L_t,
                        start=True, stop=True, tile_position=(rb, rb)))
                    TT(lambda E, L_t=L_t, U_t=U_t, rb=rb, cb=cb: E.matmul(
                        psW2[rb:rb + 32, 256 + cb:256 + cb + 32], L_t, U_t,
                        start=True, stop=True, tile_position=(rb, rb)),
                       inc=(cc == 3 and (h, c) == tiles_by_cc[cc][-1]))
            tsq = ("tensor_t", P.cnt["tensor_t"])
            P.wait("vector", tsq)
            V(lambda E: E.tensor_copy(Lp[:], psW2[:, 0:256]))
            V(lambda E: E.tensor_copy(Up[:], psW2[:, 256:512]), inc=True)
            if last:
                V(lambda E: E.memset(psW3[:, 0:1024], 0.0), inc=True)
            P.wait("tensor", ("vector_t", P.cnt["vector_t"]))
            for cc in range(4):
                for (h, c) in tiles_by_cc[cc]:
                    rb, cb = slot(h, c)
                    L_t = Lp[rb:rb + 32, cb:cb + 32]
                    D_t = Dst[rb:rb + 32, cb:cb + 32]
                    i_t = i32[rb:rb + 32, 0:32]
                    if not last:
                        outap = psW3[rb:rb + 32, cb:cb + 32]
                    else:
                        outap = psW3[rb:rb + 32, (h * 8 + c) * 32:(h * 8 + c) * 32 + 32]
                    tp = (rb, rb)
                    TT(lambda E, i_t=i_t, D_t=D_t, outap=outap, tp=tp: E.matmul(
                        outap, i_t, D_t, start=True, stop=False, tile_position=tp))
                    TT(lambda E, L_t=L_t, D_t=D_t, outap=outap, tp=tp: E.matmul(
                        outap, L_t, D_t, start=False, stop=True, tile_position=tp),
                       inc=(cc == 3 and (h, c) == tiles_by_cc[cc][-1]))
            tch = ("tensor_t", P.cnt["tensor_t"])
            P.wait("vector", tch)
            if not last:
                V(lambda E: E.tensor_copy(Dst[:], psW3[:, 0:256]), inc=True)
            else:
                V(lambda E: E.tensor_copy(DfinPad[:], psW3[:, 0:1024]), inc=True)
            P.wait("tensor", ("vector_t", P.cnt["vector_t"]))

        if p3stop < 7:
            continue
        def dfinpad(h, c):
            return DfinPad[:, (h * 8 + c) * 32:(h * 8 + c) * 32 + 32]

        for h in range(H):
            for tj in range(2):
                rps = rot_acquire()
                for cc in range(4):
                    c = tj * 4 + cc
                    TT(lambda E, h=h, c=c, cc=cc, tj=tj, rps=rps: E.matmul(
                        rps[32 * cc:32 * cc + 32, 0:256],
                        dfinpad(h, c),
                        vbT[:, (h * 2 + tj) * 256:(h * 2 + tj) * 256 + 256],
                        start=True, stop=True, tile_position=(0, 32 * cc)), inc=(cc == 3))
                P.wait("vector", ("tensor_t", P.cnt["tensor_t"]))
                V(lambda E, h=h, tj=tj, rps=rps: E.tensor_copy(ut[:, (h * 2 + tj) * 256:(h * 2 + tj) * 256 + 256],
                                                               rps[:, 0:256]), inc=True)
                rot_release(("vector_t", P.cnt["vector_t"]))
            for dkh in range(2):
                rps = rot_acquire()
                for c in range(8):
                    tj = c // 4
                    TT(lambda E, h=h, c=c, tj=tj, dkh=dkh, rps=rps: E.matmul(
                        rps[:, c * 32:(c + 1) * 32],
                        kbT[:, (h * 2 + tj) * 256 + dkh * 128: (h * 2 + tj) * 256 + dkh * 128 + 128],
                        dfinpad(h, c),
                        start=True, stop=True), inc=(c == 7))
                P.wait("vector", ("tensor_t", P.cnt["tensor_t"]))
                V(lambda E, h=h, dkh=dkh, rps=rps: E.tensor_copy(wf[:, (h * 2 + dkh) * 256:(h * 2 + dkh) * 256 + 256],
                                                                 rps[:, 0:256]), inc=True)
                rot_release(("vector_t", P.cnt["vector_t"]))

        if p3stop < 8:
            continue
        for _ in range(3):
            p5a_slot()
        for h in range(H):
            for tj in range(2):
                rps = rot_acquire()
                for dkh in range(2):
                    TT(lambda E, h=h, tj=tj, dkh=dkh, rps=rps: E.matmul(
                        rps[:, 0:256],
                        wf[:, (h * 2 + dkh) * 256 + tj * 128: (h * 2 + dkh) * 256 + tj * 128 + 128],
                        S_t[:, (h * 2 + dkh) * 256:(h * 2 + dkh) * 256 + 256],
                        start=(dkh == 0), stop=(dkh == 1)), inc=(dkh == 1))
                P.wait("vector", ("tensor_t", P.cnt["tensor_t"]))
                V(lambda E, h=h, tj=tj, rps=rps: E.tensor_copy(WSt[:, (h * 2 + tj) * 256:(h * 2 + tj) * 256 + 256],
                                                               rps[:, 0:256]), inc=True)
                rot_release(("vector_t", P.cnt["vector_t"]))
                if own:
                    rps = rot_acquire()
                    for dkh in range(2):
                        TT(lambda E, h=h, tj=tj, dkh=dkh, rps=rps: E.matmul(
                            rps[:, 0:256],
                            qtf[:, (2 * h + dkh) * 256 + tj * 128: (2 * h + dkh) * 256 + tj * 128 + 128],
                            S_t[:, (h * 2 + dkh) * 256:(h * 2 + dkh) * 256 + 256],
                            start=(dkh == 0), stop=(dkh == 1)), inc=(dkh == 1))
                    P.wait("vector", ("tensor_t", P.cnt["tensor_t"]))
                    V(lambda E, h=h, tj=tj, rps=rps: E.tensor_copy(QSt[:, (h * 2 + tj) * 256:(h * 2 + tj) * 256 + 256],
                                                                   rps[:, 0:256]), inc=True)
                    rot_release(("vector_t", P.cnt["vector_t"]))

        if p3stop < 9:
            continue
        V(lambda E: E.tensor_tensor(upb[:], ut[:], WSt[:], op=OP.subtract), inc=True)
        tup0 = ("vector_t", P.cnt["vector_t"])
        P.wait("tensor", tup0)

        if p3stop < 10:
            continue
        for h in range(H):
            for tl in range(2):
                rps = rot_acquire()
                for dkh in range(2):
                    TT(lambda E, h=h, tl=tl, dkh=dkh, rps=rps: E.matmul(
                        rps[:, 0:256],
                        knf[:, (2 * h + dkh) * 256 + tl * 128: (2 * h + dkh) * 256 + tl * 128 + 128],
                        wf[:, (h * 2 + dkh) * 256:(h * 2 + dkh) * 256 + 256],
                        start=(dkh == 0), stop=(dkh == 1)), inc=(dkh == 1))
                P.wait("vector", ("tensor_t", P.cnt["tensor_t"]))
                V(lambda E, tl=tl, rps=rps: E.tensor_copy(PTt[:, tl * 256:(tl + 1) * 256], rps[:, 0:256]), inc=True)
                rot_release(("vector_t", P.cnt["vector_t"]))
            if own:
                for tl in range(2):
                    rps = rot_acquire()
                    for dkh in range(2):
                        TT(lambda E, h=h, tl=tl, dkh=dkh, rps=rps: E.matmul(
                            rps[:, 0:256],
                            knf[:, (2 * h + dkh) * 256 + tl * 128: (2 * h + dkh) * 256 + tl * 128 + 128],
                            qtf[:, (2 * h + dkh) * 256:(2 * h + dkh) * 256 + 256],
                            start=(dkh == 0), stop=(dkh == 1)), inc=(dkh == 1))
                    P.wait("vector", ("tensor_t", P.cnt["tensor_t"]))
                    V(lambda E, h=h, tl=tl, rps=rps: E.tensor_tensor(GTt[:, tl * 256:(tl + 1) * 256],
                                                                     rps[:, 0:256], ghmask[h][tl][:], op=OP.mult), inc=True)
                    rot_release(("vector_t", P.cnt["vector_t"]))
            rot_fence()
            for j in range(1, WIN):
                nk = min(32 * j, 128)
                TT(lambda E, h=h, j=j, nk=nk: E.matmul(
                    psW1[32 * (j % 4):32 * (j % 4) + 32, 0:256],
                    PTt[0:nk, 0 * 256 + j * 32: 0 * 256 + j * 32 + 32],
                    upb[0:nk, (h * 2 + 0) * 256:(h * 2 + 0) * 256 + 256],
                    start=True, stop=(j <= 4), tile_position=(0, 32 * (j % 4))), inc=(j <= 4))
                if j > 4:
                    nk2 = 32 * j - 128
                    TT(lambda E, h=h, j=j, nk2=nk2: E.matmul(
                        psW1[32 * (j % 4):32 * (j % 4) + 32, 0:256],
                        PTt[0:nk2, 1 * 256 + j * 32: 1 * 256 + j * 32 + 32],
                        upb[0:nk2, (h * 2 + 1) * 256:(h * 2 + 1) * 256 + 256],
                        start=False, stop=True, tile_position=(0, 32 * (j % 4))), inc=True)
                P.wait("vector", ("tensor_t", P.cnt["tensor_t"]))
                V(lambda E, h=h, j=j: E.scalar_tensor_tensor(
                    upb[32 * (j % 4):32 * (j % 4) + 32, (h * 2 + j // 4) * 256 + 0:(h * 2 + j // 4) * 256 + 256],
                    psW1[32 * (j % 4):32 * (j % 4) + 32, 0:256],
                    lamcol[32 * (j % 4):32 * (j % 4) + 32, 4 * h:4 * h + 1],
                    upb[32 * (j % 4):32 * (j % 4) + 32, (h * 2 + j // 4) * 256 + 0:(h * 2 + j // 4) * 256 + 256],
                    op0=OP.mult, op1=OP.add), inc=True)
                P.wait("tensor", ("vector_t", P.cnt["vector_t"]))
            if own:
                for tjo in range(2):
                    for tl in range(2):
                        TT(lambda E, h=h, tjo=tjo, tl=tl: E.matmul(
                            psW2[:, 0:256],
                            GTt[:, tl * 256 + tjo * 128: tl * 256 + tjo * 128 + 128],
                            upb[:, (h * 2 + tl) * 256:(h * 2 + tl) * 256 + 256],
                            start=(tl == 0), stop=(tl == 1)), inc=(tl == 1))
                    P.wait("vector", ("tensor_t", P.cnt["tensor_t"]))
                    V(lambda E, h=h, tjo=tjo: E.scalar_tensor_tensor(
                        osb[:], psW2[:, 0:256], 1.0,
                        QSt[:, (h * 2 + tjo) * 256:(h * 2 + tjo) * 256 + 256],
                        op0=OP.mult, op1=OP.add), inc=True)
                    tosb = ("vector_t", P.cnt["vector_t"])
                    P.wait("tensor", tosb)
                    P.wait("gpsimd", tosb)
                    st(delta_s[W0 - 2048 + tjo * 128: W0 - 2048 + (tjo + 1) * 128, h * 256:(h + 1) * 256], osb[:])
                    P.wait("vector", ("gpsimd_d", P.cnt["gpsimd_d"]))
            for dkm in range(2):
                for tj in range(2):
                    TT(lambda E, h=h, dkm=dkm, tj=tj: E.matmul(
                        psW2[:, 256:512],
                        ktT[:, (h * 2 + tj) * 256 + dkm * 128: (h * 2 + tj) * 256 + dkm * 128 + 128],
                        upb[:, (h * 2 + tj) * 256:(h * 2 + tj) * 256 + 256],
                        start=(tj == 0), stop=(tj == 1)), inc=(tj == 1))
                P.wait("vector", ("tensor_t", P.cnt["tensor_t"]))
                V(lambda E, h=h, dkm=dkm: E.scalar_tensor_tensor(
                    S_t[:, (h * 2 + dkm) * 256:(h * 2 + dkm) * 256 + 256],
                    S_t[:, (h * 2 + dkm) * 256:(h * 2 + dkm) * 256 + 256],
                    lamcol[:, 4 * h + 1:4 * h + 2],
                    psW2[:, 256:512], op0=OP.mult, op1=OP.add))
                V(lambda E, h=h, dkm=dkm: E.tensor_scalar(
                    S_t[:, (h * 2 + dkm) * 256:(h * 2 + dkm) * 256 + 256],
                    S_t[:, (h * 2 + dkm) * 256:(h * 2 + dkm) * 256 + 256],
                    lamcol[:, 4 * h + 2:4 * h + 3], None, op0=OP.mult), inc=True)
                P.wait("tensor", ("vector_t", P.cnt["vector_t"]))
        win_free[0] = P.token("vector")
        for _ in range(3):
            p5a_slot()

    while p5c["tr"] < NJOB:
        p5a_slot(drain=True)
    P.wait("scalar", ("scalar_d", P.cnt["scalar_d"]))
    P.barrier()
    es3.close()
    if upto in ("P3", "P3w1"):
        P.finalize(block)
        es.close()
        return nc

    es4 = contextlib.ExitStack()
    sb4 = lambda name, p, f, dt=F32: es4.enter_context(nc.sbuf_tensor(name, [p, f], dt))
    ps4 = lambda name, p, f: es4.enter_context(nc.psum_tensor(name, [p, f], F32))

    w1_t = sb4("w1_t", 128, 9 * 2048)
    wo_t2 = sb4("wo_t2", 128, 8192)
    w2_t = sb4("w2_t", 128, 16 * 16)
    b1c = sb4("b1c_t", 128, 16)
    b2t = sb4("b2t_t", 128, 16)
    normw_t = sb4("normw_tt", 128, 1024)
    rtempt = sb4("rtempt", 128, 16)
    floort = sb4("floort", 128, 16)
    tmp41 = sb4("tmp41", 4, 16)
    vwin = sb4("vwin", 128, 8 * 318)
    shrt = sb4("shrt", 128, 8 * 256)
    lng = sb4("lng", 128, 8 * 256)
    xnw = sb4("xnw", 128, 8 * 256)
    dtm = sb4("dtm", 128, 1024)
    stm4 = sb4("stm4", 128, 1024)
    ltm4 = sb4("ltm4", 128, 1024)
    vtm4 = sb4("vtm4", 128, 1024)
    sq4 = sb4("sq4", 128, 1024)
    statc = sb4("statc", 128, 64)
    stF = sb4("stF", 64, 128)
    hmid2 = [sb4("hmidA", 128, 128), sb4("hmidB", 128, 128)]
    tmpP = sb4("tmpP", 128, 256)
    ptile = sb4("ptile", 128, 16)
    otm = sb4("otm", 128, 1024)
    onF = sb4("onF", 128, 8 * 128)
    xtile = sb4("xtile", 128, 1024)
    y1t = otm
    mcol = sb4("mcol", 128, 16)

    psQ0 = ps4("psQ0", 128, 512)
    psQ1 = ps4("psQ1", 128, 512)

    t4l = []
    for ki in range(8):
        t4l.append(ld(w1_t[:, ki * 2048:(ki + 1) * 2048], IN["w1"][ki * 128:(ki + 1) * 128, :]))
    t4l.append(ld(w1_t[0:64, 8 * 2048:9 * 2048], IN["w1"][1024:1088, :]))
    for ki in range(8):
        t4l.append(ld(wo_t2[:, ki * 1024:(ki + 1) * 1024], IN["wo"][ki * 128:(ki + 1) * 128, :]))
    for ki in range(16):
        t4l.append(ld(w2_t[:, ki * 16:(ki + 1) * 16], IN["w2"][ki * 128:(ki + 1) * 128, :]))
    t4l.append(ld(b1c[:], IN["b1c"]))
    t4l.append(ld(b2t[:], IN["b2t"]))
    t4l.append(ld(normw_t[:], IN["normw_t"]))
    t4l.append(ld(tmp41[0:4, 0:1], IN["ltemp"]))
    t4l.append(ld(misc[0:1, 0:16], IN["floorp"]))
    P.waits("scalar", t4l)
    P.waits("vector", t4l)
    P.waits("tensor", t4l)
    P.ins("scalar", lambda E: E.activation(tmp41[0:4, 1:2], tmp41[0:4, 0:1], AF.Exp), inc=True)
    P.wait("vector", ("scalar_t", P.cnt["scalar_t"]))
    V(lambda E: E.reciprocal(tmp41[0:4, 2:3], tmp41[0:4, 1:2]))
    V(lambda E: E.nop(), inc=True)
    P.wait("tensor", ("vector_t", P.cnt["vector_t"]))
    for h in range(H):
        TT(lambda E, h=h: E.matmul(psQ0[:, h:h + 1], sel4[:, h * 128:(h + 1) * 128], tmp41[0:4, 2:3],
                                   start=True, stop=True), inc=(h == 3))
    P.wait("vector", ("tensor_t", P.cnt["tensor_t"]))
    for h in range(H):
        V(lambda E, h=h: E.tensor_copy(rtempt[:, 4 * h:4 * h + 4], bcast_ap(psQ0[:, h:h + 1], 4)))
    V(lambda E: E.nop(), inc=True)
    P.wait("tensor", ("vector_t", P.cnt["vector_t"]))
    TT(lambda E: E.matmul(psQ0[:, 0:16], ones128[0:1, :], misc[0:1, 0:16], start=True, stop=True), inc=True)
    P.wait("scalar", ("tensor_t", P.cnt["tensor_t"]))
    P.ins("scalar", lambda E: E.activation(floort[:], psQ0[:, 0:16], AF.Sigmoid), inc=True)
    P.wait("vector", ("scalar_t", P.cnt["scalar_t"]))
    V(lambda E: E.tensor_scalar(floort[:], floort[:], 0.075, None, op0=OP.mult))
    tb4 = P.token("vector")
    P.waits("tensor", [tb4])
    P.waits("sync", [tb4])

    for ti in range(8):
        T0 = ti * 256
        tl4 = []
        for ci in range(8):
            lo = T0 - 62
            for (seg0, seg1) in ((lo, T0), (T0, T0 + 256)):
                dst0 = seg0 - lo
                t = seg0
                while t < seg1:
                    if t < 0:
                        src_row = 3 * 8 + ci
                        col = 512 + t
                        n = min(seg1, 0) - t
                    else:
                        src_row = (4 + t // 512) * 8 + ci
                        col = t % 512
                        n = min(seg1 - t, 512 - col)
                    tl4.append(ld(vwin[:, ci * 318 + (t - lo): ci * 318 + (t - lo) + n],
                                  v_s[src_row][:, col:col + n]))
                    t += n
            tl4.append(ld(xnw[:, ci * 256:(ci + 1) * 256],
                          xnT_s[(ti // 2) * 8 + ci][:, (ti % 2) * 256:(ti % 2) * 256 + 256]))
        P.waits("vector", tl4)
        if ti == 0:
            V(lambda E: E.tensor_scalar(
                vwin[:].rearrange("p (c x) -> p c x", x=318)[:, :, 0:62],
                vwin[:].rearrange("p (c x) -> p c x", x=318)[:, :, 0:62],
                flag[:, 0:1], None, op0=OP.mult))
        CPL = 3
        if ti == 0:
            tflg = P.token("vector")
            P.wait("gpsimd", tflg)
        P.waits("gpsimd", tl4)
        for ci in range(CPL):
            base = ci * 318
            P.ins("gpsimd", lambda E, ci=ci, base=base: (E.tensor_scalar(
                lng[:, ci * 256:(ci + 1) * 256], vwin[:, base + 0: base + 256],
                firw_l[:, ci * 63 + 0: ci * 63 + 1], None, op0=OP.mult), E.drain()))
            for j in range(1, 63):
                P.ins("gpsimd", lambda E, ci=ci, base=base, j=j: (E.tensor_scalar(
                    tmpP[:], vwin[:, base + j: base + 256 + j],
                    firw_l[:, ci * 63 + j: ci * 63 + j + 1], None, op0=OP.mult), E.drain()))
                P.ins("gpsimd", lambda E, ci=ci: (E.tensor_tensor(
                    lng[:, ci * 256:(ci + 1) * 256], lng[:, ci * 256:(ci + 1) * 256],
                    tmpP[:], op=OP.add), E.drain()))
        gfir = P.token("gpsimd")
        for ci in range(8):
            base = ci * 318
            V(lambda E, ci=ci, base=base: E.tensor_scalar(
                shrt[:, ci * 256:(ci + 1) * 256], vwin[:, base + 60: base + 316],
                firw_s[:, ci * 3 + 0: ci * 3 + 1], None, op0=OP.mult))
            for j in (1, 2):
                V(lambda E, ci=ci, base=base, j=j: E.scalar_tensor_tensor(
                    shrt[:, ci * 256:(ci + 1) * 256], vwin[:, base + 60 + j: base + 316 + j],
                    firw_s[:, ci * 3 + j: ci * 3 + j + 1], shrt[:, ci * 256:(ci + 1) * 256],
                    op0=OP.mult, op1=OP.add))
            if ci < CPL:
                continue
            V(lambda E, ci=ci, base=base: E.tensor_scalar(
                lng[:, ci * 256:(ci + 1) * 256], vwin[:, base + 0: base + 256],
                firw_l[:, ci * 63 + 0: ci * 63 + 1], None, op0=OP.mult))
            for j in range(1, 63):
                V(lambda E, ci=ci, base=base, j=j: E.scalar_tensor_tensor(
                    lng[:, ci * 256:(ci + 1) * 256], vwin[:, base + j: base + 256 + j],
                    firw_l[:, ci * 63 + j: ci * 63 + j + 1], lng[:, ci * 256:(ci + 1) * 256],
                    op0=OP.mult, op1=OP.add))
        tfir = P.token("vector")
        P.waits("tensor", [tfir, gfir])
        P.waits("sync", [tfir, gfir])
        P.waits("vector", [gfir])

        for s in range(2):
            tdl = ld(dtm[:], delta_s[T0 + s * 128: T0 + (s + 1) * 128, :])
            jobs = []
            for (srct, dstt, srcw, hal) in ((shrt, stm4, 256, 0), (lng, ltm4, 256, 0), (vwin, vtm4, 318, 62)):
                for ci in range(8):
                    jobs.append((srct, dstt, ci * srcw + hal + s * 128, ci))
            for j0 in range(0, len(jobs), 4):
                grp = jobs[j0:j0 + 4]
                for gi, (srct, dstt, off, ci) in enumerate(grp):
                    TT(lambda E, srct=srct, off=off, gi=gi: E.transpose(
                        psQ0[:, gi * 128:(gi + 1) * 128], srct[:, off: off + 128], ident[:]),
                       inc=(gi == len(grp) - 1))
                P.wait("scalar", ("tensor_t", P.cnt["tensor_t"]))
                for gi, (srct, dstt, off, ci) in enumerate(grp):
                    P.ins("scalar", lambda E, dstt=dstt, ci=ci, gi=gi: E.copy(
                        dstt[:, ci * 128:(ci + 1) * 128], psQ0[:, gi * 128:(gi + 1) * 128]),
                          inc=(gi == len(grp) - 1))
                P.wait("tensor", ("scalar_t", P.cnt["scalar_t"]))
            ttr4 = P.token("scalar")
            P.waits("vector", [ttr4, tdl])

            for (tix, tsr) in enumerate((stm4, ltm4, dtm, vtm4)):
                tt = tsr[:]
                tt3 = tt.rearrange("p (h d) -> p h d", d=256)
                V(lambda E: E.drain())
                V(lambda E, tt3=tt3, tix=tix: E.tensor_reduce(statc[:, tix * 16 + 0:tix * 16 + 4], tt3, axis=AXX, op=OP.add))
                V(lambda E, tt=tt: E.tensor_tensor(sq4[:], tt, tt, op=OP.mult))
                V(lambda E: E.drain())
                V(lambda E, tix=tix: E.tensor_reduce(statc[:, tix * 16 + 4:tix * 16 + 8],
                                                     sq4[:].rearrange("p (h d) -> p h d", d=256), axis=AXX, op=OP.add))
                V(lambda E: E.drain())
                V(lambda E, tt3=tt3, tix=tix: E.tensor_reduce(statc[:, tix * 16 + 8:tix * 16 + 12], tt3, axis=AXX, op=OP.max))
            for tix in range(4):
                c0 = tix * 16
                V(lambda E, c0=c0: E.tensor_scalar(statc[:, c0 + 0:c0 + 4], statc[:, c0 + 0:c0 + 4], 1.0 / 256, None, op0=OP.mult))
                V(lambda E, c0=c0: E.tensor_tensor(mcol[:, 0:4], statc[:, c0 + 0:c0 + 4], statc[:, c0 + 0:c0 + 4], op=OP.mult))
                V(lambda E, c0=c0: E.tensor_scalar(statc[:, c0 + 12:c0 + 16], statc[:, c0 + 4:c0 + 8], 1.0 / 256, None, op0=OP.mult))
                V(lambda E, c0=c0: E.tensor_tensor(statc[:, c0 + 12:c0 + 16], statc[:, c0 + 12:c0 + 16], mcol[:, 0:4], op=OP.subtract))
                V(lambda E: E.nop(), inc=True)
                P.wait("scalar", ("vector_t", P.cnt["vector_t"]))
                P.ins("scalar", lambda E, c0=c0: E.activation(statc[:, c0 + 4:c0 + 8], statc[:, c0 + 4:c0 + 8], AF.Sqrt), inc=True)
                P.wait("vector", ("scalar_t", P.cnt["scalar_t"]))
            for tix in range(4):
                c0 = tix * 16
                for h in range(H):
                    V(lambda E, c0=c0, h=h: E.tensor_copy(sq4[:, c0 + 4 * h + 0:c0 + 4 * h + 1], statc[:, c0 + 0 + h:c0 + 1 + h]))
                    V(lambda E, c0=c0, h=h: E.tensor_copy(sq4[:, c0 + 4 * h + 1:c0 + 4 * h + 2], statc[:, c0 + 12 + h:c0 + 13 + h]))
                    V(lambda E, c0=c0, h=h: E.tensor_copy(sq4[:, c0 + 4 * h + 2:c0 + 4 * h + 3], statc[:, c0 + 8 + h:c0 + 9 + h]))
                    V(lambda E, c0=c0, h=h: E.tensor_copy(sq4[:, c0 + 4 * h + 3:c0 + 4 * h + 4], statc[:, c0 + 4 + h:c0 + 5 + h]))
            V(lambda E: E.nop(), inc=True)
            P.wait("tensor", ("vector_t", P.cnt["vector_t"]))
            TT(lambda E: E.transpose(psQ1[0:64, 0:128], sq4[:, 0:64], ident[:]), inc=True)
            P.wait("scalar", ("tensor_t", P.cnt["tensor_t"]))
            P.ins("scalar", lambda E: E.copy(stF[0:64, 0:128], psQ1[0:64, 0:128]), inc=True)
            P.wait("tensor", ("scalar_t", P.cnt["scalar_t"]))

            g_tok = {}
            w1tok = {}
            gbanks = [psQ0, psB0, psB1]
            for mi in range(17):
                if mi < 16:
                    bank = gbanks[mi % 3]
                    P.wait("tensor", g_tok.get(mi - 3))
                    for ki in range(8):
                        TT(lambda E, mi=mi, ki=ki, s=s, bank=bank: E.matmul(
                            bank[:, 0:128], w1_t[:, ki * 2048 + mi * 128: ki * 2048 + (mi + 1) * 128],
                            xnw[:, ki * 256 + s * 128: ki * 256 + s * 128 + 128], start=(ki == 0), stop=False))
                    TT(lambda E, mi=mi, bank=bank: E.matmul(
                        bank[:, 0:128], w1_t[0:64, 8 * 2048 + mi * 128: 8 * 2048 + (mi + 1) * 128],
                        stF[0:64, 0:128], start=False, stop=True), inc=True)
                    w1tok[mi] = ("tensor_t", P.cnt["tensor_t"])
                if mi >= 1:
                    mp = mi - 1
                    P.wait("tensor", g_tok[mp])
                    P.cnt["w2s"] += 1
                    P.ins("tensor", lambda E, mp=mp: E.matmul(
                        psQ1[:, 0:16], hmid2[mp % 2][:],
                        w2_t[:, mp * 16:(mp + 1) * 16], start=(mp == 0), stop=(mp == 15)).then_inc(P.sem["w2s"], 1))
                if mi < 16:
                    P.wait("scalar", w1tok[mi])
                    if mi >= 2:
                        P.wait("scalar", ("w2s", P.cnt["w2s"]))
                    P.ins("scalar", lambda E, mi=mi, bank=gbanks[mi % 3]: E.activation(
                        hmid2[mi % 2][:], bank[:, 0:128], AF.Gelu,
                        bias=b1c[:, mi:mi + 1], scale=1.0), inc=True)
                    g_tok[mi] = ("scalar_t", P.cnt["scalar_t"])
            P.wait("vector", ("w2s", P.cnt["w2s"]))
            pt = ptile[:]
            V(lambda E, pt=pt: E.tensor_tensor(pt, b2t[:], psQ1[:, 0:16], op=OP.add))
            V(lambda E, pt=pt: E.tensor_tensor(pt, pt, rtempt[:], op=OP.mult))
            V(lambda E: E.drain())
            V(lambda E, pt=pt: E.tensor_reduce(mcol[:, 0:4], pt.rearrange("p (h g) -> p h g", g=4), axis=AXX, op=OP.max))
            for h in range(H):
                V(lambda E, pt=pt, h=h: E.tensor_scalar(pt[:, 4 * h:4 * h + 4], pt[:, 4 * h:4 * h + 4],
                                                        mcol[:, h:h + 1], None, op0=OP.subtract))
            V(lambda E: E.nop(), inc=True)
            P.wait("scalar", ("vector_t", P.cnt["vector_t"]))
            P.ins("scalar", lambda E, pt=pt: E.activation(pt, pt, AF.Exp), inc=True)
            P.wait("vector", ("scalar_t", P.cnt["scalar_t"]))
            V(lambda E: E.drain())
            V(lambda E, pt=pt: E.tensor_reduce(mcol[:, 0:4], pt.rearrange("p (h g) -> p h g", g=4), axis=AXX, op=OP.add))
            V(lambda E: E.reciprocal(mcol[:, 4:8], mcol[:, 0:4]))
            V(lambda E: E.drain())
            for h in range(H):
                V(lambda E, pt=pt, h=h: E.tensor_scalar(pt[:, 4 * h:4 * h + 4], pt[:, 4 * h:4 * h + 4],
                                                        mcol[:, 4 + h:5 + h], None, op0=OP.mult))
            V(lambda E, pt=pt: E.tensor_tensor(pt, pt, floort[:], op=OP.max))
            V(lambda E: E.drain())
            V(lambda E, pt=pt: E.tensor_reduce(mcol[:, 0:4], pt.rearrange("p (h g) -> p h g", g=4), axis=AXX, op=OP.add))
            V(lambda E: E.reciprocal(mcol[:, 4:8], mcol[:, 0:4]))
            V(lambda E: E.drain())
            for h in range(H):
                V(lambda E, pt=pt, h=h: E.tensor_scalar(pt[:, 4 * h:4 * h + 4], pt[:, 4 * h:4 * h + 4],
                                                        mcol[:, 4 + h:5 + h], None, op0=OP.mult))

            for h in range(H):
                reg = h * 256
                V(lambda E, pt=pt, reg=reg, h=h: E.tensor_scalar(
                    otm[:, reg:reg + 256], stm4[:, reg:reg + 256], pt[:, 4 * h + 0:4 * h + 1], None, op0=OP.mult))
                for (g, tsr) in ((1, ltm4), (2, dtm), (3, vtm4)):
                    V(lambda E, pt=pt, reg=reg, h=h, g=g, tsr=tsr: E.scalar_tensor_tensor(
                        otm[:, reg:reg + 256], tsr[:, reg:reg + 256], pt[:, 4 * h + g:4 * h + g + 1],
                        otm[:, reg:reg + 256], op0=OP.mult, op1=OP.add))
            V(lambda E: E.tensor_tensor(sq4[:], otm[:], otm[:], op=OP.mult))
            V(lambda E: E.drain())
            V(lambda E: E.tensor_reduce(mcol[:, 0:4], sq4[:].rearrange("p (h d) -> p h d", d=256), axis=AXX, op=OP.add))
            V(lambda E: E.tensor_scalar(mcol[:, 0:4], mcol[:, 0:4], 1.0 / 256, 1e-5, op0=OP.mult, op1=OP.add))
            V(lambda E: E.nop(), inc=True)
            P.wait("scalar", ("vector_t", P.cnt["vector_t"]))
            P.ins("scalar", lambda E: E.activation(mcol[:, 4:8], mcol[:, 0:4], AF.Sqrt), inc=True)
            P.wait("vector", ("scalar_t", P.cnt["scalar_t"]))
            V(lambda E: E.reciprocal(mcol[:, 8:12], mcol[:, 4:8]))
            V(lambda E: E.drain())
            for h in range(H):
                reg = h * 256
                V(lambda E, reg=reg, h=h: E.tensor_scalar(
                    otm[:, reg:reg + 256], otm[:, reg:reg + 256], mcol[:, 8 + h:9 + h], None, op0=OP.mult))
            V(lambda E: E.tensor_tensor(otm[:], otm[:], normw_t[:], op=OP.mult))
            tcomb = P.token("vector")
            P.wait("tensor", tcomb)

            for cg in range(2):
                for cj in range(4):
                    ci = cg * 4 + cj
                    TT(lambda E, ci=ci, cj=cj: E.transpose(
                        psQ0[:, cj * 128:(cj + 1) * 128], otm[:, ci * 128:(ci + 1) * 128], ident[:]), inc=(cj == 3))
                P.wait("scalar", ("tensor_t", P.cnt["tensor_t"]))
                for cj in range(4):
                    ci = cg * 4 + cj
                    P.ins("scalar", lambda E, ci=ci, cj=cj: E.copy(
                        onF[:, ci * 128:(ci + 1) * 128], psQ0[:, cj * 128:(cj + 1) * 128]), inc=(cj == 3))
                P.wait("tensor", ("scalar_t", P.cnt["scalar_t"]))

            tx4 = ld(xtile[:], IN["x12"][2048 + T0 + s * 128: 2048 + T0 + (s + 1) * 128, :])
            for nn in range(2):
                for ki in range(8):
                    TT(lambda E, nn=nn, ki=ki: E.matmul(
                        psQ1[:, 0:512], onF[:, ki * 128:(ki + 1) * 128],
                        wo_t2[:, ki * 1024 + nn * 512: ki * 1024 + nn * 512 + 512],
                        start=(ki == 0), stop=(ki == 7)), inc=(ki == 7))
                P.waits("vector", [("tensor_t", P.cnt["tensor_t"]), tx4])
                V(lambda E, nn=nn: E.tensor_tensor(y1t[:, nn * 512:(nn + 1) * 512],
                                                   xtile[:, nn * 512:(nn + 1) * 512], psQ1[:, 0:512], op=OP.add), inc=True)
                P.wait("tensor", ("vector_t", P.cnt["vector_t"]))
            ty1 = P.token("vector")
            P.wait("scalar", ty1)
            tspill = st(y1_s[T0 + s * 128: T0 + (s + 1) * 128, :], y1t[:], eng="scalar")
            P.wait("vector", tspill)
            P.wait("sync", tspill)
        tend = P.token("vector")
        P.wait("sync", tend)

    P.barrier()
    es4.close()
    if upto == "P4":
        P.finalize(block)
        es.close()
        return nc

    es5 = contextlib.ExitStack()
    sb5 = lambda name, p, f, dt=F32: es5.enter_context(nc.sbuf_tensor(name, [p, f], dt))
    ps5 = lambda name, p, f: es5.enter_context(nc.psum_tensor(name, [p, f], F32))

    wbuf = sb5("wbuf", 128, 2816)
    wbuf2 = sb5("wbuf2", 128, 2816)
    uq5 = sb5("uq5", 128, 2816)
    gq5 = sb5("gq5", 128, 2816)
    sg5 = sb5("sg5", 128, 2816)
    qr5 = sb5("qr5", 128, 176)
    rc5 = sb5("rc5", 128, 176)
    psF = [psB0, psB1] + [ps5("psF%d" % i, 128, 512) for i in range(6)]

    def emit_quant(buf, K):
        nb = K // 16
        bv = buf.rearrange("p (nb b) -> p nb b", b=16)
        V(lambda E: E.drain())
        V(lambda E, bv=bv, nb=nb: E.tensor_reduce(qr5[:, 0:nb], bv, axis=AXX, op=OP.max, apply_absolute_value=True))
        V(lambda E, nb=nb: E.tensor_scalar(qr5[:, 0:nb], qr5[:, 0:nb], 1.0 / 6, 1e-12, op0=OP.mult, op1=OP.max))
        V(lambda E, nb=nb: E.reciprocal(rc5[:, 0:nb], qr5[:, 0:nb]))
        V(lambda E: E.drain())
        rb = bcast_ap(rc5[:, 0:nb], 16)
        V(lambda E, bv=bv, rb=rb, K=K: E.tensor_tensor(uq5[:, 0:K].rearrange("p (nb b) -> p nb b", b=16), bv, rb, op=OP.mult))
        V(lambda E, K=K: E.tensor_scalar(sg5[:, 0:K], uq5[:, 0:K], 0.0, None, op0=OP.is_gt))
        V(lambda E, K=K: E.tensor_scalar(gq5[:, 0:K], uq5[:, 0:K], 0.0, None, op0=OP.is_lt))
        V(lambda E, K=K: E.tensor_tensor(sg5[:, 0:K], sg5[:, 0:K], gq5[:, 0:K], op=OP.subtract))
        V(lambda E, K=K: E.tensor_tensor(uq5[:, 0:K], uq5[:, 0:K], sg5[:, 0:K], op=OP.mult))
        V(lambda E, K=K: E.tensor_scalar(gq5[:, 0:K], uq5[:, 0:K], 0.25, None, op0=OP.is_gt))
        for thr in (0.75, 1.25, 1.75):
            V(lambda E, K=K, thr=thr: E.scalar_tensor_tensor(gq5[:, 0:K], uq5[:, 0:K], thr, gq5[:, 0:K], op0=OP.is_gt, op1=OP.add))
        V(lambda E, K=K: E.tensor_scalar(gq5[:, 0:K], gq5[:, 0:K], 0.5, None, op0=OP.mult))
        for thr in (2.5, 3.5, 5.0, 5.0):
            V(lambda E, K=K, thr=thr: E.scalar_tensor_tensor(gq5[:, 0:K], uq5[:, 0:K], thr, gq5[:, 0:K], op0=OP.is_gt, op1=OP.add))
        V(lambda E, K=K: E.tensor_tensor(gq5[:, 0:K], gq5[:, 0:K], sg5[:, 0:K], op=OP.mult))
        qb = bcast_ap(qr5[:, 0:nb], 16)
        V(lambda E, buf=buf, K=K, qb=qb, nb=nb: E.tensor_tensor(
            buf.rearrange("p (nb b) -> p nb b", b=16),
            gq5[:, 0:K].rearrange("p (nb b) -> p nb b", b=16), qb, op=OP.mult))

    xq5 = sb5("xq5", 128, 8 * 2048)
    wgk = sb5("wgk", 128, 8 * 512)
    wuk = sb5("wuk", 128, 8 * 512)
    gsil = sb5("gsil", 128, 512)
    hqbuf = sb5("hqbuf", 128, 512)
    t5x = []
    for ci in range(8):
        for tt in range(4):
            t5x.append(ld(xq5[:, ci * 2048 + tt * 512: ci * 2048 + (tt + 1) * 512], xqT_s[tt * 8 + ci]))
    P.waits("tensor", t5x)
    ncols = [(0, 512), (512, 512), (1024, 512), (1536, 512), (2048, 512), (2560, 256)]
    for (n0, nsz) in ncols:
        tlw = []
        for kc in range(8):
            tlw.append(ld(wgk[:, kc * 512: kc * 512 + nsz], wgT_s[kc][:, n0:n0 + nsz]))
            tlw.append(ld(wuk[:, kc * 512: kc * 512 + nsz], wuT_s[kc][:, n0:n0 + nsz]))
        P.waits("tensor", tlw)
        ps_cons = [None, None]
        mmtok = [None] * 17
        for m in range(17):
            if m < 16:
                pr = m % 2
                P.wait("tensor", ps_cons[pr])
                for kc in range(8):
                    TT(lambda E, m=m, kc=kc, nsz=nsz, pr=pr: E.matmul(
                        psF[pr * 2][:, 0:nsz], xq5[:, kc * 2048 + m * 128: kc * 2048 + (m + 1) * 128],
                        wgk[:, kc * 512: kc * 512 + nsz], start=(kc == 0), stop=(kc == 7)))
                for kc in range(8):
                    TT(lambda E, m=m, kc=kc, nsz=nsz, pr=pr: E.matmul(
                        psF[pr * 2 + 1][:, 0:nsz], xq5[:, kc * 2048 + m * 128: kc * 2048 + (m + 1) * 128],
                        wuk[:, kc * 512: kc * 512 + nsz], start=(kc == 0), stop=(kc == 7)), inc=(kc == 7))
                mmtok[m] = ("tensor_t", P.cnt["tensor_t"])
            if m >= 1:
                mp = m - 1
                pr = mp % 2
                wb = wbuf if pr == 0 else wbuf2
                P.wait("scalar", mmtok[mp])
                P.wait("scalar", ps_cons[1 - pr])
                P.ins("scalar", lambda E, nsz=nsz, pr=pr: E.activation(gsil[:, 0:nsz], psF[pr * 2][:, 0:nsz], AF.Silu), inc=True)
                P.wait("vector", ("scalar_t", P.cnt["scalar_t"]))
                V(lambda E, nsz=nsz, pr=pr, wb=wb: E.tensor_tensor(wb[:, 0:nsz], gsil[:, 0:nsz], psF[pr * 2 + 1][:, 0:nsz], op=OP.mult), inc=True)
                ps_cons[pr] = ("vector_t", P.cnt["vector_t"])
                emit_quant(wb[:, 0:nsz], nsz)
                tq = P.token("vector")
                P.wait("tensor", tq)
                for kc in range(nsz // 128):
                    TT(lambda E, kc=kc, wb=wb: E.transpose(psF[4][:, kc * 128:(kc + 1) * 128],
                                                           wb[:, kc * 128:(kc + 1) * 128], ident[:]), inc=(kc == nsz // 128 - 1))
                P.wait("scalar", ("tensor_t", P.cnt["tensor_t"]))
                for kc in range(nsz // 128):
                    P.ins("scalar", lambda E, kc=kc: E.copy(hqbuf[:, kc * 128:(kc + 1) * 128],
                                                            psF[4][:, kc * 128:(kc + 1) * 128]), inc=(kc == nsz // 128 - 1))
                tsp = ("scalar_t", P.cnt["scalar_t"])
                P.wait("tensor", tsp)
                P.wait("gpsimd", tsp)
                for kc in range(nsz // 128):
                    st(hqT_s[n0 // 128 + kc][:, mp * 128:(mp + 1) * 128], hqbuf[:, kc * 128:(kc + 1) * 128])
                P.wait("scalar", ("gpsimd_d", P.cnt["gpsimd_d"]))
                P.wait("vector", ("gpsimd_d", P.cnt["gpsimd_d"]))
        P.wait("sync", ("tensor_t", P.cnt["tensor_t"]))
    P.barrier()

    wdk = [sb5("wdk0", 128, 1024), sb5("wdk1", 128, 1024), sb5("wdk2", 128, 1024)]
    hqk = [sb5("hqk0", 128, 4 * 128), sb5("hqk1", 128, 4 * 128), sb5("hqk2", 128, 4 * 128)]
    y5 = sb5("y5", 128, 1024)
    mmt = {}
    ldt = {}

    def p5c_loads(li):
        mg, kc = li // 22, li % 22
        pb = li % 3
        P.wait("sync", mmt.get(li - 3))
        P.wait("gpsimd", mmt.get(li - 3))
        t = [ld(wdk[pb][:], wdT_s[kc])]
        for mm_ in range(4):
            m = mg * 4 + mm_
            t.append(ld(hqk[pb][:, mm_ * 128:(mm_ + 1) * 128], hqT_s[kc][:, m * 128:(m + 1) * 128], eng="gpsimd"))
        ldt[li] = t

    p5c_loads(0)
    p5c_loads(1)
    p5c_loads(2)
    for mg in range(4):
        for kc in range(22):
            li = mg * 22 + kc
            pb = li % 3
            P.waits("tensor", ldt[li])
            for mm_ in range(4):
                for nn in range(2):
                    TT(lambda E, mm_=mm_, nn=nn, kc=kc, pb=pb: E.matmul(
                        psF[mm_ * 2 + nn][:, 0:512],
                        hqk[pb][:, mm_ * 128:(mm_ + 1) * 128],
                        wdk[pb][:, nn * 512:(nn + 1) * 512],
                        start=(kc == 0), stop=(kc == 21)), inc=(mm_ == 3 and nn == 1))
            mmt[li] = ("tensor_t", P.cnt["tensor_t"])
            if li + 3 < 88:
                p5c_loads(li + 3)
        tdone = ("tensor_t", P.cnt["tensor_t"])
        for mm_ in range(4):
            m = mg * 4 + mm_
            ty = ld(y5[:], y1_s[m * 128:(m + 1) * 128, :], eng="gpsimd")
            P.waits("vector", [tdone, ty])
            for nn in range(2):
                V(lambda E, mm_=mm_, nn=nn: E.tensor_tensor(
                    y5[:, nn * 512:(nn + 1) * 512], y5[:, nn * 512:(nn + 1) * 512],
                    psF[mm_ * 2 + nn][:, 0:512], op=OP.add))
            V(lambda E: E.nop(), inc=True)
            tout = ("vector_t", P.cnt["vector_t"])
            P.wait("gpsimd", tout)
            st(out_h[m * 128:(m + 1) * 128, :], y5[:])
            P.wait("vector", ("gpsimd_d", P.cnt["gpsimd_d"]))
            P.wait("tensor", ("gpsimd_d", P.cnt["gpsimd_d"]))

    P.wait("sync", ("gpsimd_d", P.cnt["gpsimd_d"]))
    P.barrier()
    es5.close()

    P.finalize(block)
    es.close()
    return nc



def kernel(**inputs):
    maps = shard_inputs(inputs)
    nc = build_nc(upto="ALL")
    res = run_bass_kernel_spmd(nc, maps, core_ids=list(range(NCHIP)))
    out = np.zeros((4, 4096, 1024), np.float32)
    for c in range(NCHIP):
        b, half = c // 2, c % 2
        out[b, 2048 * half: 2048 * half + 2048] = res.results[c]["out"]
    return out


# version 08
# speedup vs baseline: 1.2566x; 1.2566x over previous
import numpy as np
import contextlib
import concourse.bass as bass
import concourse.mybir as mybir
from concourse.alu_op_type import AluOpType as OP
from concourse.bass_utils import run_bass_kernel_spmd

F32 = mybir.dt.float32
BF16 = mybir.dt.bfloat16
AF = mybir.ActivationFunctionType
AXX = mybir.AxisListType.X

H = 4
NCHIP = 8
T2 = 2048
TW = 256
WIN = 8
NWIN = 16
PP = 512
NPP = 8
D = 1024
FFI = 2816

ENGS = ("sync", "vector", "scalar", "tensor", "gpsimd")


class Prog:
    def __init__(self, nc, es):
        self.nc = nc
        self.q = {e: [] for e in ENGS}
        self.cnt = {}
        self.sem = {}
        for e in ENGS:
            for sfx in ("_t", "_d"):
                name = e + sfx
                self.sem[name] = es.enter_context(nc.semaphore(name))
                self.cnt[name] = 0
        self.seq = 0
        self.auto_drain = True
        self.fence_pending = {}
        self.fenced = {}

    def ins(self, eng, fn, inc=False, dma=False):
        tok = None
        if dma:
            name = eng + "_d"
            pend = self.fence_pending.get(name, 0)
            if pend > self.fenced.get(name, 0):
                self.q[eng].append(lambda E, name=name, v=pend: E.wait_ge(self.sem[name], v))
                self.fenced[name] = pend
            self.cnt[name] += 16
            tok = (name, self.cnt[name])
            def wrapped(E, fn=fn, name=name):
                fn(E).then_inc(self.sem[name], 16)
        elif inc:
            name = eng + "_t"
            self.cnt[name] += 1
            tok = (name, self.cnt[name])
            if eng in ("vector", "scalar", "gpsimd"):
                def wrapped(E, fn=fn, name=name):
                    fn(E)
                    E.drain().then_inc(self.sem[name], 1)
            else:
                def wrapped(E, fn=fn, name=name):
                    fn(E).then_inc(self.sem[name], 1)
        elif eng == "vector" and self.auto_drain:
            def wrapped(E, fn=fn):
                fn(E)
                E.drain()
        else:
            wrapped = fn
        self.q[eng].append(wrapped)
        self.seq += 1
        return tok

    def wait(self, eng, tok):
        if tok is None:
            return
        name, val = tok
        if name.endswith("_d"):
            val = self.cnt[name]
            self.fence_pending[name] = max(self.fence_pending.get(name, 0), val)
        self.q[eng].append(lambda E, name=name, val=val: E.wait_ge(self.sem[name], val))

    def waits(self, eng, toks):
        best = {}
        for t in toks:
            if t is None:
                continue
            n, v = t
            if n.endswith("_d"):
                v = self.cnt[n]
                self.fence_pending[n] = max(self.fence_pending.get(n, 0), v)
            best[n] = max(best.get(n, 0), v)
        for n, v in best.items():
            self.q[eng].append(lambda E, n=n, v=v: E.wait_ge(self.sem[n], v))

    def token(self, eng):
        if eng in ("vector", "scalar", "gpsimd"):
            name = eng + "_t"
            self.cnt[name] += 1
            tok = (name, self.cnt[name])
            self.q[eng].append(lambda E, name=name: E.drain().then_inc(self.sem[name], 1))
            self.seq += 1
            return tok
        return self.ins(eng, lambda E: E.nop(), inc=True)

    def barrier(self, engs=ENGS):
        toks = [self.token(e) for e in engs]
        for e in engs:
            self.waits(e, [t for t in toks])

    def finalize(self, block):
        for e in ENGS:
            fns = self.q[e]
            def runner(E, fns=fns):
                for f in fns:
                    f(E)
            getattr(block, e)(runner)


def bcast_ap(ap, inner):
    return bass.AP(tensor=ap.tensor, offset=ap.offset,
                   ap=[list(q) for q in ap.ap] + [[0, inner]])


def rep_ap(ap, n, step=0):
    a = [list(q) for q in ap.ap]
    return bass.AP(tensor=ap.tensor, offset=ap.offset, ap=[a[0]] + [[step, n]] + a[1:])



def make_consts():
    ident = np.eye(128, dtype=np.float32)
    ones128 = np.ones((128, 128), np.float32)
    sel4 = np.zeros((4, 512), np.float32)
    for h in range(4):
        sel4[h, h * 128:(h + 1) * 128] = 1.0
    r = np.arange(32)
    maskL32 = np.tile((r[:, None] > r[None, :]).astype(np.float32), (4, 1))
    maskU32 = np.tile((r[:, None] < r[None, :]).astype(np.float32), (4, 1))
    i32 = np.tile(np.eye(32, dtype=np.float32), (4, 1))
    a = np.arange(256)
    ca, cb = a[:, None] // 32, a[None, :] // 32
    strict = (ca < cb).astype(np.float32)
    diag = ((ca == cb) & (a[:, None] % 32 <= a[None, :] % 32)).astype(np.float32)
    return dict(ident=ident, ones128=ones128, sel4=sel4, maskl32=maskL32, masku32=maskU32,
                i32=i32, maskl32r=np.tile(maskL32, (1, 8)), masku32r=np.tile(maskU32, (1, 8)),
                i32r=np.tile(i32, (1, 8)),
                gstrict0=strict[:128], gstrict1=strict[128:],
                gdiag0=diag[:128], gdiag1=diag[128:])


def shard_inputs(I):
    x = np.asarray(I["L0_attn_hidden_states"], np.float32)
    consts = make_consts()
    maps = []
    for c in range(NCHIP):
        b, half = c // 2, c % 2
        x12 = np.concatenate([x[b, 0:2048], x[b, 2048 * half: 2048 * half + 2048]], 0)
        m = {
            "x12": x12,
            "wq": I["L0_attn_Wq"], "wk": I["L0_attn_Wk"], "wv": I["L0_attn_Wv"],
            "wb": I["L0_attn_Wb"],
            "convq": I["L0_attn_conv_q"], "convk": I["L0_attn_conv_k"],
            "convv": I["L0_attn_conv_v"],
            "fir_s": np.asarray(I["L0_attn_fir_short_w"], np.float32).reshape(1024, 3),
            "fir_l": np.asarray(I["L0_attn_fir_long_w"], np.float32).reshape(1024, 63),
            "fparam": np.asarray(I["L0_attn_forget_param"], np.float32).reshape(4, 1),
            "ltemp": np.asarray(I["L0_attn_log_temp"], np.float32).reshape(4, 1),
            "floorp": np.asarray(I["L0_attn_floor_param"], np.float32).reshape(1, 16),
            "w1": I["L0_attn_W1"],
            "b1": np.asarray(I["L0_attn_b1"], np.float32).reshape(1, 2048),
            "b1c": np.asarray(I["L0_attn_b1"], np.float32).reshape(16, 128).T,
            "w2": I["L0_attn_W2"],
            "b2": np.asarray(I["L0_attn_b2"], np.float32).reshape(1, 16),
            "normw_t": np.tile(np.asarray(I["L0_attn_norm_w"], np.float32).reshape(1, 256), (128, 4)),
            "b2t": np.tile(np.asarray(I["L0_attn_b2"], np.float32).reshape(1, 16), (128, 1)),
            "wo": I["L0_attn_Wo"],
            "wg": I["L0_ffn_w_gate"], "wu": I["L0_ffn_w_up"], "wd": I["L0_ffn_w_down"],
            "flagv": np.full((128, 1), float(half), np.float32),
        }
        m.update(consts)
        maps.append({k: np.ascontiguousarray(np.asarray(v, np.float32)) for k, v in m.items()})
    return maps



def build_nc(upto="ALL", dbg=(), skip=(), p3stop=99):
    nc = bass.Bass("TRN2", target_bir_lowering=False, debug=False)
    es = contextlib.ExitStack()
    P = Prog(nc, es)

    def dram(name, shape, kind="Internal"):
        k = "ExternalOutput" if (name in dbg and kind == "Internal") else kind
        return nc.dram_tensor(name, shape, F32, kind=k).ap()

    IN = {}
    for name, shape in [
        ("x12", [4096, 1024]), ("wq", [1024, 1024]), ("wk", [1024, 1024]),
        ("wv", [1024, 1024]), ("wb", [1024, 4]),
        ("convq", [1024, 4]), ("convk", [1024, 4]), ("convv", [1024, 4]),
        ("fir_s", [1024, 3]), ("fir_l", [1024, 63]),
        ("fparam", [4, 1]), ("ltemp", [4, 1]), ("floorp", [1, 16]),
        ("w1", [1088, 2048]), ("b1", [1, 2048]), ("b1c", [128, 16]), ("w2", [2048, 16]), ("b2", [1, 16]),
        ("normw_t", [128, 1024]), ("b2t", [128, 16]), ("wo", [1024, 1024]),
        ("wg", [2816, 1024]), ("wu", [2816, 1024]), ("wd", [1024, 2816]),
        ("flagv", [128, 1]), ("ident", [128, 128]), ("ones128", [128, 128]), ("sel4", [4, 512]),
        ("maskl32", [128, 32]), ("masku32", [128, 32]), ("i32", [128, 32]),
        ("maskl32r", [128, 256]), ("masku32r", [128, 256]), ("i32r", [128, 256]),
        ("gstrict0", [128, 256]), ("gstrict1", [128, 256]),
        ("gdiag0", [128, 256]), ("gdiag1", [128, 256]),
    ]:
        IN[name] = dram(name, shape, "ExternalInput")

    out_h = dram("out", [2048, 1024], "ExternalOutput")

    khat_s = dram("khat_s", [64, 128, 512])
    v_s = dram("v_s", [64, 128, 512])
    qhat_s = dram("qhat_s", [32, 128, 512])
    xnT_s = dram("xnT_s", [32, 128, 512])
    xqT_s = dram("xqT_s", [32, 128, 512])
    beta_s = dram("beta_s", [8, 4, 512])
    xn_dbg = dram("xn_dbg", [128, 4096])
    ln_dbg = dram("ln_dbg", [128, 16])
    xtm_dbg = dram("xtm_dbg", [128, 1024])
    xnT_dbg = dram("xnT_dbg", [128, 4096])
    delta_s = dram("delta_s", [2048, 1024])
    y1_s = dram("y1_s", [2048, 1024])
    wgT_s = dram("wgT_s", [8, 128, 2816])
    wuT_s = dram("wuT_s", [8, 128, 2816])
    wdT_s = dram("wdT_s", [22, 128, 1024])
    hqT_s = dram("hqT_s", [22, 128, 2048])

    sb = lambda name, p, f, dt=F32: es.enter_context(nc.sbuf_tensor(name, [p, f], dt))

    ident = sb("identt", 128, 128)
    ones128 = sb("ones128t", 128, 128)
    sel4 = sb("sel4t", 4, 512)
    flag = sb("flagt", 128, 1)
    maskl32 = sb("maskl32t", 128, 32)
    masku32 = sb("masku32t", 128, 32)
    i32 = sb("i32t", 128, 32)
    maskl32r = sb("maskl32rt", 128, 256)
    masku32r = sb("masku32rt", 128, 256)
    i32r = sb("i32rt", 128, 256)
    convw = {t: sb("convw" + t, 128, 8 * 4) for t in "qkv"}
    firw_s = sb("firw_s", 128, 8 * 3)
    firw_l = sb("firw_l", 128, 8 * 63)
    lam_t = sb("lam", 4, 64)
    lamR = sb("lamR", 4, 512)
    rampPT = sb("rampPT", 128, 8)
    rampNT = sb("rampNT", 128, 8)
    lamcol = sb("lamcol", 128, 16)
    beta_f = sb("beta_f", 4, 512)
    wbt = sb("wbt", 128, 8 * 4)
    misc = sb("misct", 128, 512)
    row1 = sb("row1", 1, 2048)

    psum = lambda name, p, f: es.enter_context(nc.psum_tensor(name, [p, f], F32))

    block = es.enter_context(nc.Block())

    def ld(dst, src, eng="sync"):
        return P.ins(eng, lambda E, d=dst, s=src: E.dma_start(out=d, in_=s), dma=True)

    def st(dst, src, eng="gpsimd"):
        return P.ins(eng, lambda E, d=dst, s=src: E.dma_start(out=d, in_=s), dma=True)

    tl = []
    for n, t in [("ident", ident), ("ones128", ones128), ("sel4", sel4), ("flagv", flag),
                 ("maskl32", maskl32), ("masku32", masku32), ("i32", i32),
                 ("maskl32r", maskl32r), ("masku32r", masku32r), ("i32r", i32r)]:
        tl.append(ld(t[:], IN[n]))
    for t in "qkv":
        for ci in range(8):
            tl.append(ld(convw[t][:, ci * 4:(ci + 1) * 4], IN["conv" + t][ci * 128:(ci + 1) * 128, :]))
    for ci in range(8):
        tl.append(ld(wbt[:, ci * 4:(ci + 1) * 4], IN["wb"][ci * 128:(ci + 1) * 128, :]))
        tl.append(ld(firw_s[:, ci * 3:(ci + 1) * 3], IN["fir_s"][ci * 128:(ci + 1) * 128, :]))
        tl.append(ld(firw_l[:, ci * 63:(ci + 1) * 63], IN["fir_l"][ci * 128:(ci + 1) * 128, :]))
    tl.append(ld(lam_t[0:4, 0:1], IN["fparam"]))
    gd_t = [sb(f"gdt{i}", 128, 256) for i in range(2)]
    for i in range(2):
        tl.append(ld(gd_t[i][:], IN["gdiag%d" % i]))

    P.waits("scalar", tl)
    P.ins("scalar", lambda E: E.activation(lam_t[0:4, 1:2], lam_t[0:4, 0:1], AF.Sigmoid))
    ts_ = P.token("scalar")
    P.waits("vector", tl + [ts_])
    V = lambda fn, inc=False: P.ins("vector", fn, inc=inc)
    V(lambda E: E.tensor_scalar(lam_t[0:4, 2:3], lam_t[0:4, 1:2], 0.45, 0.55, op0=OP.mult, op1=OP.add))
    V(lambda E: E.memset(lam_t[0:4, 8:9], 1.0))
    V(lambda E: E.drain())
    for j in range(1, 9):
        V(lambda E, j=j: E.tensor_tensor(lam_t[0:4, 8 + j:9 + j], lam_t[0:4, 7 + j:8 + j], lam_t[0:4, 2:3], op=OP.mult))
    V(lambda E: E.reciprocal(lam_t[0:4, 3:4], lam_t[0:4, 2:3]))
    V(lambda E: E.drain())
    V(lambda E: E.memset(lam_t[0:4, 24:25], 1.0))
    V(lambda E: E.drain())
    for l in range(1, 8):
        V(lambda E, l=l: E.tensor_tensor(lam_t[0:4, 24 + l:25 + l], lam_t[0:4, 23 + l:24 + l], lam_t[0:4, 3:4], op=OP.mult))
    V(lambda E: E.tensor_scalar(lam_t[0:4, 5:6], lam_t[0:4, 3:4], -1.0, None, op0=OP.mult))
    for j in range(8):
        V(lambda E, j=j: E.tensor_copy(lamR[0:4, 32 * j:32 * j + 32], bcast_ap(lam_t[0:4, 8 + j:9 + j], 32)))
        V(lambda E, j=j: E.tensor_copy(lamR[0:4, 256 + 32 * j:256 + 32 * j + 32], bcast_ap(lam_t[0:4, 24 + j:25 + j], 32)))
    tlam = P.token("vector")

    psB0 = psum("psB0", 128, 512)
    psB1 = psum("psB1", 128, 512)
    P.wait("tensor", tlam)
    TT = lambda fn, inc=False: P.ins("tensor", fn, inc=inc)
    for h in range(H):
        sl = sel4[:, h * 128:(h + 1) * 128]
        TT(lambda E, sl=sl: E.matmul(psB1[:, 0:1], sl, lam_t[0:4, 5:6], start=True, stop=True))
        TT(lambda E, sl=sl: E.matmul(psB1[:, 1:2], sl, lam_t[0:4, 2:3], start=True, stop=True))
        TT(lambda E, sl=sl: E.matmul(psB1[:, 2:3], sl, lam_t[0:4, 15:16], start=True, stop=True))
        TT(lambda E, sl=sl: E.matmul(psB1[:, 3:4], sl, lam_t[0:4, 3:4], start=True, stop=True), inc=True)
        tmm = ("tensor_t", P.cnt["tensor_t"])
        P.wait("vector", tmm)
        V(lambda E, h=h: E.tensor_copy(lamcol[:, 4 * h:4 * h + 4], psB1[:, 0:4]), inc=True)
        tev = ("vector_t", P.cnt["vector_t"])
        P.wait("tensor", tev)
    tlam2 = P.token("vector")
    P.wait("tensor", tlam2)
    for tj in range(2):
        TT(lambda E, tj=tj: E.transpose(psB0[:, 0:4], lamR[0:4, tj * 128:tj * 128 + 128], ident[0:4, 0:4]))
        TT(lambda E, tj=tj: E.transpose(psB0[:, 4:8], lamR[0:4, 256 + tj * 128:256 + tj * 128 + 128], ident[0:4, 0:4]), inc=True)
        tmm = ("tensor_t", P.cnt["tensor_t"])
        P.wait("vector", tmm)
        V(lambda E, tj=tj: E.tensor_copy(rampPT[:, tj * 4:tj * 4 + 4], psB0[:, 0:4]))
        V(lambda E, tj=tj: E.tensor_copy(rampNT[:, tj * 4:tj * 4 + 4], psB0[:, 4:8]), inc=True)
        P.wait("tensor", ("vector_t", P.cnt["vector_t"]))

    P.barrier()
    if upto == "P0":
        P.finalize(block)
        es.close()
        return nc

    es2 = contextlib.ExitStack()
    sb2 = lambda name, p, f, dt=F32: es2.enter_context(nc.sbuf_tensor(name, [p, f], dt))

    wk_t = sb2("wk_t", 128, 8192)
    wv_t = sb2("wv_t", 128, 8192)
    wq_t = sb2("wq_t", 128, 8192)
    twl = []
    for name, t in [("wk", wk_t), ("wv", wv_t), ("wq", wq_t)]:
        for ki in range(8):
            twl.append(ld(t[:, ki * 1024:(ki + 1) * 1024], IN[name][ki * 128:(ki + 1) * 128, :]))

    x_tm = sb2("x_tm", 128, 1024)
    xn_tm = sb2("xn_tm", 128, 4 * 1024)
    xq_tm = sb2("xq_tm", 128, 1024)
    xnT = sb2("xnT", 128, 8 * 512)
    xqT = sb2("xqT", 128, 8 * 512)
    pslice = sb2("pslice", 128, 515)
    tails = {t: sb2("tails" + t, 128, 24) for t in "qkv"}
    kbuf = sb2("kbuf", 128, 1024)
    qbuf = sb2("qbuf", 128, 1024)
    vbuf = sb2("vbuf", 128, 512)
    sqr = sb2("sqr", 128, 1024)
    lncol = sb2("lncol", 128, 16)
    qrow = sb2("qrow", 128, 64)
    uq = sb2("uq", 128, 1024)
    gq = sb2("gq", 128, 1024)
    sgq = sb2("sgq", 128, 1024)

    psP = [psB0, psB1]
    psT = es2.enter_context(nc.psum_tensor("psT", [128, 512], F32))
    psR = es2.enter_context(nc.psum_tensor("psR", [4, 512], F32))
    psL = es2.enter_context(nc.psum_tensor("psL", [1, 512], F32))

    V(lambda E: E.memset(tails["q"][:], 0.0))

    V(lambda E: E.drain())
    V(lambda E: E.memset(tails["k"][:], 0.0))
    V(lambda E: E.drain())
    V(lambda E: E.memset(tails["v"][:], 0.0))
    V(lambda E: E.drain())
    tz = P.token("vector")
    P.waits("tensor", twl + [tz])
    P.waits("scalar", twl + [tz])

    xtm_free = [None]
    evtok = [None, None]
    for pp in range(NPP):
        own = pp >= 4
        for s in range(4):
            P.wait("sync", xtm_free[0])
            tx = ld(x_tm[:], IN["x12"][pp * 512 + s * 128: pp * 512 + (s + 1) * 128, :])
            P.wait("vector", tx)
            V(lambda E: E.tensor_reduce(lncol[:, 0:1], x_tm[:], axis=AXX, op=OP.add))
            V(lambda E: E.tensor_tensor(sqr[:], x_tm[:], x_tm[:], op=OP.mult))
            V(lambda E: E.drain())
            V(lambda E: E.tensor_reduce(lncol[:, 1:2], sqr[:], axis=AXX, op=OP.add))
            V(lambda E: E.tensor_scalar(lncol[:, 2:3], lncol[:, 0:1], 1.0 / 1024, None, op0=OP.mult))
            V(lambda E: E.tensor_scalar(lncol[:, 3:4], lncol[:, 1:2], 1.0 / 1024, None, op0=OP.mult))
            V(lambda E: E.tensor_tensor(lncol[:, 4:5], lncol[:, 2:3], lncol[:, 2:3], op=OP.mult))
            V(lambda E: E.tensor_tensor(lncol[:, 5:6], lncol[:, 3:4], lncol[:, 4:5], op=OP.subtract))
            V(lambda E: E.tensor_scalar(lncol[:, 5:6], lncol[:, 5:6], 1e-5, None, op0=OP.add), inc=True)
            P.wait("scalar", ("vector_t", P.cnt["vector_t"]))
            P.ins("scalar", lambda E: E.activation(lncol[:, 6:7], lncol[:, 5:6], AF.Sqrt), inc=True)
            P.wait("vector", ("scalar_t", P.cnt["scalar_t"]))
            V(lambda E: E.reciprocal(lncol[:, 7:8], lncol[:, 6:7]))
            V(lambda E: E.drain())
            V(lambda E: E.tensor_tensor(lncol[:, 8:9], lncol[:, 2:3], lncol[:, 7:8], op=OP.mult))
            V(lambda E: E.tensor_scalar(lncol[:, 9:10], lncol[:, 8:9], -1.0, None, op0=OP.mult))
            V(lambda E, s=s: E.tensor_scalar(
                xn_tm[:, s * 1024:(s + 1) * 1024], x_tm[:], lncol[:, 7:8], None, op0=OP.mult))
            V(lambda E, s=s: E.tensor_scalar(
                xn_tm[:, s * 1024:(s + 1) * 1024], xn_tm[:, s * 1024:(s + 1) * 1024],
                lncol[:, 9:10], None, op0=OP.add), inc=True)
            tln = ("vector_t", P.cnt["vector_t"])
            xtm_free[0] = tln
            if own and "quant" not in skip:
                xnt = xn_tm[:, s * 1024:(s + 1) * 1024]
                xv = xnt.rearrange("p (nb b) -> p nb b", b=16)
                V(lambda E: E.drain())
                V(lambda E, xv=xv: E.tensor_reduce(qrow[:], xv, axis=AXX, op=OP.max, apply_absolute_value=True))
                V(lambda E: E.tensor_scalar(qrow[:], qrow[:], 1.0 / 6, 1e-12, op0=OP.mult, op1=OP.max))
                V(lambda E: E.reciprocal(misc[:, 0:64], qrow[:]))
                V(lambda E: E.drain())
                rb = bcast_ap(misc[:, 0:64], 16)
                V(lambda E, xv=xv, rb=rb: E.tensor_tensor(uq[:].rearrange("p (nb b) -> p nb b", b=16), xv, rb, op=OP.mult))
                V(lambda E: E.tensor_scalar(sgq[:], uq[:], 0.0, None, op0=OP.is_gt))
                V(lambda E: E.tensor_scalar(gq[:], uq[:], 0.0, None, op0=OP.is_lt))
                V(lambda E: E.tensor_tensor(sgq[:], sgq[:], gq[:], op=OP.subtract))
                V(lambda E: E.tensor_tensor(uq[:], uq[:], sgq[:], op=OP.mult))
                V(lambda E: E.tensor_scalar(gq[:], uq[:], 0.25, None, op0=OP.is_gt))
                for thr in (0.75, 1.25, 1.75):
                    V(lambda E, thr=thr: E.scalar_tensor_tensor(gq[:], uq[:], thr, gq[:], op0=OP.is_gt, op1=OP.add))
                V(lambda E: E.tensor_scalar(gq[:], gq[:], 0.5, None, op0=OP.mult))
                for thr in (2.5, 3.5, 5.0, 5.0):
                    V(lambda E, thr=thr: E.scalar_tensor_tensor(gq[:], uq[:], thr, gq[:], op0=OP.is_gt, op1=OP.add))
                V(lambda E: E.tensor_tensor(gq[:], gq[:], sgq[:], op=OP.mult))
                qb = bcast_ap(qrow[:], 16)
                V(lambda E, qb=qb: E.tensor_tensor(
                    xq_tm[:].rearrange("p (nb b) -> p nb b", b=16),
                    gq[:].rearrange("p (nb b) -> p nb b", b=16), qb, op=OP.mult), inc=True)
                tln = ("vector_t", P.cnt["vector_t"])
            P.wait("tensor", tln)
            if "trans" in skip:
                continue
            srcs = [(xn_tm[:, s * 1024:(s + 1) * 1024], xnT)] + ([(xq_tm[:], xqT)] if own else [])
            for srct, dstt in srcs:
                for cg in range(2):
                    for cj in range(4):
                        ci = cg * 4 + cj
                        TT(lambda E, ci=ci, cj=cj, srct=srct: E.transpose(
                            psT[:, cj * 128:(cj + 1) * 128],
                            srct[:, ci * 128:(ci + 1) * 128], ident[:]), inc=(cj == 3))
                    tmm = ("tensor_t", P.cnt["tensor_t"])
                    P.wait("scalar", tmm)
                    for cj in range(4):
                        ci = cg * 4 + cj
                        P.ins("scalar", lambda E, s=s, ci=ci, cj=cj, dstt=dstt: E.copy(
                            dstt[:, ci * 512 + s * 128: ci * 512 + (s + 1) * 128],
                            psT[:, cj * 128:(cj + 1) * 128]), inc=(cj == 3))
                    P.wait("tensor", ("scalar_t", P.cnt["scalar_t"]))
        ttr = P.token("scalar")
        if own:
            P.wait("gpsimd", ttr)
            for ci in range(8):
                st(xnT_s[(pp - 4) * 8 + ci], xnT[:, ci * 512:(ci + 1) * 512])
                st(xqT_s[(pp - 4) * 8 + ci], xqT[:, ci * 512:(ci + 1) * 512])
        if pp == 4 and "xn_dbg" in dbg:
            P.wait("gpsimd", ttr)
            P.wait("gpsimd", P.token("vector"))
            st(xn_dbg, xn_tm[:])
            st(xnT_dbg, xnT[:])
            st(ln_dbg, lncol[:])
            st(xtm_dbg, x_tm[:])
            P.wait("vector", ("gpsimd_d", P.cnt["gpsimd_d"]))
            P.wait("scalar", ("gpsimd_d", P.cnt["gpsimd_d"]))
            P.wait("sync", ("gpsimd_d", P.cnt["gpsimd_d"]))

        if "rest" in skip:
            continue
        P.wait("tensor", ttr)
        for ki in range(8):
            TT(lambda E, ki=ki: E.matmul(psR[:], wbt[:, ki * 4:(ki + 1) * 4],
                                         xnT[:, ki * 512:(ki + 1) * 512],
                                         start=(ki == 0), stop=(ki == 7)), inc=(ki == 7))
        P.wait("scalar", ("tensor_t", P.cnt["tensor_t"]))
        P.ins("scalar", lambda E: E.activation(beta_f[0:4, :], psR[:], AF.Sigmoid), inc=True)
        tbeta = ("scalar_t", P.cnt["scalar_t"])
        P.wait("gpsimd", tbeta)
        st(beta_s[pp], beta_f[0:4, :])
        P.wait("tensor", tbeta)
        P.wait("scalar", ("gpsimd_d", P.cnt["gpsimd_d"]))

        if pp == 3:
            P.wait("tensor", evtok[0])
            for mi in range(8):
                for ki in range(8):
                    TT(lambda E, ki=ki, mi=mi: E.matmul(
                        psP[0][:, 0:3], wq_t[:, ki * 1024 + mi * 128: ki * 1024 + (mi + 1) * 128],
                        xnT[:, ki * 512 + 509: ki * 512 + 512],
                        start=(ki == 0), stop=(ki == 7)), inc=(ki == 7))
                P.wait("vector", ("tensor_t", P.cnt["tensor_t"]))
                V(lambda E, mi=mi: E.tensor_copy(tails["q"][:, mi * 3:(mi + 1) * 3], psP[0][:, 0:3]), inc=True)
                P.wait("tensor", ("vector_t", P.cnt["vector_t"]))

        if pp == 4:
            for tt in "qkv":
                V(lambda E, tt=tt: E.tensor_scalar(tails[tt][:], tails[tt][:], flag[:, 0:1], None, op0=OP.mult))
            tfl = P.token("vector")
            P.wait("tensor", tfl)

        tensors = ["k", "v"] + (["q"] if own else [])
        for t in tensors:
            wt = {"q": wq_t, "k": wk_t, "v": wv_t}[t]
            buf = {"q": qbuf, "k": kbuf, "v": vbuf}[t]
            for mi in range(8):
                pb = mi % 2
                P.wait("tensor", evtok[pb])
                for ki in range(8):
                    TT(lambda E, wt=wt, ki=ki, mi=mi, pb=pb: E.matmul(
                        psP[pb][:], wt[:, ki * 1024 + mi * 128: ki * 1024 + (mi + 1) * 128],
                        xnT[:, ki * 512:(ki + 1) * 512],
                        start=(ki == 0), stop=(ki == 7)), inc=(ki == 7))
                tmm = ("tensor_t", P.cnt["tensor_t"])
                P.wait("vector", tmm)
                V(lambda E, t=t, mi=mi: E.tensor_copy(pslice[:, 0:3], tails[t][:, mi * 3:(mi + 1) * 3]))
                V(lambda E, pb=pb: E.tensor_copy(pslice[:, 3:515], psP[pb][:]), inc=True)
                evtok[pb] = ("vector_t", P.cnt["vector_t"])
                V(lambda E, t=t, mi=mi: E.tensor_copy(tails[t][:, mi * 3:(mi + 1) * 3], pslice[:, 512:515]))
                cw = convw[t]
                bo = (mi % 2) * 512 if t in "kq" else 0
                V(lambda E, buf=buf, bo=bo, cw=cw, mi=mi: E.tensor_scalar(
                    buf[:, bo:bo + 512], pslice[:, 0:512],
                    cw[:, mi * 4 + 0: mi * 4 + 1], None, op0=OP.mult))
                for j in (1, 2, 3):
                    V(lambda E, buf=buf, bo=bo, cw=cw, mi=mi, j=j: E.scalar_tensor_tensor(
                        buf[:, bo:bo + 512], pslice[:, j: 512 + j],
                        cw[:, mi * 4 + j: mi * 4 + j + 1], buf[:, bo:bo + 512],
                        op0=OP.mult, op1=OP.add), inc=(j == 3))
                P.wait("scalar", ("vector_t", P.cnt["vector_t"]))
                P.ins("scalar", lambda E, buf=buf, bo=bo: E.activation(
                    buf[:, bo:bo + 512], buf[:, bo:bo + 512], AF.Silu), inc=True)
                tsil = ("scalar_t", P.cnt["scalar_t"])
                if t == "v":
                    P.wait("gpsimd", tsil)
                    st(v_s[pp * 8 + mi], vbuf[:, 0:512])
                    P.wait("vector", ("gpsimd_d", P.cnt["gpsimd_d"]))
                    P.wait("scalar", ("gpsimd_d", P.cnt["gpsimd_d"]))
                elif mi % 2 == 1:
                    h = mi // 2
                    P.wait("vector", tsil)
                    for c2 in range(2):
                        V(lambda E, buf=buf, c2=c2: E.tensor_tensor(
                            sqr[:, c2 * 512:(c2 + 1) * 512], buf[:, c2 * 512:(c2 + 1) * 512],
                            buf[:, c2 * 512:(c2 + 1) * 512], op=OP.mult), inc=(c2 == 1))
                    P.wait("tensor", ("vector_t", P.cnt["vector_t"]))
                    for c2 in range(2):
                        TT(lambda E, c2=c2: E.matmul(psL[0:1, :], ones128[:, 0:1],
                                                     sqr[:, c2 * 512:(c2 + 1) * 512],
                                                     start=(c2 == 0), stop=(c2 == 1)), inc=(c2 == 1))
                    P.wait("vector", ("tensor_t", P.cnt["tensor_t"]))
                    V(lambda E: E.tensor_scalar(row1[0:1, 0:512], psL[0:1, :], 1e-6, None, op0=OP.add), inc=True)
                    P.wait("scalar", ("vector_t", P.cnt["vector_t"]))
                    P.ins("scalar", lambda E: E.activation(row1[0:1, 0:512], row1[0:1, 0:512], AF.Sqrt), inc=True)
                    P.wait("vector", ("scalar_t", P.cnt["scalar_t"]))
                    V(lambda E: E.reciprocal(row1[0:1, 0:512], row1[0:1, 0:512]))
                    V(lambda E: E.drain(), inc=True)
                    P.wait("tensor", ("vector_t", P.cnt["vector_t"]))
                    TT(lambda E: E.matmul(psT[:], ones128[0:1, :], row1[0:1, 0:512],
                                          start=True, stop=True), inc=True)
                    P.wait("vector", ("tensor_t", P.cnt["tensor_t"]))
                    for c2 in range(2):
                        V(lambda E, buf=buf, c2=c2: E.tensor_tensor(
                            buf[:, c2 * 512:(c2 + 1) * 512], buf[:, c2 * 512:(c2 + 1) * 512],
                            psT[:], op=OP.mult), inc=(c2 == 1))
                    tl2 = ("vector_t", P.cnt["vector_t"])
                    P.wait("tensor", tl2)
                    P.wait("gpsimd", tl2)
                    dst = khat_s if t == "k" else qhat_s
                    base = pp * 8 if t == "k" else (pp - 4) * 8
                    st(dst[base + mi - 1], buf[:, 0:512])
                    st(dst[base + mi], buf[:, 512:1024])
                    P.wait("vector", ("gpsimd_d", P.cnt["gpsimd_d"]))
                    P.wait("scalar", ("gpsimd_d", P.cnt["gpsimd_d"]))

    P.barrier()
    es2.close()
    if upto == "P2":
        P.finalize(block)
        es.close()
        return nc

    es3 = contextlib.ExitStack()
    sb3 = lambda name, p, f, dt=F32: es3.enter_context(nc.sbuf_tensor(name, [p, f], dt))
    ps3 = lambda name, p, f: es3.enter_context(nc.psum_tensor(name, [p, f], F32))

    ghmask = [[sb3(f"ghm{h}_{i}", 128, 256) for i in range(2)] for h in range(H)]
    lamPb = [sb3(f"lamPb{h}", 128, 256) for h in range(H)]
    lamNb = [sb3(f"lamNb{h}", 128, 256) for h in range(H)]
    kw = sb3("kw", 128, 8 * 256)
    vw = sb3("vw", 128, 8 * 256)
    qw = sb3("qw", 128, 8 * 256)
    kbf = sb3("kbf", 128, 8 * 256)
    knf = sb3("knf", 128, 8 * 256)
    qtf = sb3("qtf", 128, 8 * 256)
    kT = sb3("kT", 128, 2048)
    vT = sb3("vT", 128, 2048)
    kbT = sb3("kbT", 128, 2048)
    ktT = sb3("ktT", 128, 2048)
    vbT = sb3("vbT", 128, 2048)
    bw = sb3("bw", 4, 256)
    bT = sb3("bT", 128, 8)
    colsc = sb3("colsc", 128, 8)
    Lst = sb3("Lst", 128, 256)
    Ust = sb3("Ust", 128, 256)
    Dst = sb3("Dst", 128, 256)
    Lp = sb3("Lp", 128, 256)
    Up = sb3("Up", 128, 256)
    DfinPad = sb3("DfinPad", 128, 1024)
    ut = sb3("ut", 128, 2048)
    wf = sb3("wf", 128, 2048)
    WSt = sb3("WSt", 128, 2048)
    QSt = sb3("QSt", 128, 2048)
    PTt = sb3("PTt", 128, 512)
    GTt = sb3("GTt", 128, 512)
    upb = sb3("upb", 128, 2048)
    S_t = sb3("S_t", 128, 2048)
    osb = sb3("osb", 128, 256)
    psW0 = ps3("psW0", 128, 512)
    psW1 = ps3("psW1", 128, 512)
    psW2 = ps3("psW2", 128, 512)
    psW3 = ps3("psW3", 128, 1024)

    V(lambda E: E.memset(S_t[:], 0.0))
    V(lambda E: E.drain())
    for h in range(H):
        sl = sel4[:, h * 128:(h + 1) * 128]
        TT(lambda E, sl=sl: E.matmul(psW0[:, 0:256], sl, lamR[0:4, 0:256], start=True, stop=True))
        TT(lambda E, sl=sl: E.matmul(psW0[:, 256:512], sl, lamR[0:4, 256:512], start=True, stop=True), inc=True)
        P.wait("vector", ("tensor_t", P.cnt["tensor_t"]))
        V(lambda E, h=h: E.tensor_copy(lamPb[h][:], psW0[:, 0:256]))
        V(lambda E, h=h: E.tensor_copy(lamNb[h][:], psW0[:, 256:512]), inc=True)
        P.wait("tensor", ("vector_t", P.cnt["vector_t"]))
    tgl = []
    for h in range(H):
        for i in range(2):
            tgl.append(ld(ghmask[h][i][:], IN["gstrict%d" % i]))
    P.waits("vector", tgl)
    for h in range(H):
        for i in range(2):
            V(lambda E, h=h, i=i: E.scalar_tensor_tensor(
                ghmask[h][i][:], ghmask[h][i][:], lamcol[:, 4 * h + 3:4 * h + 4], gd_t[i][:],
                op0=OP.mult, op1=OP.add))
    tpre = P.token("vector")
    P.waits("tensor", [tpre])
    P.waits("sync", [tpre])

    nwin_run = 1 if upto == "P3w1" else NWIN
    wqbd = [sb3("wqbA", 128, 512), sb3("wqbB", 128, 512), sb3("wqbC", 128, 512)]
    uqad = [sb3("uqaA", 128, 512), sb3("uqaB", 128, 512)]
    gqa = sb3("gqa", 128, 512)
    sga = sb3("sga", 128, 512)
    qra = sb3("qra", 128, 32)
    rca = sb3("rca", 128, 32)
    psA5 = ps3("psA5", 128, 512)

    def emit_quant_a(buf, K):
        nb = K // 16
        bv = buf.rearrange("p (nb b) -> p nb b", b=16)
        V(lambda E: E.drain())
        V(lambda E, bv=bv, nb=nb: E.tensor_reduce(qra[:, 0:nb], bv, axis=AXX, op=OP.max, apply_absolute_value=True))
        V(lambda E, nb=nb: E.tensor_scalar(qra[:, 0:nb], qra[:, 0:nb], 1.0 / 6, 1e-12, op0=OP.mult, op1=OP.max))
        V(lambda E, nb=nb: E.reciprocal(rca[:, 0:nb], qra[:, 0:nb]))
        V(lambda E: E.drain())
        rb = bcast_ap(rca[:, 0:nb], 16)
        V(lambda E, buf=buf, bv=bv, rb=rb, K=K: E.tensor_tensor(buf.rearrange("p (nb b) -> p nb b", b=16), bv, rb, op=OP.mult))
        V(lambda E, buf=buf, K=K: E.tensor_scalar(sga[:, 0:K], buf[:, 0:K], 0.0, None, op0=OP.is_gt))
        V(lambda E, buf=buf, K=K: E.tensor_scalar(gqa[:, 0:K], buf[:, 0:K], 0.0, None, op0=OP.is_lt))
        V(lambda E, K=K: E.tensor_tensor(sga[:, 0:K], sga[:, 0:K], gqa[:, 0:K], op=OP.subtract))
        V(lambda E, buf=buf, K=K: E.tensor_tensor(buf[:, 0:K], buf[:, 0:K], sga[:, 0:K], op=OP.mult))
        V(lambda E, buf=buf, K=K: E.tensor_scalar(gqa[:, 0:K], buf[:, 0:K], 0.25, None, op0=OP.is_gt))
        for thr in (0.75, 1.25, 1.75):
            V(lambda E, buf=buf, K=K, thr=thr: E.scalar_tensor_tensor(gqa[:, 0:K], buf[:, 0:K], thr, gqa[:, 0:K], op0=OP.is_gt, op1=OP.add))
        V(lambda E, K=K: E.tensor_scalar(gqa[:, 0:K], gqa[:, 0:K], 0.5, None, op0=OP.mult))
        for thr in (2.5, 3.5, 5.0, 5.0):
            V(lambda E, buf=buf, K=K, thr=thr: E.scalar_tensor_tensor(gqa[:, 0:K], buf[:, 0:K], thr, gqa[:, 0:K], op0=OP.is_gt, op1=OP.add))
        V(lambda E, K=K: E.tensor_tensor(gqa[:, 0:K], gqa[:, 0:K], sga[:, 0:K], op=OP.mult))
        qb = bcast_ap(qra[:, 0:nb], 16)
        V(lambda E, buf=buf, K=K, qb=qb: E.tensor_tensor(
            buf.rearrange("p (nb b) -> p nb b", b=16),
            gqa[:, 0:K].rearrange("p (nb b) -> p nb b", b=16), qb, op=OP.mult), inc=True)
        return ("vector_t", P.cnt["vector_t"])

    p5a_jobs = []
    for (wname, wtgt, nrc, KT) in (("wg", wgT_s, 22, 1024), ("wu", wuT_s, 22, 1024), ("wd", wdT_s, 8, 2816)):
        for rc in range(nrc):
            c0 = 0
            while c0 < KT:
                K = min(512, KT - c0)
                p5a_jobs.append((wname, wtgt, rc, c0, K))
                c0 += K
    NJOB = len(p5a_jobs)
    p5c = {"ld": 0, "q": 0, "tr": 0}
    for _ps in ("p5at0", "p5at1", "w2s"):
        P.sem[_ps] = es.enter_context(nc.semaphore(_ps))
        P.cnt[_ps] = 0
    t_ld = {}
    t_q = {}
    t_tr = {}
    t_ev = {}
    t_sp = {}

    def p5a_slot(drain=False):
        if p5c["tr"] < p5c["q"] - (0 if drain else 1):
            n = p5c["tr"]
            (wname, wtgt, rc, c0, K) = p5a_jobs[n]
            P.wait("tensor", t_q[n])
            P.wait("tensor", t_ev.get(n - 1))
            sname = "p5at%d" % (n % 2)
            for kc in range(K // 128):
                last = kc == K // 128 - 1
                if last:
                    P.cnt[sname] += 1
                    P.ins("tensor", lambda E, kc=kc, n=n, sname=sname: E.transpose(
                        psA5[:, kc * 128:(kc + 1) * 128],
                        wqbd[n % 3][:, kc * 128:(kc + 1) * 128], ident[:]).then_inc(P.sem[sname], 1))
                else:
                    TT(lambda E, kc=kc, n=n: E.transpose(psA5[:, kc * 128:(kc + 1) * 128],
                                                         wqbd[n % 3][:, kc * 128:(kc + 1) * 128], ident[:]))
            t_tr[n] = (sname, P.cnt[sname])
            P.wait("scalar", t_tr[n])
            P.wait("scalar", t_sp.get(n - 2))
            for kc in range(K // 128):
                P.ins("scalar", lambda E, kc=kc, n=n: E.copy(
                    uqad[n % 2][:, kc * 128:(kc + 1) * 128], psA5[:, kc * 128:(kc + 1) * 128]),
                    inc=(kc == K // 128 - 1))
            t_ev[n] = ("scalar_t", P.cnt["scalar_t"])
            for kc in range(K // 128):
                t_sp[n] = st(wtgt[c0 // 128 + kc][:, rc * 128:(rc + 1) * 128],
                             uqad[n % 2][:, kc * 128:(kc + 1) * 128], eng="scalar")
            p5c["tr"] = n + 1
        if p5c["ld"] < NJOB and p5c["ld"] - p5c["tr"] < 3:
            m = p5c["ld"]
            (wname, wtgt, rc, c0, K) = p5a_jobs[m]
            t_ld[m] = ld(wqbd[m % 3][:, 0:K], IN[wname][rc * 128:(rc + 1) * 128, c0:c0 + K], eng="scalar")
            p5c["ld"] = m + 1
        if p5c["q"] < p5c["ld"]:
            p = p5c["q"]
            (wname, wtgt, rc, c0, K) = p5a_jobs[p]
            P.wait("vector", t_ld[p])
            t_q[p] = emit_quant_a(wqbd[p % 3][:, 0:K], K)
            p5c["q"] = p + 1

    rot_banks = [psB0, psB1, psW0]
    rot = {"i": 0, "ev": [None, None, None]}

    def rot_acquire():
        i = rot["i"]
        rot["i"] += 1
        P.wait("tensor", rot["ev"][i % 3])
        return rot_banks[i % 3]

    def rot_release(tok):
        rot["ev"][(rot["i"] - 1) % 3] = tok

    def rot_fence():
        P.waits("tensor", [t for t in rot["ev"] if t is not None])

    rot["ev"] = [tpre, tpre, tpre]
    win_free = [None]
    for wi in range(nwin_run):
        own = wi >= NWIN // 2
        p4 = wi // 2
        off = (wi % 2) * 256
        W0 = wi * 256

        if wi == NWIN // 2:
            V(lambda E: E.tensor_scalar(S_t[:], S_t[:], flag[:, 0:1], None, op0=OP.mult), inc=True)
            P.wait("tensor", ("vector_t", P.cnt["vector_t"]))

        P.wait("sync", win_free[0])
        tload = []
        for ci in range(8):
            tload.append(ld(kw[:, ci * 256:(ci + 1) * 256], khat_s[p4 * 8 + ci][:, off:off + 256]))
            tload.append(ld(vw[:, ci * 256:(ci + 1) * 256], v_s[p4 * 8 + ci][:, off:off + 256]))
            if own:
                tload.append(ld(qw[:, ci * 256:(ci + 1) * 256], qhat_s[(p4 - 4) * 8 + ci][:, off:off + 256]))
        tload.append(ld(bw[:], beta_s[p4][:, off:off + 256]))
        if p3stop < 1:
            continue
        P.waits("tensor", tload)
        P.waits("vector", tload)

        for tj in range(2):
            rps = rot_acquire()
            TT(lambda E, tj=tj, rps=rps: E.transpose(rps[:, 0:4], bw[0:4, tj * 128:(tj + 1) * 128], ident[0:4, 0:4]), inc=True)
            P.wait("vector", ("tensor_t", P.cnt["tensor_t"]))
            V(lambda E, tj=tj, rps=rps: E.tensor_copy(bT[:, tj * 4:tj * 4 + 4], rps[:, 0:4]), inc=True)
            rot_release(("vector_t", P.cnt["vector_t"]))

        if p3stop < 2:
            continue
        for h in range(H):
            sl = sel4[:, h * 128:(h + 1) * 128]
            rps = rot_acquire()
            TT(lambda E, sl=sl, rps=rps: E.matmul(rps[:, 0:256], sl, bw[0:4, :], start=True, stop=True), inc=True)
            P.wait("vector", ("tensor_t", P.cnt["tensor_t"]))
            for ci in (2 * h, 2 * h + 1):
                V(lambda E, ci=ci, rps=rps: E.tensor_tensor(kbf[:, ci * 256:(ci + 1) * 256],
                                                            kw[:, ci * 256:(ci + 1) * 256], rps[:, 0:256], op=OP.mult))
                V(lambda E, ci=ci, h=h: E.tensor_tensor(knf[:, ci * 256:(ci + 1) * 256],
                                                        kw[:, ci * 256:(ci + 1) * 256], lamNb[h][:], op=OP.mult))
                if own:
                    V(lambda E, ci=ci, h=h: E.tensor_tensor(qtf[:, ci * 256:(ci + 1) * 256],
                                                            qw[:, ci * 256:(ci + 1) * 256], lamPb[h][:], op=OP.mult))
            V(lambda E: E.nop(), inc=True)
            rot_release(("vector_t", P.cnt["vector_t"]))

        if p3stop < 3:
            continue
        for (srct, dstt) in ((kw, kT), (vw, vT)):
            for h in range(H):
                for tj in range(2):
                    rps = rot_acquire()
                    for c2 in range(2):
                        ci = 2 * h + c2
                        TT(lambda E, srct=srct, ci=ci, tj=tj, c2=c2, rps=rps: E.transpose(
                            rps[:, c2 * 128:c2 * 128 + 128],
                            srct[:, ci * 256 + tj * 128: ci * 256 + tj * 128 + 128], ident[:]), inc=(c2 == 1))
                    P.wait("scalar", ("tensor_t", P.cnt["tensor_t"]))
                    for c2 in range(2):
                        P.ins("scalar", lambda E, dstt=dstt, h=h, tj=tj, c2=c2, rps=rps: E.copy(
                            dstt[:, (h * 2 + tj) * 256 + c2 * 128: (h * 2 + tj) * 256 + c2 * 128 + 128],
                            rps[:, c2 * 128:c2 * 128 + 128]), inc=(c2 == 1))
                    rot_release(("scalar_t", P.cnt["scalar_t"]))
        ttr3 = P.token("scalar")
        P.wait("vector", ttr3)

        if p3stop < 4:
            continue
        for h in range(H):
            for tj in range(2):
                r = (h * 2 + tj) * 256
                V(lambda E, h=h, tj=tj: E.tensor_tensor(colsc[:, 0:1], bT[:, tj * 4 + h:tj * 4 + h + 1],
                                                        rampPT[:, tj * 4 + h:tj * 4 + h + 1], op=OP.mult))
                V(lambda E, r=r: E.tensor_scalar(kbT[:, r:r + 256], kT[:, r:r + 256], colsc[:, 0:1], None, op0=OP.mult))
                V(lambda E, r=r, h=h, tj=tj: E.tensor_scalar(ktT[:, r:r + 256], kT[:, r:r + 256],
                                                             rampNT[:, tj * 4 + h:tj * 4 + h + 1], None, op0=OP.mult))
                V(lambda E, r=r, h=h, tj=tj: E.tensor_scalar(vbT[:, r:r + 256], vT[:, r:r + 256],
                                                             bT[:, tj * 4 + h:tj * 4 + h + 1], None, op0=OP.mult))
        tsc = P.token("vector")
        P.wait("tensor", tsc)
        for _ in range(3):
            p5a_slot()

        if p3stop < 5:
            continue
        rot_fence()
        def slot(h, c):
            return (32 * (c % 4), h * 64 + (c // 4) * 32)
        for h in range(H):
            for c in range(8):
                rb, cb = slot(h, c)
                for c2 in range(2):
                    ci = 2 * h + c2
                    TT(lambda E, rb=rb, cb=cb, ci=ci, c=c, c2=c2: E.matmul(
                        psW1[rb:rb + 32, cb:cb + 32],
                        kbf[:, ci * 256 + c * 32: ci * 256 + (c + 1) * 32],
                        kw[:, ci * 256 + c * 32: ci * 256 + (c + 1) * 32],
                        start=(c2 == 0), stop=(c2 == 1), tile_position=(0, rb)))
                for c2 in range(2):
                    ci = 2 * h + c2
                    TT(lambda E, rb=rb, cb=cb, ci=ci, c=c, c2=c2: E.matmul(
                        psW1[rb:rb + 32, 256 + cb: 256 + cb + 32],
                        kw[:, ci * 256 + c * 32: ci * 256 + (c + 1) * 32],
                        kbf[:, ci * 256 + c * 32: ci * 256 + (c + 1) * 32],
                        start=(c2 == 0), stop=(c2 == 1), tile_position=(0, rb)),
                       inc=(h == H - 1 and c == 7 and c2 == 1))
        tA = ("tensor_t", P.cnt["tensor_t"])
        P.wait("vector", tA)
        V(lambda E: E.tensor_tensor(Lst[:], maskl32r[:], psW1[:, 0:256], op=OP.mult))
        V(lambda E: E.tensor_scalar(Lst[:], Lst[:], -1.0, None, op0=OP.mult))
        V(lambda E: E.tensor_tensor(Ust[:], masku32r[:], psW1[:, 256:512], op=OP.mult))
        V(lambda E: E.tensor_scalar(Ust[:], Ust[:], -1.0, None, op0=OP.mult))
        V(lambda E: E.tensor_tensor(Dst[:], Ust[:], i32r[:], op=OP.add))
        V(lambda E: E.tensor_copy(Up[:], Ust[:]))
        V(lambda E: E.tensor_copy(Lp[:], Lst[:]))
        V(lambda E: E.nop(), inc=True)
        P.wait("tensor", ("vector_t", P.cnt["vector_t"]))

        if p3stop < 6:
            continue
        tiles_by_cc = [[(h, c) for h in range(H) for c in (cc, cc + 4)] for cc in range(4)]
        for lvl in range(4):
            last = (lvl == 3)
            for cc in range(4):
                for (h, c) in tiles_by_cc[cc]:
                    rb, cb = slot(h, c)
                    L_t = Lp[rb:rb + 32, cb:cb + 32]
                    U_t = Up[rb:rb + 32, cb:cb + 32]
                    TT(lambda E, L_t=L_t, U_t=U_t, rb=rb, cb=cb: E.matmul(
                        psW2[rb:rb + 32, cb:cb + 32], U_t, L_t,
                        start=True, stop=True, tile_position=(rb, rb)))
                    TT(lambda E, L_t=L_t, U_t=U_t, rb=rb, cb=cb: E.matmul(
                        psW2[rb:rb + 32, 256 + cb:256 + cb + 32], L_t, U_t,
                        start=True, stop=True, tile_position=(rb, rb)),
                       inc=(cc == 3 and (h, c) == tiles_by_cc[cc][-1]))
            tsq = ("tensor_t", P.cnt["tensor_t"])
            P.wait("vector", tsq)
            V(lambda E: E.tensor_copy(Lp[:], psW2[:, 0:256]))
            V(lambda E: E.tensor_copy(Up[:], psW2[:, 256:512]), inc=True)
            if last:
                V(lambda E: E.memset(psW3[:, 0:1024], 0.0), inc=True)
            P.wait("tensor", ("vector_t", P.cnt["vector_t"]))
            for cc in range(4):
                for (h, c) in tiles_by_cc[cc]:
                    rb, cb = slot(h, c)
                    L_t = Lp[rb:rb + 32, cb:cb + 32]
                    D_t = Dst[rb:rb + 32, cb:cb + 32]
                    i_t = i32[rb:rb + 32, 0:32]
                    if not last:
                        outap = psW3[rb:rb + 32, cb:cb + 32]
                    else:
                        outap = psW3[rb:rb + 32, (h * 8 + c) * 32:(h * 8 + c) * 32 + 32]
                    tp = (rb, rb)
                    TT(lambda E, i_t=i_t, D_t=D_t, outap=outap, tp=tp: E.matmul(
                        outap, i_t, D_t, start=True, stop=False, tile_position=tp))
                    TT(lambda E, L_t=L_t, D_t=D_t, outap=outap, tp=tp: E.matmul(
                        outap, L_t, D_t, start=False, stop=True, tile_position=tp),
                       inc=(cc == 3 and (h, c) == tiles_by_cc[cc][-1]))
            tch = ("tensor_t", P.cnt["tensor_t"])
            P.wait("vector", tch)
            if not last:
                V(lambda E: E.tensor_copy(Dst[:], psW3[:, 0:256]), inc=True)
            else:
                V(lambda E: E.tensor_copy(DfinPad[:], psW3[:, 0:1024]), inc=True)
            P.wait("tensor", ("vector_t", P.cnt["vector_t"]))

        if p3stop < 7:
            continue
        def dfinpad(h, c):
            return DfinPad[:, (h * 8 + c) * 32:(h * 8 + c) * 32 + 32]

        for h in range(H):
            for tj in range(2):
                rps = rot_acquire()
                for cc in range(4):
                    c = tj * 4 + cc
                    TT(lambda E, h=h, c=c, cc=cc, tj=tj, rps=rps: E.matmul(
                        rps[32 * cc:32 * cc + 32, 0:256],
                        dfinpad(h, c),
                        vbT[:, (h * 2 + tj) * 256:(h * 2 + tj) * 256 + 256],
                        start=True, stop=True, tile_position=(0, 32 * cc)), inc=(cc == 3))
                P.wait("vector", ("tensor_t", P.cnt["tensor_t"]))
                V(lambda E, h=h, tj=tj, rps=rps: E.tensor_copy(ut[:, (h * 2 + tj) * 256:(h * 2 + tj) * 256 + 256],
                                                               rps[:, 0:256]), inc=True)
                rot_release(("vector_t", P.cnt["vector_t"]))
            for dkh in range(2):
                rps = rot_acquire()
                for c in range(8):
                    tj = c // 4
                    TT(lambda E, h=h, c=c, tj=tj, dkh=dkh, rps=rps: E.matmul(
                        rps[:, c * 32:(c + 1) * 32],
                        kbT[:, (h * 2 + tj) * 256 + dkh * 128: (h * 2 + tj) * 256 + dkh * 128 + 128],
                        dfinpad(h, c),
                        start=True, stop=True), inc=(c == 7))
                P.wait("vector", ("tensor_t", P.cnt["tensor_t"]))
                V(lambda E, h=h, dkh=dkh, rps=rps: E.tensor_copy(wf[:, (h * 2 + dkh) * 256:(h * 2 + dkh) * 256 + 256],
                                                                 rps[:, 0:256]), inc=True)
                rot_release(("vector_t", P.cnt["vector_t"]))

        if p3stop < 8:
            continue
        for _ in range(3):
            p5a_slot()
        for h in range(H):
            for tj in range(2):
                rps = rot_acquire()
                for dkh in range(2):
                    TT(lambda E, h=h, tj=tj, dkh=dkh, rps=rps: E.matmul(
                        rps[:, 0:256],
                        wf[:, (h * 2 + dkh) * 256 + tj * 128: (h * 2 + dkh) * 256 + tj * 128 + 128],
                        S_t[:, (h * 2 + dkh) * 256:(h * 2 + dkh) * 256 + 256],
                        start=(dkh == 0), stop=(dkh == 1)), inc=(dkh == 1))
                P.wait("vector", ("tensor_t", P.cnt["tensor_t"]))
                V(lambda E, h=h, tj=tj, rps=rps: E.tensor_copy(WSt[:, (h * 2 + tj) * 256:(h * 2 + tj) * 256 + 256],
                                                               rps[:, 0:256]), inc=True)
                rot_release(("vector_t", P.cnt["vector_t"]))
                if own:
                    rps = rot_acquire()
                    for dkh in range(2):
                        TT(lambda E, h=h, tj=tj, dkh=dkh, rps=rps: E.matmul(
                            rps[:, 0:256],
                            qtf[:, (2 * h + dkh) * 256 + tj * 128: (2 * h + dkh) * 256 + tj * 128 + 128],
                            S_t[:, (h * 2 + dkh) * 256:(h * 2 + dkh) * 256 + 256],
                            start=(dkh == 0), stop=(dkh == 1)), inc=(dkh == 1))
                    P.wait("vector", ("tensor_t", P.cnt["tensor_t"]))
                    V(lambda E, h=h, tj=tj, rps=rps: E.tensor_copy(QSt[:, (h * 2 + tj) * 256:(h * 2 + tj) * 256 + 256],
                                                                   rps[:, 0:256]), inc=True)
                    rot_release(("vector_t", P.cnt["vector_t"]))

        if p3stop < 9:
            continue
        V(lambda E: E.tensor_tensor(upb[:], ut[:], WSt[:], op=OP.subtract), inc=True)
        tup0 = ("vector_t", P.cnt["vector_t"])
        P.wait("tensor", tup0)

        if p3stop < 10:
            continue
        for h in range(H):
            for tl in range(2):
                rps = rot_acquire()
                for dkh in range(2):
                    TT(lambda E, h=h, tl=tl, dkh=dkh, rps=rps: E.matmul(
                        rps[:, 0:256],
                        knf[:, (2 * h + dkh) * 256 + tl * 128: (2 * h + dkh) * 256 + tl * 128 + 128],
                        wf[:, (h * 2 + dkh) * 256:(h * 2 + dkh) * 256 + 256],
                        start=(dkh == 0), stop=(dkh == 1)), inc=(dkh == 1))
                P.wait("vector", ("tensor_t", P.cnt["tensor_t"]))
                V(lambda E, tl=tl, rps=rps: E.tensor_copy(PTt[:, tl * 256:(tl + 1) * 256], rps[:, 0:256]), inc=True)
                rot_release(("vector_t", P.cnt["vector_t"]))
            if own:
                for tl in range(2):
                    rps = rot_acquire()
                    for dkh in range(2):
                        TT(lambda E, h=h, tl=tl, dkh=dkh, rps=rps: E.matmul(
                            rps[:, 0:256],
                            knf[:, (2 * h + dkh) * 256 + tl * 128: (2 * h + dkh) * 256 + tl * 128 + 128],
                            qtf[:, (2 * h + dkh) * 256:(2 * h + dkh) * 256 + 256],
                            start=(dkh == 0), stop=(dkh == 1)), inc=(dkh == 1))
                    P.wait("vector", ("tensor_t", P.cnt["tensor_t"]))
                    V(lambda E, h=h, tl=tl, rps=rps: E.tensor_tensor(GTt[:, tl * 256:(tl + 1) * 256],
                                                                     rps[:, 0:256], ghmask[h][tl][:], op=OP.mult), inc=True)
                    rot_release(("vector_t", P.cnt["vector_t"]))
            rot_fence()
            for j in range(1, WIN):
                nk = min(32 * j, 128)
                TT(lambda E, h=h, j=j, nk=nk: E.matmul(
                    psW1[32 * (j % 4):32 * (j % 4) + 32, 0:256],
                    PTt[0:nk, 0 * 256 + j * 32: 0 * 256 + j * 32 + 32],
                    upb[0:nk, (h * 2 + 0) * 256:(h * 2 + 0) * 256 + 256],
                    start=True, stop=(j <= 4), tile_position=(0, 32 * (j % 4))), inc=(j <= 4))
                if j > 4:
                    nk2 = 32 * j - 128
                    TT(lambda E, h=h, j=j, nk2=nk2: E.matmul(
                        psW1[32 * (j % 4):32 * (j % 4) + 32, 0:256],
                        PTt[0:nk2, 1 * 256 + j * 32: 1 * 256 + j * 32 + 32],
                        upb[0:nk2, (h * 2 + 1) * 256:(h * 2 + 1) * 256 + 256],
                        start=False, stop=True, tile_position=(0, 32 * (j % 4))), inc=True)
                P.wait("vector", ("tensor_t", P.cnt["tensor_t"]))
                V(lambda E, h=h, j=j: E.scalar_tensor_tensor(
                    upb[32 * (j % 4):32 * (j % 4) + 32, (h * 2 + j // 4) * 256 + 0:(h * 2 + j // 4) * 256 + 256],
                    psW1[32 * (j % 4):32 * (j % 4) + 32, 0:256],
                    lamcol[32 * (j % 4):32 * (j % 4) + 32, 4 * h:4 * h + 1],
                    upb[32 * (j % 4):32 * (j % 4) + 32, (h * 2 + j // 4) * 256 + 0:(h * 2 + j // 4) * 256 + 256],
                    op0=OP.mult, op1=OP.add), inc=True)
                P.wait("tensor", ("vector_t", P.cnt["vector_t"]))
            if own:
                for tjo in range(2):
                    for tl in range(2):
                        TT(lambda E, h=h, tjo=tjo, tl=tl: E.matmul(
                            psW2[:, 0:256],
                            GTt[:, tl * 256 + tjo * 128: tl * 256 + tjo * 128 + 128],
                            upb[:, (h * 2 + tl) * 256:(h * 2 + tl) * 256 + 256],
                            start=(tl == 0), stop=(tl == 1)), inc=(tl == 1))
                    P.wait("vector", ("tensor_t", P.cnt["tensor_t"]))
                    V(lambda E, h=h, tjo=tjo: E.scalar_tensor_tensor(
                        osb[:], psW2[:, 0:256], 1.0,
                        QSt[:, (h * 2 + tjo) * 256:(h * 2 + tjo) * 256 + 256],
                        op0=OP.mult, op1=OP.add), inc=True)
                    tosb = ("vector_t", P.cnt["vector_t"])
                    P.wait("tensor", tosb)
                    P.wait("gpsimd", tosb)
                    st(delta_s[W0 - 2048 + tjo * 128: W0 - 2048 + (tjo + 1) * 128, h * 256:(h + 1) * 256], osb[:])
                    P.wait("vector", ("gpsimd_d", P.cnt["gpsimd_d"]))
            for dkm in range(2):
                for tj in range(2):
                    TT(lambda E, h=h, dkm=dkm, tj=tj: E.matmul(
                        psW2[:, 256:512],
                        ktT[:, (h * 2 + tj) * 256 + dkm * 128: (h * 2 + tj) * 256 + dkm * 128 + 128],
                        upb[:, (h * 2 + tj) * 256:(h * 2 + tj) * 256 + 256],
                        start=(tj == 0), stop=(tj == 1)), inc=(tj == 1))
                P.wait("vector", ("tensor_t", P.cnt["tensor_t"]))
                V(lambda E, h=h, dkm=dkm: E.scalar_tensor_tensor(
                    S_t[:, (h * 2 + dkm) * 256:(h * 2 + dkm) * 256 + 256],
                    S_t[:, (h * 2 + dkm) * 256:(h * 2 + dkm) * 256 + 256],
                    lamcol[:, 4 * h + 1:4 * h + 2],
                    psW2[:, 256:512], op0=OP.mult, op1=OP.add))
                V(lambda E, h=h, dkm=dkm: E.tensor_scalar(
                    S_t[:, (h * 2 + dkm) * 256:(h * 2 + dkm) * 256 + 256],
                    S_t[:, (h * 2 + dkm) * 256:(h * 2 + dkm) * 256 + 256],
                    lamcol[:, 4 * h + 2:4 * h + 3], None, op0=OP.mult), inc=True)
                P.wait("tensor", ("vector_t", P.cnt["vector_t"]))
        win_free[0] = P.token("vector")
        for _ in range(3):
            p5a_slot()

    while p5c["tr"] < NJOB:
        p5a_slot(drain=True)
    P.wait("scalar", ("scalar_d", P.cnt["scalar_d"]))
    P.barrier()
    es3.close()
    if upto in ("P3", "P3w1"):
        P.finalize(block)
        es.close()
        return nc

    es4 = contextlib.ExitStack()
    sb4 = lambda name, p, f, dt=F32: es4.enter_context(nc.sbuf_tensor(name, [p, f], dt))
    ps4 = lambda name, p, f: es4.enter_context(nc.psum_tensor(name, [p, f], F32))

    w1_t = sb4("w1_t", 128, 9 * 2048)
    wo_t2 = sb4("wo_t2", 128, 8192)
    w2_t = sb4("w2_t", 128, 16 * 16)
    b1c = sb4("b1c_t", 128, 16)
    b2t = sb4("b2t_t", 128, 16)
    normw_t = sb4("normw_tt", 128, 1024)
    rtempt = sb4("rtempt", 128, 16)
    floort = sb4("floort", 128, 16)
    tmp41 = sb4("tmp41", 4, 16)
    vwin = sb4("vwin", 128, 8 * 318)
    shrt = sb4("shrt", 128, 8 * 256)
    lng = sb4("lng", 128, 8 * 256)
    xnw = sb4("xnw", 128, 8 * 256)
    dtm = sb4("dtm", 128, 1024)
    stm4 = sb4("stm4", 128, 1024)
    ltm4 = sb4("ltm4", 128, 1024)
    vtm4 = sb4("vtm4", 128, 1024)
    sq4 = sb4("sq4", 128, 1024)
    statc = sb4("statc", 128, 64)
    stF = sb4("stF", 64, 128)
    hmid2 = [sb4("hmidA", 128, 128), sb4("hmidB", 128, 128)]
    tmpP = sb4("tmpP", 128, 256)
    ptile = sb4("ptile", 128, 16)
    otm = sb4("otm", 128, 1024)
    onF = sb4("onF", 128, 8 * 128)
    xtile = sb4("xtile", 128, 1024)
    y1t = otm
    mcol = sb4("mcol", 128, 16)

    psQ0 = ps4("psQ0", 128, 512)
    psQ1 = ps4("psQ1", 128, 512)

    t4l = []
    for ki in range(8):
        t4l.append(ld(w1_t[:, ki * 2048:(ki + 1) * 2048], IN["w1"][ki * 128:(ki + 1) * 128, :]))
    t4l.append(ld(w1_t[0:64, 8 * 2048:9 * 2048], IN["w1"][1024:1088, :]))
    for ki in range(8):
        t4l.append(ld(wo_t2[:, ki * 1024:(ki + 1) * 1024], IN["wo"][ki * 128:(ki + 1) * 128, :]))
    for ki in range(16):
        t4l.append(ld(w2_t[:, ki * 16:(ki + 1) * 16], IN["w2"][ki * 128:(ki + 1) * 128, :]))
    t4l.append(ld(b1c[:], IN["b1c"]))
    t4l.append(ld(b2t[:], IN["b2t"]))
    t4l.append(ld(normw_t[:], IN["normw_t"]))
    t4l.append(ld(tmp41[0:4, 0:1], IN["ltemp"]))
    t4l.append(ld(misc[0:1, 0:16], IN["floorp"]))
    P.waits("scalar", t4l)
    P.waits("vector", t4l)
    P.waits("tensor", t4l)
    P.ins("scalar", lambda E: E.activation(tmp41[0:4, 1:2], tmp41[0:4, 0:1], AF.Exp), inc=True)
    P.wait("vector", ("scalar_t", P.cnt["scalar_t"]))
    V(lambda E: E.reciprocal(tmp41[0:4, 2:3], tmp41[0:4, 1:2]))
    V(lambda E: E.nop(), inc=True)
    P.wait("tensor", ("vector_t", P.cnt["vector_t"]))
    for h in range(H):
        TT(lambda E, h=h: E.matmul(psQ0[:, h:h + 1], sel4[:, h * 128:(h + 1) * 128], tmp41[0:4, 2:3],
                                   start=True, stop=True), inc=(h == 3))
    P.wait("vector", ("tensor_t", P.cnt["tensor_t"]))
    for h in range(H):
        V(lambda E, h=h: E.tensor_copy(rtempt[:, 4 * h:4 * h + 4], bcast_ap(psQ0[:, h:h + 1], 4)))
    V(lambda E: E.nop(), inc=True)
    P.wait("tensor", ("vector_t", P.cnt["vector_t"]))
    TT(lambda E: E.matmul(psQ0[:, 0:16], ones128[0:1, :], misc[0:1, 0:16], start=True, stop=True), inc=True)
    P.wait("scalar", ("tensor_t", P.cnt["tensor_t"]))
    P.ins("scalar", lambda E: E.activation(floort[:], psQ0[:, 0:16], AF.Sigmoid), inc=True)
    P.wait("vector", ("scalar_t", P.cnt["scalar_t"]))
    V(lambda E: E.tensor_scalar(floort[:], floort[:], 0.075, None, op0=OP.mult))
    tb4 = P.token("vector")
    P.waits("tensor", [tb4])
    P.waits("sync", [tb4])

    for ti in range(8):
        T0 = ti * 256
        tl4 = []
        for ci in range(8):
            lo = T0 - 62
            for (seg0, seg1) in ((lo, T0), (T0, T0 + 256)):
                dst0 = seg0 - lo
                t = seg0
                while t < seg1:
                    if t < 0:
                        src_row = 3 * 8 + ci
                        col = 512 + t
                        n = min(seg1, 0) - t
                    else:
                        src_row = (4 + t // 512) * 8 + ci
                        col = t % 512
                        n = min(seg1 - t, 512 - col)
                    tl4.append(ld(vwin[:, ci * 318 + (t - lo): ci * 318 + (t - lo) + n],
                                  v_s[src_row][:, col:col + n]))
                    t += n
            tl4.append(ld(xnw[:, ci * 256:(ci + 1) * 256],
                          xnT_s[(ti // 2) * 8 + ci][:, (ti % 2) * 256:(ti % 2) * 256 + 256]))
        P.waits("vector", tl4)
        if ti == 0:
            V(lambda E: E.tensor_scalar(
                vwin[:].rearrange("p (c x) -> p c x", x=318)[:, :, 0:62],
                vwin[:].rearrange("p (c x) -> p c x", x=318)[:, :, 0:62],
                flag[:, 0:1], None, op0=OP.mult))
        for ci in range(8):
            base = ci * 318
            V(lambda E, ci=ci, base=base: E.tensor_scalar(
                shrt[:, ci * 256:(ci + 1) * 256], vwin[:, base + 60: base + 316],
                firw_s[:, ci * 3 + 0: ci * 3 + 1], None, op0=OP.mult))
            for j in (1, 2):
                V(lambda E, ci=ci, base=base, j=j: E.scalar_tensor_tensor(
                    shrt[:, ci * 256:(ci + 1) * 256], vwin[:, base + 60 + j: base + 316 + j],
                    firw_s[:, ci * 3 + j: ci * 3 + j + 1], shrt[:, ci * 256:(ci + 1) * 256],
                    op0=OP.mult, op1=OP.add))
            V(lambda E, ci=ci, base=base: E.tensor_scalar(
                lng[:, ci * 256:(ci + 1) * 256], vwin[:, base + 0: base + 256],
                firw_l[:, ci * 63 + 0: ci * 63 + 1], None, op0=OP.mult))
            for j in range(1, 63):
                V(lambda E, ci=ci, base=base, j=j: E.scalar_tensor_tensor(
                    lng[:, ci * 256:(ci + 1) * 256], vwin[:, base + j: base + 256 + j],
                    firw_l[:, ci * 63 + j: ci * 63 + j + 1], lng[:, ci * 256:(ci + 1) * 256],
                    op0=OP.mult, op1=OP.add))
        tfir = P.token("vector")
        P.wait("tensor", tfir)
        P.wait("sync", tfir)

        for s in range(2):
            tdl = ld(dtm[:], delta_s[T0 + s * 128: T0 + (s + 1) * 128, :])
            jobs = []
            for (srct, dstt, srcw, hal) in ((shrt, stm4, 256, 0), (lng, ltm4, 256, 0), (vwin, vtm4, 318, 62)):
                for ci in range(8):
                    jobs.append((srct, dstt, ci * srcw + hal + s * 128, ci))
            for j0 in range(0, len(jobs), 4):
                grp = jobs[j0:j0 + 4]
                for gi, (srct, dstt, off, ci) in enumerate(grp):
                    TT(lambda E, srct=srct, off=off, gi=gi: E.transpose(
                        psQ0[:, gi * 128:(gi + 1) * 128], srct[:, off: off + 128], ident[:]),
                       inc=(gi == len(grp) - 1))
                P.wait("scalar", ("tensor_t", P.cnt["tensor_t"]))
                for gi, (srct, dstt, off, ci) in enumerate(grp):
                    P.ins("scalar", lambda E, dstt=dstt, ci=ci, gi=gi: E.copy(
                        dstt[:, ci * 128:(ci + 1) * 128], psQ0[:, gi * 128:(gi + 1) * 128]),
                          inc=(gi == len(grp) - 1))
                P.wait("tensor", ("scalar_t", P.cnt["scalar_t"]))
            ttr4 = P.token("scalar")
            P.waits("vector", [ttr4, tdl])

            for (tix, tsr) in enumerate((stm4, ltm4, dtm, vtm4)):
                tt = tsr[:]
                tt3 = tt.rearrange("p (h d) -> p h d", d=256)
                V(lambda E: E.drain())
                V(lambda E, tt3=tt3, tix=tix: E.tensor_reduce(statc[:, tix * 16 + 0:tix * 16 + 4], tt3, axis=AXX, op=OP.add))
                V(lambda E, tt=tt: E.tensor_tensor(sq4[:], tt, tt, op=OP.mult))
                V(lambda E: E.drain())
                V(lambda E, tix=tix: E.tensor_reduce(statc[:, tix * 16 + 4:tix * 16 + 8],
                                                     sq4[:].rearrange("p (h d) -> p h d", d=256), axis=AXX, op=OP.add))
                V(lambda E: E.drain())
                V(lambda E, tt3=tt3, tix=tix: E.tensor_reduce(statc[:, tix * 16 + 8:tix * 16 + 12], tt3, axis=AXX, op=OP.max))
            for tix in range(4):
                c0 = tix * 16
                V(lambda E, c0=c0: E.tensor_scalar(statc[:, c0 + 0:c0 + 4], statc[:, c0 + 0:c0 + 4], 1.0 / 256, None, op0=OP.mult))
                V(lambda E, c0=c0: E.tensor_tensor(mcol[:, 0:4], statc[:, c0 + 0:c0 + 4], statc[:, c0 + 0:c0 + 4], op=OP.mult))
                V(lambda E, c0=c0: E.tensor_scalar(statc[:, c0 + 12:c0 + 16], statc[:, c0 + 4:c0 + 8], 1.0 / 256, None, op0=OP.mult))
                V(lambda E, c0=c0: E.tensor_tensor(statc[:, c0 + 12:c0 + 16], statc[:, c0 + 12:c0 + 16], mcol[:, 0:4], op=OP.subtract))
                V(lambda E: E.nop(), inc=True)
                P.wait("scalar", ("vector_t", P.cnt["vector_t"]))
                P.ins("scalar", lambda E, c0=c0: E.activation(statc[:, c0 + 4:c0 + 8], statc[:, c0 + 4:c0 + 8], AF.Sqrt), inc=True)
                P.wait("vector", ("scalar_t", P.cnt["scalar_t"]))
            for tix in range(4):
                c0 = tix * 16
                for h in range(H):
                    V(lambda E, c0=c0, h=h: E.tensor_copy(sq4[:, c0 + 4 * h + 0:c0 + 4 * h + 1], statc[:, c0 + 0 + h:c0 + 1 + h]))
                    V(lambda E, c0=c0, h=h: E.tensor_copy(sq4[:, c0 + 4 * h + 1:c0 + 4 * h + 2], statc[:, c0 + 12 + h:c0 + 13 + h]))
                    V(lambda E, c0=c0, h=h: E.tensor_copy(sq4[:, c0 + 4 * h + 2:c0 + 4 * h + 3], statc[:, c0 + 8 + h:c0 + 9 + h]))
                    V(lambda E, c0=c0, h=h: E.tensor_copy(sq4[:, c0 + 4 * h + 3:c0 + 4 * h + 4], statc[:, c0 + 4 + h:c0 + 5 + h]))
            V(lambda E: E.nop(), inc=True)
            P.wait("tensor", ("vector_t", P.cnt["vector_t"]))
            TT(lambda E: E.transpose(psQ1[0:64, 0:128], sq4[:, 0:64], ident[:]), inc=True)
            P.wait("scalar", ("tensor_t", P.cnt["tensor_t"]))
            P.ins("scalar", lambda E: E.copy(stF[0:64, 0:128], psQ1[0:64, 0:128]), inc=True)
            P.wait("tensor", ("scalar_t", P.cnt["scalar_t"]))

            g_tok = {}
            w1tok = {}
            gbanks = [psQ0, psB0, psB1]
            for mi in range(17):
                if mi < 16:
                    bank = gbanks[mi % 3]
                    P.wait("tensor", g_tok.get(mi - 3))
                    for ki in range(8):
                        TT(lambda E, mi=mi, ki=ki, s=s, bank=bank: E.matmul(
                            bank[:, 0:128], w1_t[:, ki * 2048 + mi * 128: ki * 2048 + (mi + 1) * 128],
                            xnw[:, ki * 256 + s * 128: ki * 256 + s * 128 + 128], start=(ki == 0), stop=False))
                    TT(lambda E, mi=mi, bank=bank: E.matmul(
                        bank[:, 0:128], w1_t[0:64, 8 * 2048 + mi * 128: 8 * 2048 + (mi + 1) * 128],
                        stF[0:64, 0:128], start=False, stop=True), inc=True)
                    w1tok[mi] = ("tensor_t", P.cnt["tensor_t"])
                if mi >= 1:
                    mp = mi - 1
                    P.wait("tensor", g_tok[mp])
                    P.cnt["w2s"] += 1
                    P.ins("tensor", lambda E, mp=mp: E.matmul(
                        psQ1[:, 0:16], hmid2[mp % 2][:],
                        w2_t[:, mp * 16:(mp + 1) * 16], start=(mp == 0), stop=(mp == 15)).then_inc(P.sem["w2s"], 1))
                if mi < 16:
                    P.wait("scalar", w1tok[mi])
                    if mi >= 2:
                        P.wait("scalar", ("w2s", P.cnt["w2s"]))
                    P.ins("scalar", lambda E, mi=mi, bank=gbanks[mi % 3]: E.activation(
                        hmid2[mi % 2][:], bank[:, 0:128], AF.Gelu,
                        bias=b1c[:, mi:mi + 1], scale=1.0), inc=True)
                    g_tok[mi] = ("scalar_t", P.cnt["scalar_t"])
            P.wait("vector", ("w2s", P.cnt["w2s"]))
            pt = ptile[:]
            V(lambda E, pt=pt: E.tensor_tensor(pt, b2t[:], psQ1[:, 0:16], op=OP.add))
            V(lambda E, pt=pt: E.tensor_tensor(pt, pt, rtempt[:], op=OP.mult))
            V(lambda E: E.drain())
            V(lambda E, pt=pt: E.tensor_reduce(mcol[:, 0:4], pt.rearrange("p (h g) -> p h g", g=4), axis=AXX, op=OP.max))
            for h in range(H):
                V(lambda E, pt=pt, h=h: E.tensor_scalar(pt[:, 4 * h:4 * h + 4], pt[:, 4 * h:4 * h + 4],
                                                        mcol[:, h:h + 1], None, op0=OP.subtract))
            V(lambda E: E.nop(), inc=True)
            P.wait("scalar", ("vector_t", P.cnt["vector_t"]))
            P.ins("scalar", lambda E, pt=pt: E.activation(pt, pt, AF.Exp), inc=True)
            P.wait("vector", ("scalar_t", P.cnt["scalar_t"]))
            V(lambda E: E.drain())
            V(lambda E, pt=pt: E.tensor_reduce(mcol[:, 0:4], pt.rearrange("p (h g) -> p h g", g=4), axis=AXX, op=OP.add))
            V(lambda E: E.reciprocal(mcol[:, 4:8], mcol[:, 0:4]))
            V(lambda E: E.drain())
            for h in range(H):
                V(lambda E, pt=pt, h=h: E.tensor_scalar(pt[:, 4 * h:4 * h + 4], pt[:, 4 * h:4 * h + 4],
                                                        mcol[:, 4 + h:5 + h], None, op0=OP.mult))
            V(lambda E, pt=pt: E.tensor_tensor(pt, pt, floort[:], op=OP.max))
            V(lambda E: E.drain())
            V(lambda E, pt=pt: E.tensor_reduce(mcol[:, 0:4], pt.rearrange("p (h g) -> p h g", g=4), axis=AXX, op=OP.add))
            V(lambda E: E.reciprocal(mcol[:, 4:8], mcol[:, 0:4]))
            V(lambda E: E.drain())
            for h in range(H):
                V(lambda E, pt=pt, h=h: E.tensor_scalar(pt[:, 4 * h:4 * h + 4], pt[:, 4 * h:4 * h + 4],
                                                        mcol[:, 4 + h:5 + h], None, op0=OP.mult))

            for h in range(H):
                reg = h * 256
                V(lambda E, pt=pt, reg=reg, h=h: E.tensor_scalar(
                    otm[:, reg:reg + 256], stm4[:, reg:reg + 256], pt[:, 4 * h + 0:4 * h + 1], None, op0=OP.mult))
                for (g, tsr) in ((1, ltm4), (2, dtm), (3, vtm4)):
                    V(lambda E, pt=pt, reg=reg, h=h, g=g, tsr=tsr: E.scalar_tensor_tensor(
                        otm[:, reg:reg + 256], tsr[:, reg:reg + 256], pt[:, 4 * h + g:4 * h + g + 1],
                        otm[:, reg:reg + 256], op0=OP.mult, op1=OP.add))
            V(lambda E: E.tensor_tensor(sq4[:], otm[:], otm[:], op=OP.mult))
            V(lambda E: E.drain())
            V(lambda E: E.tensor_reduce(mcol[:, 0:4], sq4[:].rearrange("p (h d) -> p h d", d=256), axis=AXX, op=OP.add))
            V(lambda E: E.tensor_scalar(mcol[:, 0:4], mcol[:, 0:4], 1.0 / 256, 1e-5, op0=OP.mult, op1=OP.add))
            V(lambda E: E.nop(), inc=True)
            P.wait("scalar", ("vector_t", P.cnt["vector_t"]))
            P.ins("scalar", lambda E: E.activation(mcol[:, 4:8], mcol[:, 0:4], AF.Sqrt), inc=True)
            P.wait("vector", ("scalar_t", P.cnt["scalar_t"]))
            V(lambda E: E.reciprocal(mcol[:, 8:12], mcol[:, 4:8]))
            V(lambda E: E.drain())
            for h in range(H):
                reg = h * 256
                V(lambda E, reg=reg, h=h: E.tensor_scalar(
                    otm[:, reg:reg + 256], otm[:, reg:reg + 256], mcol[:, 8 + h:9 + h], None, op0=OP.mult))
            V(lambda E: E.tensor_tensor(otm[:], otm[:], normw_t[:], op=OP.mult))
            tcomb = P.token("vector")
            P.wait("tensor", tcomb)

            for cg in range(2):
                for cj in range(4):
                    ci = cg * 4 + cj
                    TT(lambda E, ci=ci, cj=cj: E.transpose(
                        psQ0[:, cj * 128:(cj + 1) * 128], otm[:, ci * 128:(ci + 1) * 128], ident[:]), inc=(cj == 3))
                P.wait("scalar", ("tensor_t", P.cnt["tensor_t"]))
                for cj in range(4):
                    ci = cg * 4 + cj
                    P.ins("scalar", lambda E, ci=ci, cj=cj: E.copy(
                        onF[:, ci * 128:(ci + 1) * 128], psQ0[:, cj * 128:(cj + 1) * 128]), inc=(cj == 3))
                P.wait("tensor", ("scalar_t", P.cnt["scalar_t"]))

            tx4 = ld(xtile[:], IN["x12"][2048 + T0 + s * 128: 2048 + T0 + (s + 1) * 128, :])
            for nn in range(2):
                for ki in range(8):
                    TT(lambda E, nn=nn, ki=ki: E.matmul(
                        psQ1[:, 0:512], onF[:, ki * 128:(ki + 1) * 128],
                        wo_t2[:, ki * 1024 + nn * 512: ki * 1024 + nn * 512 + 512],
                        start=(ki == 0), stop=(ki == 7)), inc=(ki == 7))
                P.waits("vector", [("tensor_t", P.cnt["tensor_t"]), tx4])
                V(lambda E, nn=nn: E.tensor_tensor(y1t[:, nn * 512:(nn + 1) * 512],
                                                   xtile[:, nn * 512:(nn + 1) * 512], psQ1[:, 0:512], op=OP.add), inc=True)
                P.wait("tensor", ("vector_t", P.cnt["vector_t"]))
            ty1 = P.token("vector")
            P.wait("scalar", ty1)
            tspill = st(y1_s[T0 + s * 128: T0 + (s + 1) * 128, :], y1t[:], eng="scalar")
            P.wait("vector", tspill)
            P.wait("sync", tspill)
        tend = P.token("vector")
        P.wait("sync", tend)

    P.barrier()
    es4.close()
    if upto == "P4":
        P.finalize(block)
        es.close()
        return nc

    es5 = contextlib.ExitStack()
    sb5 = lambda name, p, f, dt=F32: es5.enter_context(nc.sbuf_tensor(name, [p, f], dt))
    ps5 = lambda name, p, f: es5.enter_context(nc.psum_tensor(name, [p, f], F32))

    wbuf = sb5("wbuf", 128, 2816)
    wbuf2 = sb5("wbuf2", 128, 2816)
    uq5 = sb5("uq5", 128, 2816)
    gq5 = sb5("gq5", 128, 2816)
    sg5 = sb5("sg5", 128, 2816)
    qr5 = sb5("qr5", 128, 176)
    rc5 = sb5("rc5", 128, 176)
    psF = [psB0, psB1] + [ps5("psF%d" % i, 128, 512) for i in range(6)]

    def emit_quant(buf, K):
        nb = K // 16
        bv = buf.rearrange("p (nb b) -> p nb b", b=16)
        V(lambda E: E.drain())
        V(lambda E, bv=bv, nb=nb: E.tensor_reduce(qr5[:, 0:nb], bv, axis=AXX, op=OP.max, apply_absolute_value=True))
        V(lambda E, nb=nb: E.tensor_scalar(qr5[:, 0:nb], qr5[:, 0:nb], 1.0 / 6, 1e-12, op0=OP.mult, op1=OP.max))
        V(lambda E, nb=nb: E.reciprocal(rc5[:, 0:nb], qr5[:, 0:nb]))
        V(lambda E: E.drain())
        rb = bcast_ap(rc5[:, 0:nb], 16)
        V(lambda E, bv=bv, rb=rb, K=K: E.tensor_tensor(uq5[:, 0:K].rearrange("p (nb b) -> p nb b", b=16), bv, rb, op=OP.mult))
        V(lambda E, K=K: E.tensor_scalar(sg5[:, 0:K], uq5[:, 0:K], 0.0, None, op0=OP.is_gt))
        V(lambda E, K=K: E.tensor_scalar(gq5[:, 0:K], uq5[:, 0:K], 0.0, None, op0=OP.is_lt))
        V(lambda E, K=K: E.tensor_tensor(sg5[:, 0:K], sg5[:, 0:K], gq5[:, 0:K], op=OP.subtract))
        V(lambda E, K=K: E.tensor_tensor(uq5[:, 0:K], uq5[:, 0:K], sg5[:, 0:K], op=OP.mult))
        V(lambda E, K=K: E.tensor_scalar(gq5[:, 0:K], uq5[:, 0:K], 0.25, None, op0=OP.is_gt))
        for thr in (0.75, 1.25, 1.75):
            V(lambda E, K=K, thr=thr: E.scalar_tensor_tensor(gq5[:, 0:K], uq5[:, 0:K], thr, gq5[:, 0:K], op0=OP.is_gt, op1=OP.add))
        V(lambda E, K=K: E.tensor_scalar(gq5[:, 0:K], gq5[:, 0:K], 0.5, None, op0=OP.mult))
        for thr in (2.5, 3.5, 5.0, 5.0):
            V(lambda E, K=K, thr=thr: E.scalar_tensor_tensor(gq5[:, 0:K], uq5[:, 0:K], thr, gq5[:, 0:K], op0=OP.is_gt, op1=OP.add))
        V(lambda E, K=K: E.tensor_tensor(gq5[:, 0:K], gq5[:, 0:K], sg5[:, 0:K], op=OP.mult))
        qb = bcast_ap(qr5[:, 0:nb], 16)
        V(lambda E, buf=buf, K=K, qb=qb, nb=nb: E.tensor_tensor(
            buf.rearrange("p (nb b) -> p nb b", b=16),
            gq5[:, 0:K].rearrange("p (nb b) -> p nb b", b=16), qb, op=OP.mult))

    xq5 = sb5("xq5", 128, 8 * 2048)
    wgk = sb5("wgk", 128, 8 * 512)
    wuk = sb5("wuk", 128, 8 * 512)
    gsil = sb5("gsil", 128, 512)
    hqbuf = sb5("hqbuf", 128, 512)
    t5x = []
    for ci in range(8):
        for tt in range(4):
            t5x.append(ld(xq5[:, ci * 2048 + tt * 512: ci * 2048 + (tt + 1) * 512], xqT_s[tt * 8 + ci]))
    P.waits("tensor", t5x)
    ncols = [(0, 512), (512, 512), (1024, 512), (1536, 512), (2048, 512), (2560, 256)]
    for (n0, nsz) in ncols:
        tlw = []
        for kc in range(8):
            tlw.append(ld(wgk[:, kc * 512: kc * 512 + nsz], wgT_s[kc][:, n0:n0 + nsz]))
            tlw.append(ld(wuk[:, kc * 512: kc * 512 + nsz], wuT_s[kc][:, n0:n0 + nsz]))
        P.waits("tensor", tlw)
        ps_cons = [None, None]
        mmtok = [None] * 17
        for m in range(17):
            if m < 16:
                pr = m % 2
                P.wait("tensor", ps_cons[pr])
                for kc in range(8):
                    TT(lambda E, m=m, kc=kc, nsz=nsz, pr=pr: E.matmul(
                        psF[pr * 2][:, 0:nsz], xq5[:, kc * 2048 + m * 128: kc * 2048 + (m + 1) * 128],
                        wgk[:, kc * 512: kc * 512 + nsz], start=(kc == 0), stop=(kc == 7)))
                for kc in range(8):
                    TT(lambda E, m=m, kc=kc, nsz=nsz, pr=pr: E.matmul(
                        psF[pr * 2 + 1][:, 0:nsz], xq5[:, kc * 2048 + m * 128: kc * 2048 + (m + 1) * 128],
                        wuk[:, kc * 512: kc * 512 + nsz], start=(kc == 0), stop=(kc == 7)), inc=(kc == 7))
                mmtok[m] = ("tensor_t", P.cnt["tensor_t"])
            if m >= 1:
                mp = m - 1
                pr = mp % 2
                wb = wbuf if pr == 0 else wbuf2
                P.wait("scalar", mmtok[mp])
                P.wait("scalar", ps_cons[1 - pr])
                P.ins("scalar", lambda E, nsz=nsz, pr=pr: E.activation(gsil[:, 0:nsz], psF[pr * 2][:, 0:nsz], AF.Silu), inc=True)
                P.wait("vector", ("scalar_t", P.cnt["scalar_t"]))
                V(lambda E, nsz=nsz, pr=pr, wb=wb: E.tensor_tensor(wb[:, 0:nsz], gsil[:, 0:nsz], psF[pr * 2 + 1][:, 0:nsz], op=OP.mult), inc=True)
                ps_cons[pr] = ("vector_t", P.cnt["vector_t"])
                emit_quant(wb[:, 0:nsz], nsz)
                tq = P.token("vector")
                P.wait("tensor", tq)
                for kc in range(nsz // 128):
                    TT(lambda E, kc=kc, wb=wb: E.transpose(psF[4][:, kc * 128:(kc + 1) * 128],
                                                           wb[:, kc * 128:(kc + 1) * 128], ident[:]), inc=(kc == nsz // 128 - 1))
                P.wait("scalar", ("tensor_t", P.cnt["tensor_t"]))
                for kc in range(nsz // 128):
                    P.ins("scalar", lambda E, kc=kc: E.copy(hqbuf[:, kc * 128:(kc + 1) * 128],
                                                            psF[4][:, kc * 128:(kc + 1) * 128]), inc=(kc == nsz // 128 - 1))
                tsp = ("scalar_t", P.cnt["scalar_t"])
                P.wait("tensor", tsp)
                P.wait("gpsimd", tsp)
                for kc in range(nsz // 128):
                    st(hqT_s[n0 // 128 + kc][:, mp * 128:(mp + 1) * 128], hqbuf[:, kc * 128:(kc + 1) * 128])
                P.wait("scalar", ("gpsimd_d", P.cnt["gpsimd_d"]))
                P.wait("vector", ("gpsimd_d", P.cnt["gpsimd_d"]))
        P.wait("sync", ("tensor_t", P.cnt["tensor_t"]))
    P.barrier()

    wdk = [sb5("wdk0", 128, 1024), sb5("wdk1", 128, 1024), sb5("wdk2", 128, 1024)]
    hqk = [sb5("hqk0", 128, 4 * 128), sb5("hqk1", 128, 4 * 128), sb5("hqk2", 128, 4 * 128)]
    y5 = sb5("y5", 128, 1024)
    mmt = {}
    ldt = {}

    def p5c_loads(li):
        mg, kc = li // 22, li % 22
        pb = li % 3
        P.wait("sync", mmt.get(li - 3))
        P.wait("gpsimd", mmt.get(li - 3))
        t = [ld(wdk[pb][:], wdT_s[kc])]
        for mm_ in range(4):
            m = mg * 4 + mm_
            t.append(ld(hqk[pb][:, mm_ * 128:(mm_ + 1) * 128], hqT_s[kc][:, m * 128:(m + 1) * 128], eng="gpsimd"))
        ldt[li] = t

    p5c_loads(0)
    p5c_loads(1)
    p5c_loads(2)
    for mg in range(4):
        for kc in range(22):
            li = mg * 22 + kc
            pb = li % 3
            P.waits("tensor", ldt[li])
            for mm_ in range(4):
                for nn in range(2):
                    TT(lambda E, mm_=mm_, nn=nn, kc=kc, pb=pb: E.matmul(
                        psF[mm_ * 2 + nn][:, 0:512],
                        hqk[pb][:, mm_ * 128:(mm_ + 1) * 128],
                        wdk[pb][:, nn * 512:(nn + 1) * 512],
                        start=(kc == 0), stop=(kc == 21)), inc=(mm_ == 3 and nn == 1))
            mmt[li] = ("tensor_t", P.cnt["tensor_t"])
            if li + 3 < 88:
                p5c_loads(li + 3)
        tdone = ("tensor_t", P.cnt["tensor_t"])
        for mm_ in range(4):
            m = mg * 4 + mm_
            ty = ld(y5[:], y1_s[m * 128:(m + 1) * 128, :], eng="gpsimd")
            P.waits("vector", [tdone, ty])
            for nn in range(2):
                V(lambda E, mm_=mm_, nn=nn: E.tensor_tensor(
                    y5[:, nn * 512:(nn + 1) * 512], y5[:, nn * 512:(nn + 1) * 512],
                    psF[mm_ * 2 + nn][:, 0:512], op=OP.add))
            V(lambda E: E.nop(), inc=True)
            tout = ("vector_t", P.cnt["vector_t"])
            P.wait("gpsimd", tout)
            st(out_h[m * 128:(m + 1) * 128, :], y5[:])
            P.wait("vector", ("gpsimd_d", P.cnt["gpsimd_d"]))
            P.wait("tensor", ("gpsimd_d", P.cnt["gpsimd_d"]))

    P.wait("sync", ("gpsimd_d", P.cnt["gpsimd_d"]))
    P.barrier()
    es5.close()

    P.finalize(block)
    es.close()
    return nc



def kernel(**inputs):
    maps = shard_inputs(inputs)
    nc = build_nc(upto="ALL")
    res = run_bass_kernel_spmd(nc, maps, core_ids=list(range(NCHIP)))
    out = np.zeros((4, 4096, 1024), np.float32)
    for c in range(NCHIP):
        b, half = c // 2, c % 2
        out[b, 2048 * half: 2048 * half + 2048] = res.results[c]["out"]
    return out


# version 09
# speedup vs baseline: 1.2752x; 1.0148x over previous
import numpy as np
import contextlib
import concourse.bass as bass
import concourse.mybir as mybir
from concourse.alu_op_type import AluOpType as OP
from concourse.bass_utils import run_bass_kernel_spmd

F32 = mybir.dt.float32
BF16 = mybir.dt.bfloat16
AF = mybir.ActivationFunctionType
AXX = mybir.AxisListType.X

H = 4
NCHIP = 8
T2 = 2048
TW = 256
WIN = 8
NWIN = 16
PP = 512
NPP = 8
D = 1024
FFI = 2816

ENGS = ("sync", "vector", "scalar", "tensor", "gpsimd")


class Prog:
    def __init__(self, nc, es):
        self.nc = nc
        self.q = {e: [] for e in ENGS}
        self.cnt = {}
        self.sem = {}
        for e in ENGS:
            for sfx in ("_t", "_d"):
                name = e + sfx
                self.sem[name] = es.enter_context(nc.semaphore(name))
                self.cnt[name] = 0
        self.seq = 0
        self.auto_drain = True
        self.fence_pending = {}
        self.fenced = {}

    def ins(self, eng, fn, inc=False, dma=False):
        tok = None
        if dma:
            name = eng + "_d"
            pend = self.fence_pending.get(name, 0)
            if pend > self.fenced.get(name, 0):
                self.q[eng].append(lambda E, name=name, v=pend: E.wait_ge(self.sem[name], v))
                self.fenced[name] = pend
            self.cnt[name] += 16
            tok = (name, self.cnt[name])
            def wrapped(E, fn=fn, name=name):
                fn(E).then_inc(self.sem[name], 16)
        elif inc:
            name = eng + "_t"
            self.cnt[name] += 1
            tok = (name, self.cnt[name])
            if eng in ("vector", "scalar", "gpsimd"):
                def wrapped(E, fn=fn, name=name):
                    fn(E)
                    E.drain().then_inc(self.sem[name], 1)
            else:
                def wrapped(E, fn=fn, name=name):
                    fn(E).then_inc(self.sem[name], 1)
        elif eng == "vector" and self.auto_drain:
            def wrapped(E, fn=fn):
                fn(E)
                E.drain()
        else:
            wrapped = fn
        self.q[eng].append(wrapped)
        self.seq += 1
        return tok

    def wait(self, eng, tok):
        if tok is None:
            return
        name, val = tok
        if name.endswith("_d"):
            val = self.cnt[name]
            self.fence_pending[name] = max(self.fence_pending.get(name, 0), val)
        self.q[eng].append(lambda E, name=name, val=val: E.wait_ge(self.sem[name], val))

    def waits(self, eng, toks):
        best = {}
        for t in toks:
            if t is None:
                continue
            n, v = t
            if n.endswith("_d"):
                v = self.cnt[n]
                self.fence_pending[n] = max(self.fence_pending.get(n, 0), v)
            best[n] = max(best.get(n, 0), v)
        for n, v in best.items():
            self.q[eng].append(lambda E, n=n, v=v: E.wait_ge(self.sem[n], v))

    def token(self, eng):
        if eng in ("vector", "scalar", "gpsimd"):
            name = eng + "_t"
            self.cnt[name] += 1
            tok = (name, self.cnt[name])
            self.q[eng].append(lambda E, name=name: E.drain().then_inc(self.sem[name], 1))
            self.seq += 1
            return tok
        return self.ins(eng, lambda E: E.nop(), inc=True)

    def barrier(self, engs=ENGS):
        toks = [self.token(e) for e in engs]
        for e in engs:
            self.waits(e, [t for t in toks])

    def finalize(self, block):
        for e in ENGS:
            fns = self.q[e]
            def runner(E, fns=fns):
                for f in fns:
                    f(E)
            getattr(block, e)(runner)


def bcast_ap(ap, inner):
    return bass.AP(tensor=ap.tensor, offset=ap.offset,
                   ap=[list(q) for q in ap.ap] + [[0, inner]])


def rep_ap(ap, n, step=0):
    a = [list(q) for q in ap.ap]
    return bass.AP(tensor=ap.tensor, offset=ap.offset, ap=[a[0]] + [[step, n]] + a[1:])



def make_consts():
    ident = np.eye(128, dtype=np.float32)
    ones128 = np.ones((128, 128), np.float32)
    sel4 = np.zeros((4, 512), np.float32)
    for h in range(4):
        sel4[h, h * 128:(h + 1) * 128] = 1.0
    r = np.arange(32)
    maskL32 = np.tile((r[:, None] > r[None, :]).astype(np.float32), (4, 1))
    maskU32 = np.tile((r[:, None] < r[None, :]).astype(np.float32), (4, 1))
    i32 = np.tile(np.eye(32, dtype=np.float32), (4, 1))
    a = np.arange(256)
    ca, cb = a[:, None] // 32, a[None, :] // 32
    strict = (ca < cb).astype(np.float32)
    diag = ((ca == cb) & (a[:, None] % 32 <= a[None, :] % 32)).astype(np.float32)
    return dict(ident=ident, ones128=ones128, sel4=sel4, maskl32=maskL32, masku32=maskU32,
                i32=i32, maskl32r=np.tile(maskL32, (1, 8)), masku32r=np.tile(maskU32, (1, 8)),
                i32r=np.tile(i32, (1, 8)),
                gstrict0=strict[:128], gstrict1=strict[128:],
                gdiag0=diag[:128], gdiag1=diag[128:])


def shard_inputs(I):
    x = np.asarray(I["L0_attn_hidden_states"], np.float32)
    consts = make_consts()
    maps = []
    for c in range(NCHIP):
        b, half = c // 2, c % 2
        x12 = np.concatenate([x[b, 0:2048], x[b, 2048 * half: 2048 * half + 2048]], 0)
        m = {
            "x12": x12,
            "wq": I["L0_attn_Wq"], "wk": I["L0_attn_Wk"], "wv": I["L0_attn_Wv"],
            "wb": I["L0_attn_Wb"],
            "convq": I["L0_attn_conv_q"], "convk": I["L0_attn_conv_k"],
            "convv": I["L0_attn_conv_v"],
            "fir_s": np.asarray(I["L0_attn_fir_short_w"], np.float32).reshape(1024, 3),
            "fir_l": np.asarray(I["L0_attn_fir_long_w"], np.float32).reshape(1024, 63),
            "fparam": np.asarray(I["L0_attn_forget_param"], np.float32).reshape(4, 1),
            "ltemp": np.asarray(I["L0_attn_log_temp"], np.float32).reshape(4, 1),
            "floorp": np.asarray(I["L0_attn_floor_param"], np.float32).reshape(1, 16),
            "w1": I["L0_attn_W1"],
            "b1": np.asarray(I["L0_attn_b1"], np.float32).reshape(1, 2048),
            "b1c": np.asarray(I["L0_attn_b1"], np.float32).reshape(16, 128).T,
            "w2": I["L0_attn_W2"],
            "b2": np.asarray(I["L0_attn_b2"], np.float32).reshape(1, 16),
            "normw_t": np.tile(np.asarray(I["L0_attn_norm_w"], np.float32).reshape(1, 256), (128, 4)),
            "b2t": np.tile(np.asarray(I["L0_attn_b2"], np.float32).reshape(1, 16), (128, 1)),
            "wo": I["L0_attn_Wo"],
            "wg": I["L0_ffn_w_gate"], "wu": I["L0_ffn_w_up"], "wd": I["L0_ffn_w_down"],
            "flagv": np.full((128, 1), float(half), np.float32),
        }
        m.update(consts)
        maps.append({k: np.ascontiguousarray(np.asarray(v, np.float32)) for k, v in m.items()})
    return maps



def build_nc(upto="ALL", dbg=(), skip=(), p3stop=99):
    nc = bass.Bass("TRN2", target_bir_lowering=False, debug=False)
    es = contextlib.ExitStack()
    P = Prog(nc, es)

    def dram(name, shape, kind="Internal"):
        k = "ExternalOutput" if (name in dbg and kind == "Internal") else kind
        return nc.dram_tensor(name, shape, F32, kind=k).ap()

    IN = {}
    for name, shape in [
        ("x12", [4096, 1024]), ("wq", [1024, 1024]), ("wk", [1024, 1024]),
        ("wv", [1024, 1024]), ("wb", [1024, 4]),
        ("convq", [1024, 4]), ("convk", [1024, 4]), ("convv", [1024, 4]),
        ("fir_s", [1024, 3]), ("fir_l", [1024, 63]),
        ("fparam", [4, 1]), ("ltemp", [4, 1]), ("floorp", [1, 16]),
        ("w1", [1088, 2048]), ("b1", [1, 2048]), ("b1c", [128, 16]), ("w2", [2048, 16]), ("b2", [1, 16]),
        ("normw_t", [128, 1024]), ("b2t", [128, 16]), ("wo", [1024, 1024]),
        ("wg", [2816, 1024]), ("wu", [2816, 1024]), ("wd", [1024, 2816]),
        ("flagv", [128, 1]), ("ident", [128, 128]), ("ones128", [128, 128]), ("sel4", [4, 512]),
        ("maskl32", [128, 32]), ("masku32", [128, 32]), ("i32", [128, 32]),
        ("maskl32r", [128, 256]), ("masku32r", [128, 256]), ("i32r", [128, 256]),
        ("gstrict0", [128, 256]), ("gstrict1", [128, 256]),
        ("gdiag0", [128, 256]), ("gdiag1", [128, 256]),
    ]:
        IN[name] = dram(name, shape, "ExternalInput")

    out_h = dram("out", [2048, 1024], "ExternalOutput")

    khat_s = dram("khat_s", [64, 128, 512])
    v_s = dram("v_s", [64, 128, 512])
    qhat_s = dram("qhat_s", [32, 128, 512])
    xnT_s = dram("xnT_s", [32, 128, 512])
    xqT_s = dram("xqT_s", [32, 128, 512])
    beta_s = dram("beta_s", [8, 4, 512])
    xn_dbg = dram("xn_dbg", [128, 4096])
    ln_dbg = dram("ln_dbg", [128, 16])
    xtm_dbg = dram("xtm_dbg", [128, 1024])
    xnT_dbg = dram("xnT_dbg", [128, 4096])
    delta_s = dram("delta_s", [2048, 1024])
    y1_s = dram("y1_s", [2048, 1024])
    wgT_s = dram("wgT_s", [8, 128, 2816])
    wuT_s = dram("wuT_s", [8, 128, 2816])
    wdT_s = dram("wdT_s", [22, 128, 1024])
    hqT_s = dram("hqT_s", [22, 128, 2048])

    sb = lambda name, p, f, dt=F32: es.enter_context(nc.sbuf_tensor(name, [p, f], dt))

    ident = sb("identt", 128, 128)
    ones128 = sb("ones128t", 128, 128)
    sel4 = sb("sel4t", 4, 512)
    flag = sb("flagt", 128, 1)
    maskl32 = sb("maskl32t", 128, 32)
    masku32 = sb("masku32t", 128, 32)
    i32 = sb("i32t", 128, 32)
    maskl32r = sb("maskl32rt", 128, 256)
    masku32r = sb("masku32rt", 128, 256)
    i32r = sb("i32rt", 128, 256)
    convw = {t: sb("convw" + t, 128, 8 * 4) for t in "qkv"}
    firw_s = sb("firw_s", 128, 8 * 3)
    firw_l = sb("firw_l", 128, 8 * 63)
    lam_t = sb("lam", 4, 64)
    lamR = sb("lamR", 4, 512)
    rampPT = sb("rampPT", 128, 8)
    rampNT = sb("rampNT", 128, 8)
    lamcol = sb("lamcol", 128, 16)
    beta_f = sb("beta_f", 4, 512)
    wbt = sb("wbt", 128, 8 * 4)
    misc = sb("misct", 128, 512)
    row1 = sb("row1", 1, 2048)

    psum = lambda name, p, f: es.enter_context(nc.psum_tensor(name, [p, f], F32))

    block = es.enter_context(nc.Block())
    for _ps in ("p5at0", "p5at1", "w2s"):
        P.sem[_ps] = es.enter_context(nc.semaphore(_ps))
        P.cnt[_ps] = 0

    def ld(dst, src, eng="sync"):
        return P.ins(eng, lambda E, d=dst, s=src: E.dma_start(out=d, in_=s), dma=True)

    def st(dst, src, eng="gpsimd"):
        return P.ins(eng, lambda E, d=dst, s=src: E.dma_start(out=d, in_=s), dma=True)

    tl = []
    for n, t in [("ident", ident), ("ones128", ones128), ("sel4", sel4), ("flagv", flag),
                 ("maskl32", maskl32), ("masku32", masku32), ("i32", i32),
                 ("maskl32r", maskl32r), ("masku32r", masku32r), ("i32r", i32r)]:
        tl.append(ld(t[:], IN[n]))
    for t in "qkv":
        for ci in range(8):
            tl.append(ld(convw[t][:, ci * 4:(ci + 1) * 4], IN["conv" + t][ci * 128:(ci + 1) * 128, :]))
    for ci in range(8):
        tl.append(ld(wbt[:, ci * 4:(ci + 1) * 4], IN["wb"][ci * 128:(ci + 1) * 128, :]))
        tl.append(ld(firw_s[:, ci * 3:(ci + 1) * 3], IN["fir_s"][ci * 128:(ci + 1) * 128, :]))
        tl.append(ld(firw_l[:, ci * 63:(ci + 1) * 63], IN["fir_l"][ci * 128:(ci + 1) * 128, :]))
    tl.append(ld(lam_t[0:4, 0:1], IN["fparam"]))
    gd_t = [sb(f"gdt{i}", 128, 256) for i in range(2)]
    for i in range(2):
        tl.append(ld(gd_t[i][:], IN["gdiag%d" % i]))

    P.waits("scalar", tl)
    P.ins("scalar", lambda E: E.activation(lam_t[0:4, 1:2], lam_t[0:4, 0:1], AF.Sigmoid))
    ts_ = P.token("scalar")
    P.waits("vector", tl + [ts_])
    V = lambda fn, inc=False: P.ins("vector", fn, inc=inc)
    V(lambda E: E.tensor_scalar(lam_t[0:4, 2:3], lam_t[0:4, 1:2], 0.45, 0.55, op0=OP.mult, op1=OP.add))
    V(lambda E: E.memset(lam_t[0:4, 8:9], 1.0))
    V(lambda E: E.drain())
    for j in range(1, 9):
        V(lambda E, j=j: E.tensor_tensor(lam_t[0:4, 8 + j:9 + j], lam_t[0:4, 7 + j:8 + j], lam_t[0:4, 2:3], op=OP.mult))
    V(lambda E: E.reciprocal(lam_t[0:4, 3:4], lam_t[0:4, 2:3]))
    V(lambda E: E.drain())
    V(lambda E: E.memset(lam_t[0:4, 24:25], 1.0))
    V(lambda E: E.drain())
    for l in range(1, 8):
        V(lambda E, l=l: E.tensor_tensor(lam_t[0:4, 24 + l:25 + l], lam_t[0:4, 23 + l:24 + l], lam_t[0:4, 3:4], op=OP.mult))
    V(lambda E: E.tensor_scalar(lam_t[0:4, 5:6], lam_t[0:4, 3:4], -1.0, None, op0=OP.mult))
    for j in range(8):
        V(lambda E, j=j: E.tensor_copy(lamR[0:4, 32 * j:32 * j + 32], bcast_ap(lam_t[0:4, 8 + j:9 + j], 32)))
        V(lambda E, j=j: E.tensor_copy(lamR[0:4, 256 + 32 * j:256 + 32 * j + 32], bcast_ap(lam_t[0:4, 24 + j:25 + j], 32)))
    tlam = P.token("vector")

    psB0 = psum("psB0", 128, 512)
    psB1 = psum("psB1", 128, 512)
    P.wait("tensor", tlam)
    TT = lambda fn, inc=False: P.ins("tensor", fn, inc=inc)
    for h in range(H):
        sl = sel4[:, h * 128:(h + 1) * 128]
        TT(lambda E, sl=sl: E.matmul(psB1[:, 0:1], sl, lam_t[0:4, 5:6], start=True, stop=True))
        TT(lambda E, sl=sl: E.matmul(psB1[:, 1:2], sl, lam_t[0:4, 2:3], start=True, stop=True))
        TT(lambda E, sl=sl: E.matmul(psB1[:, 2:3], sl, lam_t[0:4, 15:16], start=True, stop=True))
        TT(lambda E, sl=sl: E.matmul(psB1[:, 3:4], sl, lam_t[0:4, 3:4], start=True, stop=True), inc=True)
        tmm = ("tensor_t", P.cnt["tensor_t"])
        P.wait("vector", tmm)
        V(lambda E, h=h: E.tensor_copy(lamcol[:, 4 * h:4 * h + 4], psB1[:, 0:4]), inc=True)
        tev = ("vector_t", P.cnt["vector_t"])
        P.wait("tensor", tev)
    tlam2 = P.token("vector")
    P.wait("tensor", tlam2)
    for tj in range(2):
        TT(lambda E, tj=tj: E.transpose(psB0[:, 0:4], lamR[0:4, tj * 128:tj * 128 + 128], ident[0:4, 0:4]))
        TT(lambda E, tj=tj: E.transpose(psB0[:, 4:8], lamR[0:4, 256 + tj * 128:256 + tj * 128 + 128], ident[0:4, 0:4]), inc=True)
        tmm = ("tensor_t", P.cnt["tensor_t"])
        P.wait("vector", tmm)
        V(lambda E, tj=tj: E.tensor_copy(rampPT[:, tj * 4:tj * 4 + 4], psB0[:, 0:4]))
        V(lambda E, tj=tj: E.tensor_copy(rampNT[:, tj * 4:tj * 4 + 4], psB0[:, 4:8]), inc=True)
        P.wait("tensor", ("vector_t", P.cnt["vector_t"]))

    P.barrier()
    if upto == "P0":
        P.finalize(block)
        es.close()
        return nc

    es2 = contextlib.ExitStack()
    sb2 = lambda name, p, f, dt=F32: es2.enter_context(nc.sbuf_tensor(name, [p, f], dt))

    wk_t = sb2("wk_t", 128, 8192)
    wv_t = sb2("wv_t", 128, 8192)
    wq_t = sb2("wq_t", 128, 8192)
    twl = []
    for name, t in [("wk", wk_t), ("wv", wv_t), ("wq", wq_t)]:
        for ki in range(8):
            twl.append(ld(t[:, ki * 1024:(ki + 1) * 1024], IN[name][ki * 128:(ki + 1) * 128, :]))

    x_tm = sb2("x_tm", 128, 1024)
    xn_tm = sb2("xn_tm", 128, 4 * 1024)
    xq_tm = sb2("xq_tm", 128, 1024)
    xnT = sb2("xnT", 128, 8 * 512)
    xqT = sb2("xqT", 128, 8 * 512)
    pslice = sb2("pslice", 128, 515)
    tails = {t: sb2("tails" + t, 128, 24) for t in "qkv"}
    kbuf = sb2("kbuf", 128, 1024)
    qbuf = sb2("qbuf", 128, 1024)
    vbuf = sb2("vbuf", 128, 512)
    sqr = sb2("sqr", 128, 1024)
    lncol = sb2("lncol", 128, 16)
    qrow = sb2("qrow", 128, 64)
    uq = sb2("uq", 128, 1024)
    gq = sb2("gq", 128, 1024)
    sgq = sb2("sgq", 128, 1024)

    psP = [psB0, psB1]
    psT = es2.enter_context(nc.psum_tensor("psT", [128, 512], F32))
    psT2 = es2.enter_context(nc.psum_tensor("psT2", [128, 512], F32))
    psT3 = es2.enter_context(nc.psum_tensor("psT3", [128, 512], F32))
    p2rot = {"g": 0, "ev": {}}
    psR = es2.enter_context(nc.psum_tensor("psR", [4, 512], F32))
    psL = es2.enter_context(nc.psum_tensor("psL", [1, 512], F32))

    V(lambda E: E.memset(tails["q"][:], 0.0))

    V(lambda E: E.drain())
    V(lambda E: E.memset(tails["k"][:], 0.0))
    V(lambda E: E.drain())
    V(lambda E: E.memset(tails["v"][:], 0.0))
    V(lambda E: E.drain())
    tz = P.token("vector")
    P.waits("tensor", twl + [tz])
    P.waits("scalar", twl + [tz])

    xtm_free = [None]
    evtok = [None, None]
    for pp in range(NPP):
        own = pp >= 4
        for s in range(4):
            P.wait("sync", xtm_free[0])
            tx = ld(x_tm[:], IN["x12"][pp * 512 + s * 128: pp * 512 + (s + 1) * 128, :])
            P.wait("vector", tx)
            V(lambda E: E.tensor_reduce(lncol[:, 0:1], x_tm[:], axis=AXX, op=OP.add))
            V(lambda E: E.tensor_tensor(sqr[:], x_tm[:], x_tm[:], op=OP.mult))
            V(lambda E: E.drain())
            V(lambda E: E.tensor_reduce(lncol[:, 1:2], sqr[:], axis=AXX, op=OP.add))
            V(lambda E: E.tensor_scalar(lncol[:, 2:3], lncol[:, 0:1], 1.0 / 1024, None, op0=OP.mult))
            V(lambda E: E.tensor_scalar(lncol[:, 3:4], lncol[:, 1:2], 1.0 / 1024, None, op0=OP.mult))
            V(lambda E: E.tensor_tensor(lncol[:, 4:5], lncol[:, 2:3], lncol[:, 2:3], op=OP.mult))
            V(lambda E: E.tensor_tensor(lncol[:, 5:6], lncol[:, 3:4], lncol[:, 4:5], op=OP.subtract))
            V(lambda E: E.tensor_scalar(lncol[:, 5:6], lncol[:, 5:6], 1e-5, None, op0=OP.add), inc=True)
            P.wait("scalar", ("vector_t", P.cnt["vector_t"]))
            P.ins("scalar", lambda E: E.activation(lncol[:, 6:7], lncol[:, 5:6], AF.Sqrt), inc=True)
            P.wait("vector", ("scalar_t", P.cnt["scalar_t"]))
            V(lambda E: E.reciprocal(lncol[:, 7:8], lncol[:, 6:7]))
            V(lambda E: E.drain())
            V(lambda E: E.tensor_tensor(lncol[:, 8:9], lncol[:, 2:3], lncol[:, 7:8], op=OP.mult))
            V(lambda E: E.tensor_scalar(lncol[:, 9:10], lncol[:, 8:9], -1.0, None, op0=OP.mult))
            V(lambda E, s=s: E.tensor_scalar(
                xn_tm[:, s * 1024:(s + 1) * 1024], x_tm[:], lncol[:, 7:8], None, op0=OP.mult))
            V(lambda E, s=s: E.tensor_scalar(
                xn_tm[:, s * 1024:(s + 1) * 1024], xn_tm[:, s * 1024:(s + 1) * 1024],
                lncol[:, 9:10], None, op0=OP.add), inc=True)
            tln = ("vector_t", P.cnt["vector_t"])
            xtm_free[0] = tln
            if own and "quant" not in skip:
                xnt = xn_tm[:, s * 1024:(s + 1) * 1024]
                xv = xnt.rearrange("p (nb b) -> p nb b", b=16)
                V(lambda E: E.drain())
                V(lambda E, xv=xv: E.tensor_reduce(qrow[:], xv, axis=AXX, op=OP.max, apply_absolute_value=True))
                V(lambda E: E.tensor_scalar(qrow[:], qrow[:], 1.0 / 6, 1e-12, op0=OP.mult, op1=OP.max))
                V(lambda E: E.reciprocal(misc[:, 0:64], qrow[:]))
                V(lambda E: E.drain())
                rb = bcast_ap(misc[:, 0:64], 16)
                V(lambda E, xv=xv, rb=rb: E.tensor_tensor(uq[:].rearrange("p (nb b) -> p nb b", b=16), xv, rb, op=OP.mult))
                V(lambda E: E.tensor_scalar(sgq[:], uq[:], 0.0, None, op0=OP.is_gt))
                V(lambda E: E.tensor_scalar(gq[:], uq[:], 0.0, None, op0=OP.is_lt))
                V(lambda E: E.tensor_tensor(sgq[:], sgq[:], gq[:], op=OP.subtract))
                V(lambda E: E.tensor_tensor(uq[:], uq[:], sgq[:], op=OP.mult))
                V(lambda E: E.tensor_scalar(gq[:], uq[:], 0.25, None, op0=OP.is_gt))
                for thr in (0.75, 1.25, 1.75):
                    V(lambda E, thr=thr: E.scalar_tensor_tensor(gq[:], uq[:], thr, gq[:], op0=OP.is_gt, op1=OP.add))
                V(lambda E: E.tensor_scalar(gq[:], gq[:], 0.5, None, op0=OP.mult))
                for thr in (2.5, 3.5, 5.0, 5.0):
                    V(lambda E, thr=thr: E.scalar_tensor_tensor(gq[:], uq[:], thr, gq[:], op0=OP.is_gt, op1=OP.add))
                V(lambda E: E.tensor_tensor(gq[:], gq[:], sgq[:], op=OP.mult))
                qb = bcast_ap(qrow[:], 16)
                V(lambda E, qb=qb: E.tensor_tensor(
                    xq_tm[:].rearrange("p (nb b) -> p nb b", b=16),
                    gq[:].rearrange("p (nb b) -> p nb b", b=16), qb, op=OP.mult), inc=True)
                tln = ("vector_t", P.cnt["vector_t"])
            P.wait("tensor", tln)
            if "trans" in skip:
                continue
            srcs = [(xn_tm[:, s * 1024:(s + 1) * 1024], xnT)] + ([(xq_tm[:], xqT)] if own else [])
            for srct, dstt in srcs:
                for cg in range(2):
                    g = p2rot["g"]
                    p2rot["g"] += 1
                    bank = [psT2, psT3][g % 2]
                    P.wait("tensor", p2rot["ev"].get(g - 2))
                    sname = "p5at%d" % (g % 2)
                    for cj in range(4):
                        ci = cg * 4 + cj
                        if cj == 3:
                            P.cnt[sname] += 1
                            P.ins("tensor", lambda E, ci=ci, cj=cj, srct=srct, bank=bank, sname=sname: E.transpose(
                                bank[:, cj * 128:(cj + 1) * 128],
                                srct[:, ci * 128:(ci + 1) * 128], ident[:]).then_inc(P.sem[sname], 1))
                        else:
                            TT(lambda E, ci=ci, cj=cj, srct=srct, bank=bank: E.transpose(
                                bank[:, cj * 128:(cj + 1) * 128],
                                srct[:, ci * 128:(ci + 1) * 128], ident[:]))
                    P.wait("scalar", (sname, P.cnt[sname]))
                    for cj in range(4):
                        ci = cg * 4 + cj
                        P.ins("scalar", lambda E, s=s, ci=ci, cj=cj, dstt=dstt, bank=bank: E.copy(
                            dstt[:, ci * 512 + s * 128: ci * 512 + (s + 1) * 128],
                            bank[:, cj * 128:(cj + 1) * 128]), inc=(cj == 3))
                    p2rot["ev"][g] = ("scalar_t", P.cnt["scalar_t"])
        ttr = P.token("scalar")
        if own:
            P.wait("gpsimd", ttr)
            for ci in range(8):
                st(xnT_s[(pp - 4) * 8 + ci], xnT[:, ci * 512:(ci + 1) * 512])
                st(xqT_s[(pp - 4) * 8 + ci], xqT[:, ci * 512:(ci + 1) * 512])
        if pp == 4 and "xn_dbg" in dbg:
            P.wait("gpsimd", ttr)
            P.wait("gpsimd", P.token("vector"))
            st(xn_dbg, xn_tm[:])
            st(xnT_dbg, xnT[:])
            st(ln_dbg, lncol[:])
            st(xtm_dbg, x_tm[:])
            P.wait("vector", ("gpsimd_d", P.cnt["gpsimd_d"]))
            P.wait("scalar", ("gpsimd_d", P.cnt["gpsimd_d"]))
            P.wait("sync", ("gpsimd_d", P.cnt["gpsimd_d"]))

        if "rest" in skip:
            continue
        P.wait("tensor", ttr)
        for ki in range(8):
            TT(lambda E, ki=ki: E.matmul(psR[:], wbt[:, ki * 4:(ki + 1) * 4],
                                         xnT[:, ki * 512:(ki + 1) * 512],
                                         start=(ki == 0), stop=(ki == 7)), inc=(ki == 7))
        P.wait("scalar", ("tensor_t", P.cnt["tensor_t"]))
        P.ins("scalar", lambda E: E.activation(beta_f[0:4, :], psR[:], AF.Sigmoid), inc=True)
        tbeta = ("scalar_t", P.cnt["scalar_t"])
        P.wait("gpsimd", tbeta)
        st(beta_s[pp], beta_f[0:4, :])
        P.wait("tensor", tbeta)
        P.wait("scalar", ("gpsimd_d", P.cnt["gpsimd_d"]))

        if pp == 3:
            P.wait("tensor", evtok[0])
            for mi in range(8):
                for ki in range(8):
                    TT(lambda E, ki=ki, mi=mi: E.matmul(
                        psP[0][:, 0:3], wq_t[:, ki * 1024 + mi * 128: ki * 1024 + (mi + 1) * 128],
                        xnT[:, ki * 512 + 509: ki * 512 + 512],
                        start=(ki == 0), stop=(ki == 7)), inc=(ki == 7))
                P.wait("vector", ("tensor_t", P.cnt["tensor_t"]))
                V(lambda E, mi=mi: E.tensor_copy(tails["q"][:, mi * 3:(mi + 1) * 3], psP[0][:, 0:3]), inc=True)
                P.wait("tensor", ("vector_t", P.cnt["vector_t"]))

        if pp == 4:
            for tt in "qkv":
                V(lambda E, tt=tt: E.tensor_scalar(tails[tt][:], tails[tt][:], flag[:, 0:1], None, op0=OP.mult))
            tfl = P.token("vector")
            P.wait("tensor", tfl)

        tensors = ["k", "v"] + (["q"] if own else [])
        for t in tensors:
            wt = {"q": wq_t, "k": wk_t, "v": wv_t}[t]
            buf = {"q": qbuf, "k": kbuf, "v": vbuf}[t]
            for mi in range(8):
                pb = mi % 2
                P.wait("tensor", evtok[pb])
                for ki in range(8):
                    TT(lambda E, wt=wt, ki=ki, mi=mi, pb=pb: E.matmul(
                        psP[pb][:], wt[:, ki * 1024 + mi * 128: ki * 1024 + (mi + 1) * 128],
                        xnT[:, ki * 512:(ki + 1) * 512],
                        start=(ki == 0), stop=(ki == 7)), inc=(ki == 7))
                tmm = ("tensor_t", P.cnt["tensor_t"])
                P.wait("vector", tmm)
                V(lambda E, t=t, mi=mi: E.tensor_copy(pslice[:, 0:3], tails[t][:, mi * 3:(mi + 1) * 3]))
                V(lambda E, pb=pb: E.tensor_copy(pslice[:, 3:515], psP[pb][:]), inc=True)
                evtok[pb] = ("vector_t", P.cnt["vector_t"])
                V(lambda E, t=t, mi=mi: E.tensor_copy(tails[t][:, mi * 3:(mi + 1) * 3], pslice[:, 512:515]))
                cw = convw[t]
                bo = (mi % 2) * 512 if t in "kq" else 0
                V(lambda E, buf=buf, bo=bo, cw=cw, mi=mi: E.tensor_scalar(
                    buf[:, bo:bo + 512], pslice[:, 0:512],
                    cw[:, mi * 4 + 0: mi * 4 + 1], None, op0=OP.mult))
                for j in (1, 2, 3):
                    V(lambda E, buf=buf, bo=bo, cw=cw, mi=mi, j=j: E.scalar_tensor_tensor(
                        buf[:, bo:bo + 512], pslice[:, j: 512 + j],
                        cw[:, mi * 4 + j: mi * 4 + j + 1], buf[:, bo:bo + 512],
                        op0=OP.mult, op1=OP.add), inc=(j == 3))
                P.wait("scalar", ("vector_t", P.cnt["vector_t"]))
                P.ins("scalar", lambda E, buf=buf, bo=bo: E.activation(
                    buf[:, bo:bo + 512], buf[:, bo:bo + 512], AF.Silu), inc=True)
                tsil = ("scalar_t", P.cnt["scalar_t"])
                if t == "v":
                    P.wait("gpsimd", tsil)
                    st(v_s[pp * 8 + mi], vbuf[:, 0:512])
                    P.wait("vector", ("gpsimd_d", P.cnt["gpsimd_d"]))
                    P.wait("scalar", ("gpsimd_d", P.cnt["gpsimd_d"]))
                elif mi % 2 == 1:
                    h = mi // 2
                    P.wait("vector", tsil)
                    for c2 in range(2):
                        V(lambda E, buf=buf, c2=c2: E.tensor_tensor(
                            sqr[:, c2 * 512:(c2 + 1) * 512], buf[:, c2 * 512:(c2 + 1) * 512],
                            buf[:, c2 * 512:(c2 + 1) * 512], op=OP.mult), inc=(c2 == 1))
                    P.wait("tensor", ("vector_t", P.cnt["vector_t"]))
                    for c2 in range(2):
                        TT(lambda E, c2=c2: E.matmul(psL[0:1, :], ones128[:, 0:1],
                                                     sqr[:, c2 * 512:(c2 + 1) * 512],
                                                     start=(c2 == 0), stop=(c2 == 1)), inc=(c2 == 1))
                    P.wait("vector", ("tensor_t", P.cnt["tensor_t"]))
                    V(lambda E: E.tensor_scalar(row1[0:1, 0:512], psL[0:1, :], 1e-6, None, op0=OP.add), inc=True)
                    P.wait("scalar", ("vector_t", P.cnt["vector_t"]))
                    P.ins("scalar", lambda E: E.activation(row1[0:1, 0:512], row1[0:1, 0:512], AF.Sqrt), inc=True)
                    P.wait("vector", ("scalar_t", P.cnt["scalar_t"]))
                    V(lambda E: E.reciprocal(row1[0:1, 0:512], row1[0:1, 0:512]))
                    V(lambda E: E.drain(), inc=True)
                    P.wait("tensor", ("vector_t", P.cnt["vector_t"]))
                    TT(lambda E: E.matmul(psT[:], ones128[0:1, :], row1[0:1, 0:512],
                                          start=True, stop=True), inc=True)
                    P.wait("vector", ("tensor_t", P.cnt["tensor_t"]))
                    for c2 in range(2):
                        V(lambda E, buf=buf, c2=c2: E.tensor_tensor(
                            buf[:, c2 * 512:(c2 + 1) * 512], buf[:, c2 * 512:(c2 + 1) * 512],
                            psT[:], op=OP.mult), inc=(c2 == 1))
                    tl2 = ("vector_t", P.cnt["vector_t"])
                    P.wait("tensor", tl2)
                    P.wait("gpsimd", tl2)
                    dst = khat_s if t == "k" else qhat_s
                    base = pp * 8 if t == "k" else (pp - 4) * 8
                    st(dst[base + mi - 1], buf[:, 0:512])
                    st(dst[base + mi], buf[:, 512:1024])
                    P.wait("vector", ("gpsimd_d", P.cnt["gpsimd_d"]))
                    P.wait("scalar", ("gpsimd_d", P.cnt["gpsimd_d"]))

    P.barrier()
    es2.close()
    if upto == "P2":
        P.finalize(block)
        es.close()
        return nc

    es3 = contextlib.ExitStack()
    sb3 = lambda name, p, f, dt=F32: es3.enter_context(nc.sbuf_tensor(name, [p, f], dt))
    ps3 = lambda name, p, f: es3.enter_context(nc.psum_tensor(name, [p, f], F32))

    ghmask = [[sb3(f"ghm{h}_{i}", 128, 256) for i in range(2)] for h in range(H)]
    lamPb = [sb3(f"lamPb{h}", 128, 256) for h in range(H)]
    lamNb = [sb3(f"lamNb{h}", 128, 256) for h in range(H)]
    kw = sb3("kw", 128, 8 * 256)
    vw = sb3("vw", 128, 8 * 256)
    qw = sb3("qw", 128, 8 * 256)
    kbf = sb3("kbf", 128, 8 * 256)
    knf = sb3("knf", 128, 8 * 256)
    qtf = sb3("qtf", 128, 8 * 256)
    kT = sb3("kT", 128, 2048)
    vT = sb3("vT", 128, 2048)
    kbT = sb3("kbT", 128, 2048)
    ktT = sb3("ktT", 128, 2048)
    vbT = sb3("vbT", 128, 2048)
    bw = sb3("bw", 4, 256)
    bT = sb3("bT", 128, 8)
    colsc = sb3("colsc", 128, 8)
    Lst = sb3("Lst", 128, 256)
    Ust = sb3("Ust", 128, 256)
    Dst = sb3("Dst", 128, 256)
    Lp = sb3("Lp", 128, 256)
    Up = sb3("Up", 128, 256)
    DfinPad = sb3("DfinPad", 128, 1024)
    ut = sb3("ut", 128, 2048)
    wf = sb3("wf", 128, 2048)
    WSt = sb3("WSt", 128, 2048)
    QSt = sb3("QSt", 128, 2048)
    PTt = sb3("PTt", 128, 512)
    GTt = sb3("GTt", 128, 512)
    upb = sb3("upb", 128, 2048)
    S_t = sb3("S_t", 128, 2048)
    osb = sb3("osb", 128, 256)
    psW0 = ps3("psW0", 128, 512)
    psW1 = ps3("psW1", 128, 512)
    psW2 = ps3("psW2", 128, 512)
    psW3 = ps3("psW3", 128, 1024)

    V(lambda E: E.memset(S_t[:], 0.0))
    V(lambda E: E.drain())
    for h in range(H):
        sl = sel4[:, h * 128:(h + 1) * 128]
        TT(lambda E, sl=sl: E.matmul(psW0[:, 0:256], sl, lamR[0:4, 0:256], start=True, stop=True))
        TT(lambda E, sl=sl: E.matmul(psW0[:, 256:512], sl, lamR[0:4, 256:512], start=True, stop=True), inc=True)
        P.wait("vector", ("tensor_t", P.cnt["tensor_t"]))
        V(lambda E, h=h: E.tensor_copy(lamPb[h][:], psW0[:, 0:256]))
        V(lambda E, h=h: E.tensor_copy(lamNb[h][:], psW0[:, 256:512]), inc=True)
        P.wait("tensor", ("vector_t", P.cnt["vector_t"]))
    tgl = []
    for h in range(H):
        for i in range(2):
            tgl.append(ld(ghmask[h][i][:], IN["gstrict%d" % i]))
    P.waits("vector", tgl)
    for h in range(H):
        for i in range(2):
            V(lambda E, h=h, i=i: E.scalar_tensor_tensor(
                ghmask[h][i][:], ghmask[h][i][:], lamcol[:, 4 * h + 3:4 * h + 4], gd_t[i][:],
                op0=OP.mult, op1=OP.add))
    tpre = P.token("vector")
    P.waits("tensor", [tpre])
    P.waits("sync", [tpre])

    nwin_run = 1 if upto == "P3w1" else NWIN
    wqbd = [sb3("wqbA", 128, 512), sb3("wqbB", 128, 512), sb3("wqbC", 128, 512)]
    uqad = [sb3("uqaA", 128, 512), sb3("uqaB", 128, 512)]
    gqa = sb3("gqa", 128, 512)
    sga = sb3("sga", 128, 512)
    qra = sb3("qra", 128, 32)
    rca = sb3("rca", 128, 32)
    psA5 = ps3("psA5", 128, 512)

    def emit_quant_a(buf, K):
        nb = K // 16
        bv = buf.rearrange("p (nb b) -> p nb b", b=16)
        V(lambda E: E.drain())
        V(lambda E, bv=bv, nb=nb: E.tensor_reduce(qra[:, 0:nb], bv, axis=AXX, op=OP.max, apply_absolute_value=True))
        V(lambda E, nb=nb: E.tensor_scalar(qra[:, 0:nb], qra[:, 0:nb], 1.0 / 6, 1e-12, op0=OP.mult, op1=OP.max))
        V(lambda E, nb=nb: E.reciprocal(rca[:, 0:nb], qra[:, 0:nb]))
        V(lambda E: E.drain())
        rb = bcast_ap(rca[:, 0:nb], 16)
        V(lambda E, buf=buf, bv=bv, rb=rb, K=K: E.tensor_tensor(buf.rearrange("p (nb b) -> p nb b", b=16), bv, rb, op=OP.mult))
        V(lambda E, buf=buf, K=K: E.tensor_scalar(sga[:, 0:K], buf[:, 0:K], 0.0, None, op0=OP.is_gt))
        V(lambda E, buf=buf, K=K: E.tensor_scalar(gqa[:, 0:K], buf[:, 0:K], 0.0, None, op0=OP.is_lt))
        V(lambda E, K=K: E.tensor_tensor(sga[:, 0:K], sga[:, 0:K], gqa[:, 0:K], op=OP.subtract))
        V(lambda E, buf=buf, K=K: E.tensor_tensor(buf[:, 0:K], buf[:, 0:K], sga[:, 0:K], op=OP.mult))
        V(lambda E, buf=buf, K=K: E.tensor_scalar(gqa[:, 0:K], buf[:, 0:K], 0.25, None, op0=OP.is_gt))
        for thr in (0.75, 1.25, 1.75):
            V(lambda E, buf=buf, K=K, thr=thr: E.scalar_tensor_tensor(gqa[:, 0:K], buf[:, 0:K], thr, gqa[:, 0:K], op0=OP.is_gt, op1=OP.add))
        V(lambda E, K=K: E.tensor_scalar(gqa[:, 0:K], gqa[:, 0:K], 0.5, None, op0=OP.mult))
        for thr in (2.5, 3.5, 5.0, 5.0):
            V(lambda E, buf=buf, K=K, thr=thr: E.scalar_tensor_tensor(gqa[:, 0:K], buf[:, 0:K], thr, gqa[:, 0:K], op0=OP.is_gt, op1=OP.add))
        V(lambda E, K=K: E.tensor_tensor(gqa[:, 0:K], gqa[:, 0:K], sga[:, 0:K], op=OP.mult))
        qb = bcast_ap(qra[:, 0:nb], 16)
        V(lambda E, buf=buf, K=K, qb=qb: E.tensor_tensor(
            buf.rearrange("p (nb b) -> p nb b", b=16),
            gqa[:, 0:K].rearrange("p (nb b) -> p nb b", b=16), qb, op=OP.mult), inc=True)
        return ("vector_t", P.cnt["vector_t"])

    p5a_jobs = []
    for (wname, wtgt, nrc, KT) in (("wg", wgT_s, 22, 1024), ("wu", wuT_s, 22, 1024), ("wd", wdT_s, 8, 2816)):
        for rc in range(nrc):
            c0 = 0
            while c0 < KT:
                K = min(512, KT - c0)
                p5a_jobs.append((wname, wtgt, rc, c0, K))
                c0 += K
    NJOB = len(p5a_jobs)
    p5c = {"ld": 0, "q": 0, "tr": 0}
    t_ld = {}
    t_q = {}
    t_tr = {}
    t_ev = {}
    t_sp = {}

    def p5a_slot(drain=False):
        if p5c["tr"] < p5c["q"] - (0 if drain else 1):
            n = p5c["tr"]
            (wname, wtgt, rc, c0, K) = p5a_jobs[n]
            P.wait("tensor", t_q[n])
            P.wait("tensor", t_ev.get(n - 1))
            sname = "p5at%d" % (n % 2)
            for kc in range(K // 128):
                last = kc == K // 128 - 1
                if last:
                    P.cnt[sname] += 1
                    P.ins("tensor", lambda E, kc=kc, n=n, sname=sname: E.transpose(
                        psA5[:, kc * 128:(kc + 1) * 128],
                        wqbd[n % 3][:, kc * 128:(kc + 1) * 128], ident[:]).then_inc(P.sem[sname], 1))
                else:
                    TT(lambda E, kc=kc, n=n: E.transpose(psA5[:, kc * 128:(kc + 1) * 128],
                                                         wqbd[n % 3][:, kc * 128:(kc + 1) * 128], ident[:]))
            t_tr[n] = (sname, P.cnt[sname])
            P.wait("scalar", t_tr[n])
            P.wait("scalar", t_sp.get(n - 2))
            for kc in range(K // 128):
                P.ins("scalar", lambda E, kc=kc, n=n: E.copy(
                    uqad[n % 2][:, kc * 128:(kc + 1) * 128], psA5[:, kc * 128:(kc + 1) * 128]),
                    inc=(kc == K // 128 - 1))
            t_ev[n] = ("scalar_t", P.cnt["scalar_t"])
            for kc in range(K // 128):
                t_sp[n] = st(wtgt[c0 // 128 + kc][:, rc * 128:(rc + 1) * 128],
                             uqad[n % 2][:, kc * 128:(kc + 1) * 128], eng="scalar")
            p5c["tr"] = n + 1
        if p5c["ld"] < NJOB and p5c["ld"] - p5c["tr"] < 3:
            m = p5c["ld"]
            (wname, wtgt, rc, c0, K) = p5a_jobs[m]
            t_ld[m] = ld(wqbd[m % 3][:, 0:K], IN[wname][rc * 128:(rc + 1) * 128, c0:c0 + K], eng="scalar")
            p5c["ld"] = m + 1
        if p5c["q"] < p5c["ld"]:
            p = p5c["q"]
            (wname, wtgt, rc, c0, K) = p5a_jobs[p]
            P.wait("vector", t_ld[p])
            t_q[p] = emit_quant_a(wqbd[p % 3][:, 0:K], K)
            p5c["q"] = p + 1

    rot_banks = [psB0, psB1, psW0]
    rot = {"i": 0, "ev": [None, None, None]}

    def rot_acquire():
        i = rot["i"]
        rot["i"] += 1
        P.wait("tensor", rot["ev"][i % 3])
        return rot_banks[i % 3]

    def rot_release(tok):
        rot["ev"][(rot["i"] - 1) % 3] = tok

    def rot_fence():
        P.waits("tensor", [t for t in rot["ev"] if t is not None])

    rot["ev"] = [tpre, tpre, tpre]
    win_free = [None]
    for wi in range(nwin_run):
        own = wi >= NWIN // 2
        p4 = wi // 2
        off = (wi % 2) * 256
        W0 = wi * 256

        if wi == NWIN // 2:
            V(lambda E: E.tensor_scalar(S_t[:], S_t[:], flag[:, 0:1], None, op0=OP.mult), inc=True)
            P.wait("tensor", ("vector_t", P.cnt["vector_t"]))

        P.wait("sync", win_free[0])
        tload = []
        for ci in range(8):
            tload.append(ld(kw[:, ci * 256:(ci + 1) * 256], khat_s[p4 * 8 + ci][:, off:off + 256]))
            tload.append(ld(vw[:, ci * 256:(ci + 1) * 256], v_s[p4 * 8 + ci][:, off:off + 256]))
            if own:
                tload.append(ld(qw[:, ci * 256:(ci + 1) * 256], qhat_s[(p4 - 4) * 8 + ci][:, off:off + 256]))
        tload.append(ld(bw[:], beta_s[p4][:, off:off + 256]))
        if p3stop < 1:
            continue
        P.waits("tensor", tload)
        P.waits("vector", tload)

        for tj in range(2):
            rps = rot_acquire()
            TT(lambda E, tj=tj, rps=rps: E.transpose(rps[:, 0:4], bw[0:4, tj * 128:(tj + 1) * 128], ident[0:4, 0:4]), inc=True)
            P.wait("vector", ("tensor_t", P.cnt["tensor_t"]))
            V(lambda E, tj=tj, rps=rps: E.tensor_copy(bT[:, tj * 4:tj * 4 + 4], rps[:, 0:4]), inc=True)
            rot_release(("vector_t", P.cnt["vector_t"]))

        if p3stop < 2:
            continue
        for h in range(H):
            sl = sel4[:, h * 128:(h + 1) * 128]
            rps = rot_acquire()
            TT(lambda E, sl=sl, rps=rps: E.matmul(rps[:, 0:256], sl, bw[0:4, :], start=True, stop=True), inc=True)
            P.wait("vector", ("tensor_t", P.cnt["tensor_t"]))
            for ci in (2 * h, 2 * h + 1):
                V(lambda E, ci=ci, rps=rps: E.tensor_tensor(kbf[:, ci * 256:(ci + 1) * 256],
                                                            kw[:, ci * 256:(ci + 1) * 256], rps[:, 0:256], op=OP.mult))
                V(lambda E, ci=ci, h=h: E.tensor_tensor(knf[:, ci * 256:(ci + 1) * 256],
                                                        kw[:, ci * 256:(ci + 1) * 256], lamNb[h][:], op=OP.mult))
                if own:
                    V(lambda E, ci=ci, h=h: E.tensor_tensor(qtf[:, ci * 256:(ci + 1) * 256],
                                                            qw[:, ci * 256:(ci + 1) * 256], lamPb[h][:], op=OP.mult))
            V(lambda E: E.nop(), inc=True)
            rot_release(("vector_t", P.cnt["vector_t"]))

        if p3stop < 3:
            continue
        for (srct, dstt) in ((kw, kT), (vw, vT)):
            for h in range(H):
                for tj in range(2):
                    rps = rot_acquire()
                    for c2 in range(2):
                        ci = 2 * h + c2
                        TT(lambda E, srct=srct, ci=ci, tj=tj, c2=c2, rps=rps: E.transpose(
                            rps[:, c2 * 128:c2 * 128 + 128],
                            srct[:, ci * 256 + tj * 128: ci * 256 + tj * 128 + 128], ident[:]), inc=(c2 == 1))
                    P.wait("scalar", ("tensor_t", P.cnt["tensor_t"]))
                    for c2 in range(2):
                        P.ins("scalar", lambda E, dstt=dstt, h=h, tj=tj, c2=c2, rps=rps: E.copy(
                            dstt[:, (h * 2 + tj) * 256 + c2 * 128: (h * 2 + tj) * 256 + c2 * 128 + 128],
                            rps[:, c2 * 128:c2 * 128 + 128]), inc=(c2 == 1))
                    rot_release(("scalar_t", P.cnt["scalar_t"]))
        ttr3 = P.token("scalar")
        P.wait("vector", ttr3)

        if p3stop < 4:
            continue
        for h in range(H):
            for tj in range(2):
                r = (h * 2 + tj) * 256
                V(lambda E, h=h, tj=tj: E.tensor_tensor(colsc[:, 0:1], bT[:, tj * 4 + h:tj * 4 + h + 1],
                                                        rampPT[:, tj * 4 + h:tj * 4 + h + 1], op=OP.mult))
                V(lambda E, r=r: E.tensor_scalar(kbT[:, r:r + 256], kT[:, r:r + 256], colsc[:, 0:1], None, op0=OP.mult))
                V(lambda E, r=r, h=h, tj=tj: E.tensor_scalar(ktT[:, r:r + 256], kT[:, r:r + 256],
                                                             rampNT[:, tj * 4 + h:tj * 4 + h + 1], None, op0=OP.mult))
                V(lambda E, r=r, h=h, tj=tj: E.tensor_scalar(vbT[:, r:r + 256], vT[:, r:r + 256],
                                                             bT[:, tj * 4 + h:tj * 4 + h + 1], None, op0=OP.mult))
        tsc = P.token("vector")
        P.wait("tensor", tsc)
        for _ in range(3):
            p5a_slot()

        if p3stop < 5:
            continue
        rot_fence()
        def slot(h, c):
            return (32 * (c % 4), h * 64 + (c // 4) * 32)
        for h in range(H):
            for c in range(8):
                rb, cb = slot(h, c)
                for c2 in range(2):
                    ci = 2 * h + c2
                    TT(lambda E, rb=rb, cb=cb, ci=ci, c=c, c2=c2: E.matmul(
                        psW1[rb:rb + 32, cb:cb + 32],
                        kbf[:, ci * 256 + c * 32: ci * 256 + (c + 1) * 32],
                        kw[:, ci * 256 + c * 32: ci * 256 + (c + 1) * 32],
                        start=(c2 == 0), stop=(c2 == 1), tile_position=(0, rb)))
                for c2 in range(2):
                    ci = 2 * h + c2
                    TT(lambda E, rb=rb, cb=cb, ci=ci, c=c, c2=c2: E.matmul(
                        psW1[rb:rb + 32, 256 + cb: 256 + cb + 32],
                        kw[:, ci * 256 + c * 32: ci * 256 + (c + 1) * 32],
                        kbf[:, ci * 256 + c * 32: ci * 256 + (c + 1) * 32],
                        start=(c2 == 0), stop=(c2 == 1), tile_position=(0, rb)),
                       inc=(h == H - 1 and c == 7 and c2 == 1))
        tA = ("tensor_t", P.cnt["tensor_t"])
        P.wait("vector", tA)
        V(lambda E: E.tensor_tensor(Lst[:], maskl32r[:], psW1[:, 0:256], op=OP.mult))
        V(lambda E: E.tensor_scalar(Lst[:], Lst[:], -1.0, None, op0=OP.mult))
        V(lambda E: E.tensor_tensor(Ust[:], masku32r[:], psW1[:, 256:512], op=OP.mult))
        V(lambda E: E.tensor_scalar(Ust[:], Ust[:], -1.0, None, op0=OP.mult))
        V(lambda E: E.tensor_tensor(Dst[:], Ust[:], i32r[:], op=OP.add))
        V(lambda E: E.tensor_copy(Up[:], Ust[:]))
        V(lambda E: E.tensor_copy(Lp[:], Lst[:]))
        V(lambda E: E.nop(), inc=True)
        P.wait("tensor", ("vector_t", P.cnt["vector_t"]))

        if p3stop < 6:
            continue
        tiles_by_cc = [[(h, c) for h in range(H) for c in (cc, cc + 4)] for cc in range(4)]
        for lvl in range(4):
            last = (lvl == 3)
            for cc in range(4):
                for (h, c) in tiles_by_cc[cc]:
                    rb, cb = slot(h, c)
                    L_t = Lp[rb:rb + 32, cb:cb + 32]
                    U_t = Up[rb:rb + 32, cb:cb + 32]
                    TT(lambda E, L_t=L_t, U_t=U_t, rb=rb, cb=cb: E.matmul(
                        psW2[rb:rb + 32, cb:cb + 32], U_t, L_t,
                        start=True, stop=True, tile_position=(rb, rb)))
                    TT(lambda E, L_t=L_t, U_t=U_t, rb=rb, cb=cb: E.matmul(
                        psW2[rb:rb + 32, 256 + cb:256 + cb + 32], L_t, U_t,
                        start=True, stop=True, tile_position=(rb, rb)),
                       inc=(cc == 3 and (h, c) == tiles_by_cc[cc][-1]))
            tsq = ("tensor_t", P.cnt["tensor_t"])
            P.wait("vector", tsq)
            V(lambda E: E.tensor_copy(Lp[:], psW2[:, 0:256]))
            V(lambda E: E.tensor_copy(Up[:], psW2[:, 256:512]), inc=True)
            if last:
                V(lambda E: E.memset(psW3[:, 0:1024], 0.0), inc=True)
            P.wait("tensor", ("vector_t", P.cnt["vector_t"]))
            for cc in range(4):
                for (h, c) in tiles_by_cc[cc]:
                    rb, cb = slot(h, c)
                    L_t = Lp[rb:rb + 32, cb:cb + 32]
                    D_t = Dst[rb:rb + 32, cb:cb + 32]
                    i_t = i32[rb:rb + 32, 0:32]
                    if not last:
                        outap = psW3[rb:rb + 32, cb:cb + 32]
                    else:
                        outap = psW3[rb:rb + 32, (h * 8 + c) * 32:(h * 8 + c) * 32 + 32]
                    tp = (rb, rb)
                    TT(lambda E, i_t=i_t, D_t=D_t, outap=outap, tp=tp: E.matmul(
                        outap, i_t, D_t, start=True, stop=False, tile_position=tp))
                    TT(lambda E, L_t=L_t, D_t=D_t, outap=outap, tp=tp: E.matmul(
                        outap, L_t, D_t, start=False, stop=True, tile_position=tp),
                       inc=(cc == 3 and (h, c) == tiles_by_cc[cc][-1]))
            tch = ("tensor_t", P.cnt["tensor_t"])
            P.wait("vector", tch)
            if not last:
                V(lambda E: E.tensor_copy(Dst[:], psW3[:, 0:256]), inc=True)
            else:
                V(lambda E: E.tensor_copy(DfinPad[:], psW3[:, 0:1024]), inc=True)
            P.wait("tensor", ("vector_t", P.cnt["vector_t"]))

        if p3stop < 7:
            continue
        def dfinpad(h, c):
            return DfinPad[:, (h * 8 + c) * 32:(h * 8 + c) * 32 + 32]

        for h in range(H):
            for tj in range(2):
                rps = rot_acquire()
                for cc in range(4):
                    c = tj * 4 + cc
                    TT(lambda E, h=h, c=c, cc=cc, tj=tj, rps=rps: E.matmul(
                        rps[32 * cc:32 * cc + 32, 0:256],
                        dfinpad(h, c),
                        vbT[:, (h * 2 + tj) * 256:(h * 2 + tj) * 256 + 256],
                        start=True, stop=True, tile_position=(0, 32 * cc)), inc=(cc == 3))
                P.wait("vector", ("tensor_t", P.cnt["tensor_t"]))
                V(lambda E, h=h, tj=tj, rps=rps: E.tensor_copy(ut[:, (h * 2 + tj) * 256:(h * 2 + tj) * 256 + 256],
                                                               rps[:, 0:256]), inc=True)
                rot_release(("vector_t", P.cnt["vector_t"]))
            for dkh in range(2):
                rps = rot_acquire()
                for c in range(8):
                    tj = c // 4
                    TT(lambda E, h=h, c=c, tj=tj, dkh=dkh, rps=rps: E.matmul(
                        rps[:, c * 32:(c + 1) * 32],
                        kbT[:, (h * 2 + tj) * 256 + dkh * 128: (h * 2 + tj) * 256 + dkh * 128 + 128],
                        dfinpad(h, c),
                        start=True, stop=True), inc=(c == 7))
                P.wait("vector", ("tensor_t", P.cnt["tensor_t"]))
                V(lambda E, h=h, dkh=dkh, rps=rps: E.tensor_copy(wf[:, (h * 2 + dkh) * 256:(h * 2 + dkh) * 256 + 256],
                                                                 rps[:, 0:256]), inc=True)
                rot_release(("vector_t", P.cnt["vector_t"]))

        if p3stop < 8:
            continue
        for _ in range(3):
            p5a_slot()
        for h in range(H):
            for tj in range(2):
                rps = rot_acquire()
                for dkh in range(2):
                    TT(lambda E, h=h, tj=tj, dkh=dkh, rps=rps: E.matmul(
                        rps[:, 0:256],
                        wf[:, (h * 2 + dkh) * 256 + tj * 128: (h * 2 + dkh) * 256 + tj * 128 + 128],
                        S_t[:, (h * 2 + dkh) * 256:(h * 2 + dkh) * 256 + 256],
                        start=(dkh == 0), stop=(dkh == 1)), inc=(dkh == 1))
                P.wait("vector", ("tensor_t", P.cnt["tensor_t"]))
                V(lambda E, h=h, tj=tj, rps=rps: E.tensor_copy(WSt[:, (h * 2 + tj) * 256:(h * 2 + tj) * 256 + 256],
                                                               rps[:, 0:256]), inc=True)
                rot_release(("vector_t", P.cnt["vector_t"]))
                if own:
                    rps = rot_acquire()
                    for dkh in range(2):
                        TT(lambda E, h=h, tj=tj, dkh=dkh, rps=rps: E.matmul(
                            rps[:, 0:256],
                            qtf[:, (2 * h + dkh) * 256 + tj * 128: (2 * h + dkh) * 256 + tj * 128 + 128],
                            S_t[:, (h * 2 + dkh) * 256:(h * 2 + dkh) * 256 + 256],
                            start=(dkh == 0), stop=(dkh == 1)), inc=(dkh == 1))
                    P.wait("vector", ("tensor_t", P.cnt["tensor_t"]))
                    V(lambda E, h=h, tj=tj, rps=rps: E.tensor_copy(QSt[:, (h * 2 + tj) * 256:(h * 2 + tj) * 256 + 256],
                                                                   rps[:, 0:256]), inc=True)
                    rot_release(("vector_t", P.cnt["vector_t"]))

        if p3stop < 9:
            continue
        V(lambda E: E.tensor_tensor(upb[:], ut[:], WSt[:], op=OP.subtract), inc=True)
        tup0 = ("vector_t", P.cnt["vector_t"])
        P.wait("tensor", tup0)

        if p3stop < 10:
            continue
        for h in range(H):
            for tl in range(2):
                rps = rot_acquire()
                for dkh in range(2):
                    TT(lambda E, h=h, tl=tl, dkh=dkh, rps=rps: E.matmul(
                        rps[:, 0:256],
                        knf[:, (2 * h + dkh) * 256 + tl * 128: (2 * h + dkh) * 256 + tl * 128 + 128],
                        wf[:, (h * 2 + dkh) * 256:(h * 2 + dkh) * 256 + 256],
                        start=(dkh == 0), stop=(dkh == 1)), inc=(dkh == 1))
                P.wait("vector", ("tensor_t", P.cnt["tensor_t"]))
                V(lambda E, tl=tl, rps=rps: E.tensor_copy(PTt[:, tl * 256:(tl + 1) * 256], rps[:, 0:256]), inc=True)
                rot_release(("vector_t", P.cnt["vector_t"]))
            if own:
                for tl in range(2):
                    rps = rot_acquire()
                    for dkh in range(2):
                        TT(lambda E, h=h, tl=tl, dkh=dkh, rps=rps: E.matmul(
                            rps[:, 0:256],
                            knf[:, (2 * h + dkh) * 256 + tl * 128: (2 * h + dkh) * 256 + tl * 128 + 128],
                            qtf[:, (2 * h + dkh) * 256:(2 * h + dkh) * 256 + 256],
                            start=(dkh == 0), stop=(dkh == 1)), inc=(dkh == 1))
                    P.wait("vector", ("tensor_t", P.cnt["tensor_t"]))
                    V(lambda E, h=h, tl=tl, rps=rps: E.tensor_tensor(GTt[:, tl * 256:(tl + 1) * 256],
                                                                     rps[:, 0:256], ghmask[h][tl][:], op=OP.mult), inc=True)
                    rot_release(("vector_t", P.cnt["vector_t"]))
            rot_fence()
            for j in range(1, WIN):
                nk = min(32 * j, 128)
                TT(lambda E, h=h, j=j, nk=nk: E.matmul(
                    psW1[32 * (j % 4):32 * (j % 4) + 32, 0:256],
                    PTt[0:nk, 0 * 256 + j * 32: 0 * 256 + j * 32 + 32],
                    upb[0:nk, (h * 2 + 0) * 256:(h * 2 + 0) * 256 + 256],
                    start=True, stop=(j <= 4), tile_position=(0, 32 * (j % 4))), inc=(j <= 4))
                if j > 4:
                    nk2 = 32 * j - 128
                    TT(lambda E, h=h, j=j, nk2=nk2: E.matmul(
                        psW1[32 * (j % 4):32 * (j % 4) + 32, 0:256],
                        PTt[0:nk2, 1 * 256 + j * 32: 1 * 256 + j * 32 + 32],
                        upb[0:nk2, (h * 2 + 1) * 256:(h * 2 + 1) * 256 + 256],
                        start=False, stop=True, tile_position=(0, 32 * (j % 4))), inc=True)
                P.wait("vector", ("tensor_t", P.cnt["tensor_t"]))
                V(lambda E, h=h, j=j: E.scalar_tensor_tensor(
                    upb[32 * (j % 4):32 * (j % 4) + 32, (h * 2 + j // 4) * 256 + 0:(h * 2 + j // 4) * 256 + 256],
                    psW1[32 * (j % 4):32 * (j % 4) + 32, 0:256],
                    lamcol[32 * (j % 4):32 * (j % 4) + 32, 4 * h:4 * h + 1],
                    upb[32 * (j % 4):32 * (j % 4) + 32, (h * 2 + j // 4) * 256 + 0:(h * 2 + j // 4) * 256 + 256],
                    op0=OP.mult, op1=OP.add), inc=True)
                P.wait("tensor", ("vector_t", P.cnt["vector_t"]))
            if own:
                for tjo in range(2):
                    for tl in range(2):
                        TT(lambda E, h=h, tjo=tjo, tl=tl: E.matmul(
                            psW2[:, 0:256],
                            GTt[:, tl * 256 + tjo * 128: tl * 256 + tjo * 128 + 128],
                            upb[:, (h * 2 + tl) * 256:(h * 2 + tl) * 256 + 256],
                            start=(tl == 0), stop=(tl == 1)), inc=(tl == 1))
                    P.wait("vector", ("tensor_t", P.cnt["tensor_t"]))
                    V(lambda E, h=h, tjo=tjo: E.scalar_tensor_tensor(
                        osb[:], psW2[:, 0:256], 1.0,
                        QSt[:, (h * 2 + tjo) * 256:(h * 2 + tjo) * 256 + 256],
                        op0=OP.mult, op1=OP.add), inc=True)
                    tosb = ("vector_t", P.cnt["vector_t"])
                    P.wait("tensor", tosb)
                    P.wait("gpsimd", tosb)
                    st(delta_s[W0 - 2048 + tjo * 128: W0 - 2048 + (tjo + 1) * 128, h * 256:(h + 1) * 256], osb[:])
                    P.wait("vector", ("gpsimd_d", P.cnt["gpsimd_d"]))
            for dkm in range(2):
                for tj in range(2):
                    TT(lambda E, h=h, dkm=dkm, tj=tj: E.matmul(
                        psW2[:, 256:512],
                        ktT[:, (h * 2 + tj) * 256 + dkm * 128: (h * 2 + tj) * 256 + dkm * 128 + 128],
                        upb[:, (h * 2 + tj) * 256:(h * 2 + tj) * 256 + 256],
                        start=(tj == 0), stop=(tj == 1)), inc=(tj == 1))
                P.wait("vector", ("tensor_t", P.cnt["tensor_t"]))
                V(lambda E, h=h, dkm=dkm: E.scalar_tensor_tensor(
                    S_t[:, (h * 2 + dkm) * 256:(h * 2 + dkm) * 256 + 256],
                    S_t[:, (h * 2 + dkm) * 256:(h * 2 + dkm) * 256 + 256],
                    lamcol[:, 4 * h + 1:4 * h + 2],
                    psW2[:, 256:512], op0=OP.mult, op1=OP.add))
                V(lambda E, h=h, dkm=dkm: E.tensor_scalar(
                    S_t[:, (h * 2 + dkm) * 256:(h * 2 + dkm) * 256 + 256],
                    S_t[:, (h * 2 + dkm) * 256:(h * 2 + dkm) * 256 + 256],
                    lamcol[:, 4 * h + 2:4 * h + 3], None, op0=OP.mult), inc=True)
                P.wait("tensor", ("vector_t", P.cnt["vector_t"]))
        win_free[0] = P.token("vector")
        for _ in range(3):
            p5a_slot()

    while p5c["tr"] < NJOB:
        p5a_slot(drain=True)
    P.wait("scalar", ("scalar_d", P.cnt["scalar_d"]))
    P.barrier()
    es3.close()
    if upto in ("P3", "P3w1"):
        P.finalize(block)
        es.close()
        return nc

    es4 = contextlib.ExitStack()
    sb4 = lambda name, p, f, dt=F32: es4.enter_context(nc.sbuf_tensor(name, [p, f], dt))
    ps4 = lambda name, p, f: es4.enter_context(nc.psum_tensor(name, [p, f], F32))

    w1_t = sb4("w1_t", 128, 9 * 2048)
    wo_t2 = sb4("wo_t2", 128, 8192)
    w2_t = sb4("w2_t", 128, 16 * 16)
    b1c = sb4("b1c_t", 128, 16)
    b2t = sb4("b2t_t", 128, 16)
    normw_t = sb4("normw_tt", 128, 1024)
    rtempt = sb4("rtempt", 128, 16)
    floort = sb4("floort", 128, 16)
    tmp41 = sb4("tmp41", 4, 16)
    vwin = sb4("vwin", 128, 8 * 318)
    shrt = sb4("shrt", 128, 8 * 256)
    lng = sb4("lng", 128, 8 * 256)
    xnw = sb4("xnw", 128, 8 * 256)
    dtm = sb4("dtm", 128, 1024)
    stm4 = sb4("stm4", 128, 1024)
    ltm4 = sb4("ltm4", 128, 1024)
    vtm4 = sb4("vtm4", 128, 1024)
    sq4 = sb4("sq4", 128, 1024)
    statc = sb4("statc", 128, 64)
    stF = sb4("stF", 64, 128)
    hmid2 = [sb4("hmidA", 128, 128), sb4("hmidB", 128, 128)]
    tmpP = sb4("tmpP", 128, 256)
    p4rot = {"g": 0, "ev": {}}
    p4banks_ref = []
    ptile = sb4("ptile", 128, 16)
    otm = sb4("otm", 128, 1024)
    onF = sb4("onF", 128, 8 * 128)
    xtile = sb4("xtile", 128, 1024)
    y1t = otm
    mcol = sb4("mcol", 128, 16)

    psQ0 = ps4("psQ0", 128, 512)
    psQ1 = ps4("psQ1", 128, 512)

    t4l = []
    for ki in range(8):
        t4l.append(ld(w1_t[:, ki * 2048:(ki + 1) * 2048], IN["w1"][ki * 128:(ki + 1) * 128, :]))
    t4l.append(ld(w1_t[0:64, 8 * 2048:9 * 2048], IN["w1"][1024:1088, :]))
    for ki in range(8):
        t4l.append(ld(wo_t2[:, ki * 1024:(ki + 1) * 1024], IN["wo"][ki * 128:(ki + 1) * 128, :]))
    for ki in range(16):
        t4l.append(ld(w2_t[:, ki * 16:(ki + 1) * 16], IN["w2"][ki * 128:(ki + 1) * 128, :]))
    t4l.append(ld(b1c[:], IN["b1c"]))
    t4l.append(ld(b2t[:], IN["b2t"]))
    t4l.append(ld(normw_t[:], IN["normw_t"]))
    t4l.append(ld(tmp41[0:4, 0:1], IN["ltemp"]))
    t4l.append(ld(misc[0:1, 0:16], IN["floorp"]))
    P.waits("scalar", t4l)
    P.waits("vector", t4l)
    P.waits("tensor", t4l)
    P.ins("scalar", lambda E: E.activation(tmp41[0:4, 1:2], tmp41[0:4, 0:1], AF.Exp), inc=True)
    P.wait("vector", ("scalar_t", P.cnt["scalar_t"]))
    V(lambda E: E.reciprocal(tmp41[0:4, 2:3], tmp41[0:4, 1:2]))
    V(lambda E: E.nop(), inc=True)
    P.wait("tensor", ("vector_t", P.cnt["vector_t"]))
    for h in range(H):
        TT(lambda E, h=h: E.matmul(psQ0[:, h:h + 1], sel4[:, h * 128:(h + 1) * 128], tmp41[0:4, 2:3],
                                   start=True, stop=True), inc=(h == 3))
    P.wait("vector", ("tensor_t", P.cnt["tensor_t"]))
    for h in range(H):
        V(lambda E, h=h: E.tensor_copy(rtempt[:, 4 * h:4 * h + 4], bcast_ap(psQ0[:, h:h + 1], 4)))
    V(lambda E: E.nop(), inc=True)
    P.wait("tensor", ("vector_t", P.cnt["vector_t"]))
    TT(lambda E: E.matmul(psQ0[:, 0:16], ones128[0:1, :], misc[0:1, 0:16], start=True, stop=True), inc=True)
    P.wait("scalar", ("tensor_t", P.cnt["tensor_t"]))
    P.ins("scalar", lambda E: E.activation(floort[:], psQ0[:, 0:16], AF.Sigmoid), inc=True)
    P.wait("vector", ("scalar_t", P.cnt["scalar_t"]))
    V(lambda E: E.tensor_scalar(floort[:], floort[:], 0.075, None, op0=OP.mult))
    tb4 = P.token("vector")
    P.waits("tensor", [tb4])
    P.waits("sync", [tb4])

    for ti in range(8):
        T0 = ti * 256
        tl4 = []
        for ci in range(8):
            lo = T0 - 62
            for (seg0, seg1) in ((lo, T0), (T0, T0 + 256)):
                dst0 = seg0 - lo
                t = seg0
                while t < seg1:
                    if t < 0:
                        src_row = 3 * 8 + ci
                        col = 512 + t
                        n = min(seg1, 0) - t
                    else:
                        src_row = (4 + t // 512) * 8 + ci
                        col = t % 512
                        n = min(seg1 - t, 512 - col)
                    tl4.append(ld(vwin[:, ci * 318 + (t - lo): ci * 318 + (t - lo) + n],
                                  v_s[src_row][:, col:col + n]))
                    t += n
            tl4.append(ld(xnw[:, ci * 256:(ci + 1) * 256],
                          xnT_s[(ti // 2) * 8 + ci][:, (ti % 2) * 256:(ti % 2) * 256 + 256]))
        P.waits("vector", tl4)
        if ti == 0:
            V(lambda E: E.tensor_scalar(
                vwin[:].rearrange("p (c x) -> p c x", x=318)[:, :, 0:62],
                vwin[:].rearrange("p (c x) -> p c x", x=318)[:, :, 0:62],
                flag[:, 0:1], None, op0=OP.mult))
        for ci in range(8):
            base = ci * 318
            V(lambda E, ci=ci, base=base: E.tensor_scalar(
                shrt[:, ci * 256:(ci + 1) * 256], vwin[:, base + 60: base + 316],
                firw_s[:, ci * 3 + 0: ci * 3 + 1], None, op0=OP.mult))
            for j in (1, 2):
                V(lambda E, ci=ci, base=base, j=j: E.scalar_tensor_tensor(
                    shrt[:, ci * 256:(ci + 1) * 256], vwin[:, base + 60 + j: base + 316 + j],
                    firw_s[:, ci * 3 + j: ci * 3 + j + 1], shrt[:, ci * 256:(ci + 1) * 256],
                    op0=OP.mult, op1=OP.add))
            V(lambda E, ci=ci, base=base: E.tensor_scalar(
                lng[:, ci * 256:(ci + 1) * 256], vwin[:, base + 0: base + 256],
                firw_l[:, ci * 63 + 0: ci * 63 + 1], None, op0=OP.mult))
            for j in range(1, 63):
                V(lambda E, ci=ci, base=base, j=j: E.scalar_tensor_tensor(
                    lng[:, ci * 256:(ci + 1) * 256], vwin[:, base + j: base + 256 + j],
                    firw_l[:, ci * 63 + j: ci * 63 + j + 1], lng[:, ci * 256:(ci + 1) * 256],
                    op0=OP.mult, op1=OP.add))
        tfir = P.token("vector")
        P.wait("tensor", tfir)
        P.wait("sync", tfir)

        for s in range(2):
            tdl = ld(dtm[:], delta_s[T0 + s * 128: T0 + (s + 1) * 128, :])
            jobs = []
            for (srct, dstt, srcw, hal) in ((shrt, stm4, 256, 0), (lng, ltm4, 256, 0), (vwin, vtm4, 318, 62)):
                for ci in range(8):
                    jobs.append((srct, dstt, ci * srcw + hal + s * 128, ci))
            for j0 in range(0, len(jobs), 4):
                grp = jobs[j0:j0 + 4]
                g = p4rot["g"]
                p4rot["g"] += 1
                bank = [psQ0, psB0, psB1][g % 3]
                P.wait("tensor", p4rot["ev"].get(g - 3))
                sname = "p5at%d" % (g % 2)
                for gi, (srct, dstt, off, ci) in enumerate(grp):
                    if gi == len(grp) - 1:
                        P.cnt[sname] += 1
                        P.ins("tensor", lambda E, srct=srct, off=off, gi=gi, bank=bank, sname=sname: E.transpose(
                            bank[:, gi * 128:(gi + 1) * 128], srct[:, off: off + 128], ident[:]).then_inc(P.sem[sname], 1))
                    else:
                        TT(lambda E, srct=srct, off=off, gi=gi, bank=bank: E.transpose(
                            bank[:, gi * 128:(gi + 1) * 128], srct[:, off: off + 128], ident[:]))
                P.wait("scalar", (sname, P.cnt[sname]))
                for gi, (srct, dstt, off, ci) in enumerate(grp):
                    P.ins("scalar", lambda E, dstt=dstt, ci=ci, gi=gi, bank=bank: E.copy(
                        dstt[:, ci * 128:(ci + 1) * 128], bank[:, gi * 128:(gi + 1) * 128]),
                          inc=(gi == len(grp) - 1))
                p4rot["ev"][g] = ("scalar_t", P.cnt["scalar_t"])
            ttr4 = P.token("scalar")
            P.waits("vector", [ttr4, tdl])

            for (tix, tsr) in enumerate((stm4, ltm4, dtm, vtm4)):
                tt = tsr[:]
                tt3 = tt.rearrange("p (h d) -> p h d", d=256)
                V(lambda E: E.drain())
                V(lambda E, tt3=tt3, tix=tix: E.tensor_reduce(statc[:, tix * 16 + 0:tix * 16 + 4], tt3, axis=AXX, op=OP.add))
                V(lambda E, tt=tt: E.tensor_tensor(sq4[:], tt, tt, op=OP.mult))
                V(lambda E: E.drain())
                V(lambda E, tix=tix: E.tensor_reduce(statc[:, tix * 16 + 4:tix * 16 + 8],
                                                     sq4[:].rearrange("p (h d) -> p h d", d=256), axis=AXX, op=OP.add))
                V(lambda E: E.drain())
                V(lambda E, tt3=tt3, tix=tix: E.tensor_reduce(statc[:, tix * 16 + 8:tix * 16 + 12], tt3, axis=AXX, op=OP.max))
            for tix in range(4):
                c0 = tix * 16
                V(lambda E, c0=c0: E.tensor_scalar(statc[:, c0 + 0:c0 + 4], statc[:, c0 + 0:c0 + 4], 1.0 / 256, None, op0=OP.mult))
                V(lambda E, c0=c0: E.tensor_tensor(mcol[:, 0:4], statc[:, c0 + 0:c0 + 4], statc[:, c0 + 0:c0 + 4], op=OP.mult))
                V(lambda E, c0=c0: E.tensor_scalar(statc[:, c0 + 12:c0 + 16], statc[:, c0 + 4:c0 + 8], 1.0 / 256, None, op0=OP.mult))
                V(lambda E, c0=c0: E.tensor_tensor(statc[:, c0 + 12:c0 + 16], statc[:, c0 + 12:c0 + 16], mcol[:, 0:4], op=OP.subtract))
                V(lambda E: E.nop(), inc=True)
                P.wait("scalar", ("vector_t", P.cnt["vector_t"]))
                P.ins("scalar", lambda E, c0=c0: E.activation(statc[:, c0 + 4:c0 + 8], statc[:, c0 + 4:c0 + 8], AF.Sqrt), inc=True)
                P.wait("vector", ("scalar_t", P.cnt["scalar_t"]))
            for tix in range(4):
                c0 = tix * 16
                for h in range(H):
                    V(lambda E, c0=c0, h=h: E.tensor_copy(sq4[:, c0 + 4 * h + 0:c0 + 4 * h + 1], statc[:, c0 + 0 + h:c0 + 1 + h]))
                    V(lambda E, c0=c0, h=h: E.tensor_copy(sq4[:, c0 + 4 * h + 1:c0 + 4 * h + 2], statc[:, c0 + 12 + h:c0 + 13 + h]))
                    V(lambda E, c0=c0, h=h: E.tensor_copy(sq4[:, c0 + 4 * h + 2:c0 + 4 * h + 3], statc[:, c0 + 8 + h:c0 + 9 + h]))
                    V(lambda E, c0=c0, h=h: E.tensor_copy(sq4[:, c0 + 4 * h + 3:c0 + 4 * h + 4], statc[:, c0 + 4 + h:c0 + 5 + h]))
            V(lambda E: E.nop(), inc=True)
            P.wait("tensor", ("vector_t", P.cnt["vector_t"]))
            TT(lambda E: E.transpose(psQ1[0:64, 0:128], sq4[:, 0:64], ident[:]), inc=True)
            P.wait("scalar", ("tensor_t", P.cnt["tensor_t"]))
            P.ins("scalar", lambda E: E.copy(stF[0:64, 0:128], psQ1[0:64, 0:128]), inc=True)
            P.wait("tensor", ("scalar_t", P.cnt["scalar_t"]))

            g_tok = {}
            w1tok = {}
            gbanks = [psQ0, psB0, psB1]
            P.wait("tensor", p4rot["ev"][p4rot["g"] - 1])
            for mi in range(17):
                if mi < 16:
                    bank = gbanks[mi % 3]
                    P.wait("tensor", g_tok.get(mi - 3))
                    for ki in range(8):
                        TT(lambda E, mi=mi, ki=ki, s=s, bank=bank: E.matmul(
                            bank[:, 0:128], w1_t[:, ki * 2048 + mi * 128: ki * 2048 + (mi + 1) * 128],
                            xnw[:, ki * 256 + s * 128: ki * 256 + s * 128 + 128], start=(ki == 0), stop=False))
                    TT(lambda E, mi=mi, bank=bank: E.matmul(
                        bank[:, 0:128], w1_t[0:64, 8 * 2048 + mi * 128: 8 * 2048 + (mi + 1) * 128],
                        stF[0:64, 0:128], start=False, stop=True), inc=True)
                    w1tok[mi] = ("tensor_t", P.cnt["tensor_t"])
                if mi >= 1:
                    mp = mi - 1
                    P.wait("tensor", g_tok[mp])
                    P.cnt["w2s"] += 1
                    P.ins("tensor", lambda E, mp=mp: E.matmul(
                        psQ1[:, 0:16], hmid2[mp % 2][:],
                        w2_t[:, mp * 16:(mp + 1) * 16], start=(mp == 0), stop=(mp == 15)).then_inc(P.sem["w2s"], 1))
                if mi < 16:
                    P.wait("scalar", w1tok[mi])
                    if mi >= 2:
                        P.wait("scalar", ("w2s", P.cnt["w2s"]))
                    P.ins("scalar", lambda E, mi=mi, bank=gbanks[mi % 3]: E.activation(
                        hmid2[mi % 2][:], bank[:, 0:128], AF.Gelu,
                        bias=b1c[:, mi:mi + 1], scale=1.0), inc=True)
                    g_tok[mi] = ("scalar_t", P.cnt["scalar_t"])
            P.wait("vector", ("w2s", P.cnt["w2s"]))
            pt = ptile[:]
            V(lambda E, pt=pt: E.tensor_tensor(pt, b2t[:], psQ1[:, 0:16], op=OP.add))
            V(lambda E, pt=pt: E.tensor_tensor(pt, pt, rtempt[:], op=OP.mult))
            V(lambda E: E.drain())
            V(lambda E, pt=pt: E.tensor_reduce(mcol[:, 0:4], pt.rearrange("p (h g) -> p h g", g=4), axis=AXX, op=OP.max))
            for h in range(H):
                V(lambda E, pt=pt, h=h: E.tensor_scalar(pt[:, 4 * h:4 * h + 4], pt[:, 4 * h:4 * h + 4],
                                                        mcol[:, h:h + 1], None, op0=OP.subtract))
            V(lambda E: E.nop(), inc=True)
            P.wait("scalar", ("vector_t", P.cnt["vector_t"]))
            P.ins("scalar", lambda E, pt=pt: E.activation(pt, pt, AF.Exp), inc=True)
            P.wait("vector", ("scalar_t", P.cnt["scalar_t"]))
            V(lambda E: E.drain())
            V(lambda E, pt=pt: E.tensor_reduce(mcol[:, 0:4], pt.rearrange("p (h g) -> p h g", g=4), axis=AXX, op=OP.add))
            V(lambda E: E.reciprocal(mcol[:, 4:8], mcol[:, 0:4]))
            V(lambda E: E.drain())
            for h in range(H):
                V(lambda E, pt=pt, h=h: E.tensor_scalar(pt[:, 4 * h:4 * h + 4], pt[:, 4 * h:4 * h + 4],
                                                        mcol[:, 4 + h:5 + h], None, op0=OP.mult))
            V(lambda E, pt=pt: E.tensor_tensor(pt, pt, floort[:], op=OP.max))
            V(lambda E: E.drain())
            V(lambda E, pt=pt: E.tensor_reduce(mcol[:, 0:4], pt.rearrange("p (h g) -> p h g", g=4), axis=AXX, op=OP.add))
            V(lambda E: E.reciprocal(mcol[:, 4:8], mcol[:, 0:4]))
            V(lambda E: E.drain())
            for h in range(H):
                V(lambda E, pt=pt, h=h: E.tensor_scalar(pt[:, 4 * h:4 * h + 4], pt[:, 4 * h:4 * h + 4],
                                                        mcol[:, 4 + h:5 + h], None, op0=OP.mult))

            for h in range(H):
                reg = h * 256
                V(lambda E, pt=pt, reg=reg, h=h: E.tensor_scalar(
                    otm[:, reg:reg + 256], stm4[:, reg:reg + 256], pt[:, 4 * h + 0:4 * h + 1], None, op0=OP.mult))
                for (g, tsr) in ((1, ltm4), (2, dtm), (3, vtm4)):
                    V(lambda E, pt=pt, reg=reg, h=h, g=g, tsr=tsr: E.scalar_tensor_tensor(
                        otm[:, reg:reg + 256], tsr[:, reg:reg + 256], pt[:, 4 * h + g:4 * h + g + 1],
                        otm[:, reg:reg + 256], op0=OP.mult, op1=OP.add))
            V(lambda E: E.tensor_tensor(sq4[:], otm[:], otm[:], op=OP.mult))
            V(lambda E: E.drain())
            V(lambda E: E.tensor_reduce(mcol[:, 0:4], sq4[:].rearrange("p (h d) -> p h d", d=256), axis=AXX, op=OP.add))
            V(lambda E: E.tensor_scalar(mcol[:, 0:4], mcol[:, 0:4], 1.0 / 256, 1e-5, op0=OP.mult, op1=OP.add))
            V(lambda E: E.nop(), inc=True)
            P.wait("scalar", ("vector_t", P.cnt["vector_t"]))
            P.ins("scalar", lambda E: E.activation(mcol[:, 4:8], mcol[:, 0:4], AF.Sqrt), inc=True)
            P.wait("vector", ("scalar_t", P.cnt["scalar_t"]))
            V(lambda E: E.reciprocal(mcol[:, 8:12], mcol[:, 4:8]))
            V(lambda E: E.drain())
            for h in range(H):
                reg = h * 256
                V(lambda E, reg=reg, h=h: E.tensor_scalar(
                    otm[:, reg:reg + 256], otm[:, reg:reg + 256], mcol[:, 8 + h:9 + h], None, op0=OP.mult))
            V(lambda E: E.tensor_tensor(otm[:], otm[:], normw_t[:], op=OP.mult))
            tcomb = P.token("vector")
            P.wait("tensor", tcomb)

            for cg in range(2):
                g = p4rot["g"]
                p4rot["g"] += 1
                bank = [psQ0, psB0, psB1][g % 3]
                P.wait("tensor", p4rot["ev"].get(g - 3))
                sname = "p5at%d" % (g % 2)
                for cj in range(4):
                    ci = cg * 4 + cj
                    if cj == 3:
                        P.cnt[sname] += 1
                        P.ins("tensor", lambda E, ci=ci, cj=cj, bank=bank, sname=sname: E.transpose(
                            bank[:, cj * 128:(cj + 1) * 128], otm[:, ci * 128:(ci + 1) * 128], ident[:]).then_inc(P.sem[sname], 1))
                    else:
                        TT(lambda E, ci=ci, cj=cj, bank=bank: E.transpose(
                            bank[:, cj * 128:(cj + 1) * 128], otm[:, ci * 128:(ci + 1) * 128], ident[:]))
                P.wait("scalar", (sname, P.cnt[sname]))
                for cj in range(4):
                    ci = cg * 4 + cj
                    P.ins("scalar", lambda E, ci=ci, cj=cj, bank=bank: E.copy(
                        onF[:, ci * 128:(ci + 1) * 128], bank[:, cj * 128:(cj + 1) * 128]), inc=(cj == 3))
                p4rot["ev"][g] = ("scalar_t", P.cnt["scalar_t"])
            P.wait("tensor", p4rot["ev"][p4rot["g"] - 1])

            tx4 = ld(xtile[:], IN["x12"][2048 + T0 + s * 128: 2048 + T0 + (s + 1) * 128, :])
            for nn in range(2):
                for ki in range(8):
                    TT(lambda E, nn=nn, ki=ki: E.matmul(
                        psQ1[:, 0:512], onF[:, ki * 128:(ki + 1) * 128],
                        wo_t2[:, ki * 1024 + nn * 512: ki * 1024 + nn * 512 + 512],
                        start=(ki == 0), stop=(ki == 7)), inc=(ki == 7))
                P.waits("vector", [("tensor_t", P.cnt["tensor_t"]), tx4])
                V(lambda E, nn=nn: E.tensor_tensor(y1t[:, nn * 512:(nn + 1) * 512],
                                                   xtile[:, nn * 512:(nn + 1) * 512], psQ1[:, 0:512], op=OP.add), inc=True)
                P.wait("tensor", ("vector_t", P.cnt["vector_t"]))
            ty1 = P.token("vector")
            P.wait("scalar", ty1)
            tspill = st(y1_s[T0 + s * 128: T0 + (s + 1) * 128, :], y1t[:], eng="scalar")
            P.wait("vector", tspill)
            P.wait("sync", tspill)
        tend = P.token("vector")
        P.wait("sync", tend)

    P.barrier()
    es4.close()
    if upto == "P4":
        P.finalize(block)
        es.close()
        return nc

    es5 = contextlib.ExitStack()
    sb5 = lambda name, p, f, dt=F32: es5.enter_context(nc.sbuf_tensor(name, [p, f], dt))
    ps5 = lambda name, p, f: es5.enter_context(nc.psum_tensor(name, [p, f], F32))

    wbuf = sb5("wbuf", 128, 2816)
    wbuf2 = sb5("wbuf2", 128, 2816)
    uq5 = sb5("uq5", 128, 2816)
    gq5 = sb5("gq5", 128, 2816)
    sg5 = sb5("sg5", 128, 2816)
    qr5 = sb5("qr5", 128, 176)
    rc5 = sb5("rc5", 128, 176)
    psF = [psB0, psB1] + [ps5("psF%d" % i, 128, 512) for i in range(6)]

    def emit_quant(buf, K):
        nb = K // 16
        bv = buf.rearrange("p (nb b) -> p nb b", b=16)
        V(lambda E: E.drain())
        V(lambda E, bv=bv, nb=nb: E.tensor_reduce(qr5[:, 0:nb], bv, axis=AXX, op=OP.max, apply_absolute_value=True))
        V(lambda E, nb=nb: E.tensor_scalar(qr5[:, 0:nb], qr5[:, 0:nb], 1.0 / 6, 1e-12, op0=OP.mult, op1=OP.max))
        V(lambda E, nb=nb: E.reciprocal(rc5[:, 0:nb], qr5[:, 0:nb]))
        V(lambda E: E.drain())
        rb = bcast_ap(rc5[:, 0:nb], 16)
        V(lambda E, bv=bv, rb=rb, K=K: E.tensor_tensor(uq5[:, 0:K].rearrange("p (nb b) -> p nb b", b=16), bv, rb, op=OP.mult))
        V(lambda E, K=K: E.tensor_scalar(sg5[:, 0:K], uq5[:, 0:K], 0.0, None, op0=OP.is_gt))
        V(lambda E, K=K: E.tensor_scalar(gq5[:, 0:K], uq5[:, 0:K], 0.0, None, op0=OP.is_lt))
        V(lambda E, K=K: E.tensor_tensor(sg5[:, 0:K], sg5[:, 0:K], gq5[:, 0:K], op=OP.subtract))
        V(lambda E, K=K: E.tensor_tensor(uq5[:, 0:K], uq5[:, 0:K], sg5[:, 0:K], op=OP.mult))
        V(lambda E, K=K: E.tensor_scalar(gq5[:, 0:K], uq5[:, 0:K], 0.25, None, op0=OP.is_gt))
        for thr in (0.75, 1.25, 1.75):
            V(lambda E, K=K, thr=thr: E.scalar_tensor_tensor(gq5[:, 0:K], uq5[:, 0:K], thr, gq5[:, 0:K], op0=OP.is_gt, op1=OP.add))
        V(lambda E, K=K: E.tensor_scalar(gq5[:, 0:K], gq5[:, 0:K], 0.5, None, op0=OP.mult))
        for thr in (2.5, 3.5, 5.0, 5.0):
            V(lambda E, K=K, thr=thr: E.scalar_tensor_tensor(gq5[:, 0:K], uq5[:, 0:K], thr, gq5[:, 0:K], op0=OP.is_gt, op1=OP.add))
        V(lambda E, K=K: E.tensor_tensor(gq5[:, 0:K], gq5[:, 0:K], sg5[:, 0:K], op=OP.mult))
        qb = bcast_ap(qr5[:, 0:nb], 16)
        V(lambda E, buf=buf, K=K, qb=qb, nb=nb: E.tensor_tensor(
            buf.rearrange("p (nb b) -> p nb b", b=16),
            gq5[:, 0:K].rearrange("p (nb b) -> p nb b", b=16), qb, op=OP.mult))

    xq5 = sb5("xq5", 128, 8 * 2048)
    wgk = sb5("wgk", 128, 8 * 512)
    wuk = sb5("wuk", 128, 8 * 512)
    gsil = sb5("gsil", 128, 512)
    hqbuf = sb5("hqbuf", 128, 512)
    t5x = []
    for ci in range(8):
        for tt in range(4):
            t5x.append(ld(xq5[:, ci * 2048 + tt * 512: ci * 2048 + (tt + 1) * 512], xqT_s[tt * 8 + ci]))
    P.waits("tensor", t5x)
    ncols = [(0, 512), (512, 512), (1024, 512), (1536, 512), (2048, 512), (2560, 256)]
    for (n0, nsz) in ncols:
        tlw = []
        for kc in range(8):
            tlw.append(ld(wgk[:, kc * 512: kc * 512 + nsz], wgT_s[kc][:, n0:n0 + nsz]))
            tlw.append(ld(wuk[:, kc * 512: kc * 512 + nsz], wuT_s[kc][:, n0:n0 + nsz]))
        P.waits("tensor", tlw)
        ps_cons = [None, None]
        mmtok = [None] * 17
        for m in range(17):
            if m < 16:
                pr = m % 2
                P.wait("tensor", ps_cons[pr])
                for kc in range(8):
                    TT(lambda E, m=m, kc=kc, nsz=nsz, pr=pr: E.matmul(
                        psF[pr * 2][:, 0:nsz], xq5[:, kc * 2048 + m * 128: kc * 2048 + (m + 1) * 128],
                        wgk[:, kc * 512: kc * 512 + nsz], start=(kc == 0), stop=(kc == 7)))
                for kc in range(8):
                    TT(lambda E, m=m, kc=kc, nsz=nsz, pr=pr: E.matmul(
                        psF[pr * 2 + 1][:, 0:nsz], xq5[:, kc * 2048 + m * 128: kc * 2048 + (m + 1) * 128],
                        wuk[:, kc * 512: kc * 512 + nsz], start=(kc == 0), stop=(kc == 7)), inc=(kc == 7))
                mmtok[m] = ("tensor_t", P.cnt["tensor_t"])
            if m >= 1:
                mp = m - 1
                pr = mp % 2
                wb = wbuf if pr == 0 else wbuf2
                P.wait("scalar", mmtok[mp])
                P.wait("scalar", ps_cons[1 - pr])
                P.ins("scalar", lambda E, nsz=nsz, pr=pr: E.activation(gsil[:, 0:nsz], psF[pr * 2][:, 0:nsz], AF.Silu), inc=True)
                P.wait("vector", ("scalar_t", P.cnt["scalar_t"]))
                V(lambda E, nsz=nsz, pr=pr, wb=wb: E.tensor_tensor(wb[:, 0:nsz], gsil[:, 0:nsz], psF[pr * 2 + 1][:, 0:nsz], op=OP.mult), inc=True)
                ps_cons[pr] = ("vector_t", P.cnt["vector_t"])
                emit_quant(wb[:, 0:nsz], nsz)
                tq = P.token("vector")
                P.wait("tensor", tq)
                for kc in range(nsz // 128):
                    TT(lambda E, kc=kc, wb=wb: E.transpose(psF[4][:, kc * 128:(kc + 1) * 128],
                                                           wb[:, kc * 128:(kc + 1) * 128], ident[:]), inc=(kc == nsz // 128 - 1))
                P.wait("scalar", ("tensor_t", P.cnt["tensor_t"]))
                for kc in range(nsz // 128):
                    P.ins("scalar", lambda E, kc=kc: E.copy(hqbuf[:, kc * 128:(kc + 1) * 128],
                                                            psF[4][:, kc * 128:(kc + 1) * 128]), inc=(kc == nsz // 128 - 1))
                tsp = ("scalar_t", P.cnt["scalar_t"])
                P.wait("tensor", tsp)
                P.wait("gpsimd", tsp)
                for kc in range(nsz // 128):
                    st(hqT_s[n0 // 128 + kc][:, mp * 128:(mp + 1) * 128], hqbuf[:, kc * 128:(kc + 1) * 128])
                P.wait("scalar", ("gpsimd_d", P.cnt["gpsimd_d"]))
                P.wait("vector", ("gpsimd_d", P.cnt["gpsimd_d"]))
        P.wait("sync", ("tensor_t", P.cnt["tensor_t"]))
    P.barrier()

    wdk = [sb5("wdk0", 128, 1024), sb5("wdk1", 128, 1024), sb5("wdk2", 128, 1024)]
    hqk = [sb5("hqk0", 128, 4 * 128), sb5("hqk1", 128, 4 * 128), sb5("hqk2", 128, 4 * 128)]
    y5 = sb5("y5", 128, 1024)
    mmt = {}
    ldt = {}

    def p5c_loads(li):
        mg, kc = li // 22, li % 22
        pb = li % 3
        P.wait("sync", mmt.get(li - 3))
        P.wait("gpsimd", mmt.get(li - 3))
        t = [ld(wdk[pb][:], wdT_s[kc])]
        for mm_ in range(4):
            m = mg * 4 + mm_
            t.append(ld(hqk[pb][:, mm_ * 128:(mm_ + 1) * 128], hqT_s[kc][:, m * 128:(m + 1) * 128], eng="gpsimd"))
        ldt[li] = t

    p5c_loads(0)
    p5c_loads(1)
    p5c_loads(2)
    for mg in range(4):
        for kc in range(22):
            li = mg * 22 + kc
            pb = li % 3
            P.waits("tensor", ldt[li])
            for mm_ in range(4):
                for nn in range(2):
                    TT(lambda E, mm_=mm_, nn=nn, kc=kc, pb=pb: E.matmul(
                        psF[mm_ * 2 + nn][:, 0:512],
                        hqk[pb][:, mm_ * 128:(mm_ + 1) * 128],
                        wdk[pb][:, nn * 512:(nn + 1) * 512],
                        start=(kc == 0), stop=(kc == 21)), inc=(mm_ == 3 and nn == 1))
            mmt[li] = ("tensor_t", P.cnt["tensor_t"])
            if li + 3 < 88:
                p5c_loads(li + 3)
        tdone = ("tensor_t", P.cnt["tensor_t"])
        for mm_ in range(4):
            m = mg * 4 + mm_
            ty = ld(y5[:], y1_s[m * 128:(m + 1) * 128, :], eng="gpsimd")
            P.waits("vector", [tdone, ty])
            for nn in range(2):
                V(lambda E, mm_=mm_, nn=nn: E.tensor_tensor(
                    y5[:, nn * 512:(nn + 1) * 512], y5[:, nn * 512:(nn + 1) * 512],
                    psF[mm_ * 2 + nn][:, 0:512], op=OP.add))
            V(lambda E: E.nop(), inc=True)
            tout = ("vector_t", P.cnt["vector_t"])
            P.wait("gpsimd", tout)
            st(out_h[m * 128:(m + 1) * 128, :], y5[:])
            P.wait("vector", ("gpsimd_d", P.cnt["gpsimd_d"]))
            P.wait("tensor", ("gpsimd_d", P.cnt["gpsimd_d"]))

    P.wait("sync", ("gpsimd_d", P.cnt["gpsimd_d"]))
    P.barrier()
    es5.close()

    P.finalize(block)
    es.close()
    return nc



def kernel(**inputs):
    maps = shard_inputs(inputs)
    nc = build_nc(upto="ALL")
    res = run_bass_kernel_spmd(nc, maps, core_ids=list(range(NCHIP)))
    out = np.zeros((4, 4096, 1024), np.float32)
    for c in range(NCHIP):
        b, half = c // 2, c % 2
        out[b, 2048 * half: 2048 * half + 2048] = res.results[c]["out"]
    return out


# version 10
# speedup vs baseline: 1.3316x; 1.0442x over previous
import numpy as np
import contextlib
import concourse.bass as bass
import concourse.mybir as mybir
from concourse.alu_op_type import AluOpType as OP
from concourse.bass_utils import run_bass_kernel_spmd

F32 = mybir.dt.float32
BF16 = mybir.dt.bfloat16
AF = mybir.ActivationFunctionType
AXX = mybir.AxisListType.X

H = 4
NCHIP = 8
T2 = 2048
TW = 256
WIN = 8
NWIN = 16
PP = 512
NPP = 8
D = 1024
FFI = 2816

ENGS = ("sync", "vector", "scalar", "tensor", "gpsimd")


class Prog:
    def __init__(self, nc, es):
        self.nc = nc
        self.q = {e: [] for e in ENGS}
        self.cnt = {}
        self.sem = {}
        for e in ENGS:
            for sfx in ("_t", "_d"):
                name = e + sfx
                self.sem[name] = es.enter_context(nc.semaphore(name))
                self.cnt[name] = 0
        self.seq = 0
        self.auto_drain = True
        self.fence_pending = {}
        self.fenced = {}

    def ins(self, eng, fn, inc=False, dma=False):
        tok = None
        if dma:
            name = eng + "_d"
            pend = self.fence_pending.get(name, 0)
            if pend > self.fenced.get(name, 0):
                self.q[eng].append(lambda E, name=name, v=pend: E.wait_ge(self.sem[name], v))
                self.fenced[name] = pend
            self.cnt[name] += 16
            tok = (name, self.cnt[name])
            def wrapped(E, fn=fn, name=name):
                fn(E).then_inc(self.sem[name], 16)
        elif inc:
            name = eng + "_t"
            self.cnt[name] += 1
            tok = (name, self.cnt[name])
            if eng in ("vector", "scalar", "gpsimd"):
                def wrapped(E, fn=fn, name=name):
                    fn(E)
                    E.drain().then_inc(self.sem[name], 1)
            else:
                def wrapped(E, fn=fn, name=name):
                    fn(E).then_inc(self.sem[name], 1)
        elif eng == "vector" and self.auto_drain:
            def wrapped(E, fn=fn):
                fn(E)
                E.drain()
        else:
            wrapped = fn
        self.q[eng].append(wrapped)
        self.seq += 1
        return tok

    def wait(self, eng, tok):
        if tok is None:
            return
        name, val = tok
        if name.endswith("_d"):
            val = self.cnt[name]
            self.fence_pending[name] = max(self.fence_pending.get(name, 0), val)
        self.q[eng].append(lambda E, name=name, val=val: E.wait_ge(self.sem[name], val))

    def waits(self, eng, toks):
        best = {}
        for t in toks:
            if t is None:
                continue
            n, v = t
            if n.endswith("_d"):
                v = self.cnt[n]
                self.fence_pending[n] = max(self.fence_pending.get(n, 0), v)
            best[n] = max(best.get(n, 0), v)
        for n, v in best.items():
            self.q[eng].append(lambda E, n=n, v=v: E.wait_ge(self.sem[n], v))

    def token(self, eng):
        if eng in ("vector", "scalar", "gpsimd"):
            name = eng + "_t"
            self.cnt[name] += 1
            tok = (name, self.cnt[name])
            self.q[eng].append(lambda E, name=name: E.drain().then_inc(self.sem[name], 1))
            self.seq += 1
            return tok
        return self.ins(eng, lambda E: E.nop(), inc=True)

    def barrier(self, engs=ENGS):
        toks = [self.token(e) for e in engs]
        for e in engs:
            self.waits(e, [t for t in toks])

    def finalize(self, block):
        for e in ENGS:
            fns = self.q[e]
            def runner(E, fns=fns):
                for f in fns:
                    f(E)
            getattr(block, e)(runner)


def bcast_ap(ap, inner):
    return bass.AP(tensor=ap.tensor, offset=ap.offset,
                   ap=[list(q) for q in ap.ap] + [[0, inner]])


def rep_ap(ap, n, step=0):
    a = [list(q) for q in ap.ap]
    return bass.AP(tensor=ap.tensor, offset=ap.offset, ap=[a[0]] + [[step, n]] + a[1:])



def make_consts():
    ident = np.eye(128, dtype=np.float32)
    ones128 = np.ones((128, 128), np.float32)
    sel4 = np.zeros((4, 512), np.float32)
    for h in range(4):
        sel4[h, h * 128:(h + 1) * 128] = 1.0
    r = np.arange(32)
    maskL32 = np.tile((r[:, None] > r[None, :]).astype(np.float32), (4, 1))
    maskU32 = np.tile((r[:, None] < r[None, :]).astype(np.float32), (4, 1))
    i32 = np.tile(np.eye(32, dtype=np.float32), (4, 1))
    a = np.arange(256)
    ca, cb = a[:, None] // 32, a[None, :] // 32
    strict = (ca < cb).astype(np.float32)
    diag = ((ca == cb) & (a[:, None] % 32 <= a[None, :] % 32)).astype(np.float32)
    return dict(ident=ident, ones128=ones128, sel4=sel4, maskl32=maskL32, masku32=maskU32,
                i32=i32, maskl32r=np.tile(maskL32, (1, 8)), masku32r=np.tile(maskU32, (1, 8)),
                i32r=np.tile(i32, (1, 8)),
                gstrict0=strict[:128], gstrict1=strict[128:],
                gdiag0=diag[:128], gdiag1=diag[128:])


def shard_inputs(I):
    x = np.asarray(I["L0_attn_hidden_states"], np.float32)
    consts = make_consts()
    maps = []
    for c in range(NCHIP):
        b, half = c // 2, c % 2
        x12 = np.concatenate([x[b, 0:2048], x[b, 2048 * half: 2048 * half + 2048]], 0)
        m = {
            "x12": x12,
            "wq": I["L0_attn_Wq"], "wk": I["L0_attn_Wk"], "wv": I["L0_attn_Wv"],
            "wb": I["L0_attn_Wb"],
            "convq": I["L0_attn_conv_q"], "convk": I["L0_attn_conv_k"],
            "convv": I["L0_attn_conv_v"],
            "fir_s": np.asarray(I["L0_attn_fir_short_w"], np.float32).reshape(1024, 3),
            "fir_l": np.asarray(I["L0_attn_fir_long_w"], np.float32).reshape(1024, 63),
            "fparam": np.asarray(I["L0_attn_forget_param"], np.float32).reshape(4, 1),
            "ltemp": np.asarray(I["L0_attn_log_temp"], np.float32).reshape(4, 1),
            "floorp": np.asarray(I["L0_attn_floor_param"], np.float32).reshape(1, 16),
            "w1": I["L0_attn_W1"],
            "b1": np.asarray(I["L0_attn_b1"], np.float32).reshape(1, 2048),
            "b1c": np.asarray(I["L0_attn_b1"], np.float32).reshape(16, 128).T,
            "w2": I["L0_attn_W2"],
            "b2": np.asarray(I["L0_attn_b2"], np.float32).reshape(1, 16),
            "normw_t": np.tile(np.asarray(I["L0_attn_norm_w"], np.float32).reshape(1, 256), (128, 4)),
            "b2t": np.tile(np.asarray(I["L0_attn_b2"], np.float32).reshape(1, 16), (128, 1)),
            "wo": I["L0_attn_Wo"],
            "wg": I["L0_ffn_w_gate"], "wu": I["L0_ffn_w_up"], "wd": I["L0_ffn_w_down"],
            "flagv": np.full((128, 1), float(half), np.float32),
        }
        m.update(consts)
        maps.append({k: np.ascontiguousarray(np.asarray(v, np.float32)) for k, v in m.items()})
    return maps



def build_nc(upto="ALL", dbg=(), skip=(), p3stop=99):
    nc = bass.Bass("TRN2", target_bir_lowering=False, debug=False)
    es = contextlib.ExitStack()
    P = Prog(nc, es)

    def dram(name, shape, kind="Internal"):
        k = "ExternalOutput" if (name in dbg and kind == "Internal") else kind
        return nc.dram_tensor(name, shape, F32, kind=k).ap()

    IN = {}
    for name, shape in [
        ("x12", [4096, 1024]), ("wq", [1024, 1024]), ("wk", [1024, 1024]),
        ("wv", [1024, 1024]), ("wb", [1024, 4]),
        ("convq", [1024, 4]), ("convk", [1024, 4]), ("convv", [1024, 4]),
        ("fir_s", [1024, 3]), ("fir_l", [1024, 63]),
        ("fparam", [4, 1]), ("ltemp", [4, 1]), ("floorp", [1, 16]),
        ("w1", [1088, 2048]), ("b1", [1, 2048]), ("b1c", [128, 16]), ("w2", [2048, 16]), ("b2", [1, 16]),
        ("normw_t", [128, 1024]), ("b2t", [128, 16]), ("wo", [1024, 1024]),
        ("wg", [2816, 1024]), ("wu", [2816, 1024]), ("wd", [1024, 2816]),
        ("flagv", [128, 1]), ("ident", [128, 128]), ("ones128", [128, 128]), ("sel4", [4, 512]),
        ("maskl32", [128, 32]), ("masku32", [128, 32]), ("i32", [128, 32]),
        ("maskl32r", [128, 256]), ("masku32r", [128, 256]), ("i32r", [128, 256]),
        ("gstrict0", [128, 256]), ("gstrict1", [128, 256]),
        ("gdiag0", [128, 256]), ("gdiag1", [128, 256]),
    ]:
        IN[name] = dram(name, shape, "ExternalInput")

    out_h = dram("out", [2048, 1024], "ExternalOutput")

    khat_s = dram("khat_s", [64, 128, 512])
    v_s = dram("v_s", [64, 128, 512])
    qhat_s = dram("qhat_s", [32, 128, 512])
    xnT_s = dram("xnT_s", [32, 128, 512])
    xqT_s = dram("xqT_s", [32, 128, 512])
    beta_s = dram("beta_s", [8, 4, 512])
    xn_dbg = dram("xn_dbg", [128, 4096])
    ln_dbg = dram("ln_dbg", [128, 16])
    xtm_dbg = dram("xtm_dbg", [128, 1024])
    xnT_dbg = dram("xnT_dbg", [128, 4096])
    delta_s = dram("delta_s", [2048, 1024])
    y1_s = dram("y1_s", [2048, 1024])
    wgT_s = dram("wgT_s", [8, 128, 2816])
    wuT_s = dram("wuT_s", [8, 128, 2816])
    wdT_s = dram("wdT_s", [22, 128, 1024])
    hqT_s = dram("hqT_s", [22, 128, 2048])

    sb = lambda name, p, f, dt=F32: es.enter_context(nc.sbuf_tensor(name, [p, f], dt))

    ident = sb("identt", 128, 128)
    ones128 = sb("ones128t", 128, 128)
    sel4 = sb("sel4t", 4, 512)
    flag = sb("flagt", 128, 1)
    maskl32 = sb("maskl32t", 128, 32)
    masku32 = sb("masku32t", 128, 32)
    i32 = sb("i32t", 128, 32)
    maskl32r = sb("maskl32rt", 128, 256)
    masku32r = sb("masku32rt", 128, 256)
    i32r = sb("i32rt", 128, 256)
    convw = {t: sb("convw" + t, 128, 8 * 4) for t in "qkv"}
    firw_s = sb("firw_s", 128, 8 * 3)
    firw_l = sb("firw_l", 128, 8 * 63)
    lam_t = sb("lam", 4, 64)
    lamR = sb("lamR", 4, 512)
    rampPT = sb("rampPT", 128, 8)
    rampNT = sb("rampNT", 128, 8)
    lamcol = sb("lamcol", 128, 16)
    beta_f = sb("beta_f", 4, 512)
    wbt = sb("wbt", 128, 8 * 4)
    misc = sb("misct", 128, 512)
    row1 = sb("row1", 1, 2048)

    psum = lambda name, p, f: es.enter_context(nc.psum_tensor(name, [p, f], F32))

    block = es.enter_context(nc.Block())
    for _ps in ("p5at0", "p5at1", "w2s"):
        P.sem[_ps] = es.enter_context(nc.semaphore(_ps))
        P.cnt[_ps] = 0

    def ld(dst, src, eng="sync"):
        return P.ins(eng, lambda E, d=dst, s=src: E.dma_start(out=d, in_=s), dma=True)

    def st(dst, src, eng="gpsimd"):
        return P.ins(eng, lambda E, d=dst, s=src: E.dma_start(out=d, in_=s), dma=True)

    tl = []
    for n, t in [("ident", ident), ("ones128", ones128), ("sel4", sel4), ("flagv", flag),
                 ("maskl32", maskl32), ("masku32", masku32), ("i32", i32),
                 ("maskl32r", maskl32r), ("masku32r", masku32r), ("i32r", i32r)]:
        tl.append(ld(t[:], IN[n]))
    for t in "qkv":
        for ci in range(8):
            tl.append(ld(convw[t][:, ci * 4:(ci + 1) * 4], IN["conv" + t][ci * 128:(ci + 1) * 128, :]))
    for ci in range(8):
        tl.append(ld(wbt[:, ci * 4:(ci + 1) * 4], IN["wb"][ci * 128:(ci + 1) * 128, :]))
        tl.append(ld(firw_s[:, ci * 3:(ci + 1) * 3], IN["fir_s"][ci * 128:(ci + 1) * 128, :]))
        tl.append(ld(firw_l[:, ci * 63:(ci + 1) * 63], IN["fir_l"][ci * 128:(ci + 1) * 128, :]))
    tl.append(ld(lam_t[0:4, 0:1], IN["fparam"]))
    gd_t = [sb(f"gdt{i}", 128, 256) for i in range(2)]
    for i in range(2):
        tl.append(ld(gd_t[i][:], IN["gdiag%d" % i]))

    P.waits("scalar", tl)
    P.ins("scalar", lambda E: E.activation(lam_t[0:4, 1:2], lam_t[0:4, 0:1], AF.Sigmoid))
    ts_ = P.token("scalar")
    P.waits("vector", tl + [ts_])
    V = lambda fn, inc=False: P.ins("vector", fn, inc=inc)
    V(lambda E: E.tensor_scalar(lam_t[0:4, 2:3], lam_t[0:4, 1:2], 0.45, 0.55, op0=OP.mult, op1=OP.add))
    V(lambda E: E.memset(lam_t[0:4, 8:9], 1.0))
    V(lambda E: E.drain())
    for j in range(1, 9):
        V(lambda E, j=j: E.tensor_tensor(lam_t[0:4, 8 + j:9 + j], lam_t[0:4, 7 + j:8 + j], lam_t[0:4, 2:3], op=OP.mult))
    V(lambda E: E.reciprocal(lam_t[0:4, 3:4], lam_t[0:4, 2:3]))
    V(lambda E: E.drain())
    V(lambda E: E.memset(lam_t[0:4, 24:25], 1.0))
    V(lambda E: E.drain())
    for l in range(1, 8):
        V(lambda E, l=l: E.tensor_tensor(lam_t[0:4, 24 + l:25 + l], lam_t[0:4, 23 + l:24 + l], lam_t[0:4, 3:4], op=OP.mult))
    V(lambda E: E.tensor_scalar(lam_t[0:4, 5:6], lam_t[0:4, 3:4], -1.0, None, op0=OP.mult))
    for j in range(8):
        V(lambda E, j=j: E.tensor_copy(lamR[0:4, 32 * j:32 * j + 32], bcast_ap(lam_t[0:4, 8 + j:9 + j], 32)))
        V(lambda E, j=j: E.tensor_copy(lamR[0:4, 256 + 32 * j:256 + 32 * j + 32], bcast_ap(lam_t[0:4, 24 + j:25 + j], 32)))
    tlam = P.token("vector")

    psB0 = psum("psB0", 128, 512)
    psB1 = psum("psB1", 128, 512)
    P.wait("tensor", tlam)
    TT = lambda fn, inc=False: P.ins("tensor", fn, inc=inc)
    for h in range(H):
        sl = sel4[:, h * 128:(h + 1) * 128]
        TT(lambda E, sl=sl: E.matmul(psB1[:, 0:1], sl, lam_t[0:4, 5:6], start=True, stop=True))
        TT(lambda E, sl=sl: E.matmul(psB1[:, 1:2], sl, lam_t[0:4, 2:3], start=True, stop=True))
        TT(lambda E, sl=sl: E.matmul(psB1[:, 2:3], sl, lam_t[0:4, 15:16], start=True, stop=True))
        TT(lambda E, sl=sl: E.matmul(psB1[:, 3:4], sl, lam_t[0:4, 3:4], start=True, stop=True), inc=True)
        tmm = ("tensor_t", P.cnt["tensor_t"])
        P.wait("vector", tmm)
        V(lambda E, h=h: E.tensor_copy(lamcol[:, 4 * h:4 * h + 4], psB1[:, 0:4]), inc=True)
        tev = ("vector_t", P.cnt["vector_t"])
        P.wait("tensor", tev)
    tlam2 = P.token("vector")
    P.wait("tensor", tlam2)
    for tj in range(2):
        TT(lambda E, tj=tj: E.transpose(psB0[:, 0:4], lamR[0:4, tj * 128:tj * 128 + 128], ident[0:4, 0:4]))
        TT(lambda E, tj=tj: E.transpose(psB0[:, 4:8], lamR[0:4, 256 + tj * 128:256 + tj * 128 + 128], ident[0:4, 0:4]), inc=True)
        tmm = ("tensor_t", P.cnt["tensor_t"])
        P.wait("vector", tmm)
        V(lambda E, tj=tj: E.tensor_copy(rampPT[:, tj * 4:tj * 4 + 4], psB0[:, 0:4]))
        V(lambda E, tj=tj: E.tensor_copy(rampNT[:, tj * 4:tj * 4 + 4], psB0[:, 4:8]), inc=True)
        P.wait("tensor", ("vector_t", P.cnt["vector_t"]))

    P.barrier()
    if upto == "P0":
        P.finalize(block)
        es.close()
        return nc

    es2 = contextlib.ExitStack()
    sb2 = lambda name, p, f, dt=F32: es2.enter_context(nc.sbuf_tensor(name, [p, f], dt))

    wk_t = sb2("wk_t", 128, 8192)
    wv_t = sb2("wv_t", 128, 8192)
    wq_t = sb2("wq_t", 128, 8192)
    twl = []
    for name, t in [("wk", wk_t), ("wv", wv_t), ("wq", wq_t)]:
        for ki in range(8):
            twl.append(ld(t[:, ki * 1024:(ki + 1) * 1024], IN[name][ki * 128:(ki + 1) * 128, :]))

    x_tm = sb2("x_tm", 128, 1024)
    xn_tm = sb2("xn_tm", 128, 4 * 1024)
    xq_tm = sb2("xq_tm", 128, 1024)
    xnT = sb2("xnT", 128, 8 * 512)
    xqT = sb2("xqT", 128, 8 * 512)
    pslice = sb2("pslice", 128, 515)
    tails = {t: sb2("tails" + t, 128, 24) for t in "qkv"}
    kbuf = sb2("kbuf", 128, 1024)
    qbuf = sb2("qbuf", 128, 1024)
    vbuf = sb2("vbuf", 128, 512)
    sqr = sb2("sqr", 128, 1024)
    lncol = sb2("lncol", 128, 16)
    qrow = sb2("qrow", 128, 64)
    uq = sb2("uq", 128, 1024)
    gq = sb2("gq", 128, 1024)
    sgq = sb2("sgq", 128, 1024)

    psP = [psB0, psB1]
    psT = es2.enter_context(nc.psum_tensor("psT", [128, 512], F32))
    psT2 = es2.enter_context(nc.psum_tensor("psT2", [128, 512], F32))
    psT3 = es2.enter_context(nc.psum_tensor("psT3", [128, 512], F32))
    p2rot = {"g": 0, "ev": {}}
    psR = es2.enter_context(nc.psum_tensor("psR", [4, 512], F32))
    psL = es2.enter_context(nc.psum_tensor("psL", [1, 512], F32))

    V(lambda E: E.memset(tails["q"][:], 0.0))

    V(lambda E: E.drain())
    V(lambda E: E.memset(tails["k"][:], 0.0))
    V(lambda E: E.drain())
    V(lambda E: E.memset(tails["v"][:], 0.0))
    V(lambda E: E.drain())
    tz = P.token("vector")
    P.waits("tensor", twl + [tz])
    P.waits("scalar", twl + [tz])

    xtm_free = [None]
    evtok = [None, None]
    for pp in range(NPP):
        own = pp >= 4
        for s in range(4):
            P.wait("sync", xtm_free[0])
            tx = ld(x_tm[:], IN["x12"][pp * 512 + s * 128: pp * 512 + (s + 1) * 128, :])
            P.wait("vector", tx)
            V(lambda E: E.tensor_reduce(lncol[:, 0:1], x_tm[:], axis=AXX, op=OP.add))
            V(lambda E: E.tensor_tensor(sqr[:], x_tm[:], x_tm[:], op=OP.mult))
            V(lambda E: E.drain())
            V(lambda E: E.tensor_reduce(lncol[:, 1:2], sqr[:], axis=AXX, op=OP.add))
            V(lambda E: E.tensor_scalar(lncol[:, 2:3], lncol[:, 0:1], 1.0 / 1024, None, op0=OP.mult))
            V(lambda E: E.tensor_scalar(lncol[:, 3:4], lncol[:, 1:2], 1.0 / 1024, None, op0=OP.mult))
            V(lambda E: E.tensor_tensor(lncol[:, 4:5], lncol[:, 2:3], lncol[:, 2:3], op=OP.mult))
            V(lambda E: E.tensor_tensor(lncol[:, 5:6], lncol[:, 3:4], lncol[:, 4:5], op=OP.subtract))
            V(lambda E: E.tensor_scalar(lncol[:, 5:6], lncol[:, 5:6], 1e-5, None, op0=OP.add), inc=True)
            P.wait("scalar", ("vector_t", P.cnt["vector_t"]))
            P.ins("scalar", lambda E: E.activation(lncol[:, 6:7], lncol[:, 5:6], AF.Sqrt), inc=True)
            P.wait("vector", ("scalar_t", P.cnt["scalar_t"]))
            V(lambda E: E.reciprocal(lncol[:, 7:8], lncol[:, 6:7]))
            V(lambda E: E.drain())
            V(lambda E: E.tensor_tensor(lncol[:, 8:9], lncol[:, 2:3], lncol[:, 7:8], op=OP.mult))
            V(lambda E: E.tensor_scalar(lncol[:, 9:10], lncol[:, 8:9], -1.0, None, op0=OP.mult))
            V(lambda E, s=s: E.tensor_scalar(
                xn_tm[:, s * 1024:(s + 1) * 1024], x_tm[:], lncol[:, 7:8], None, op0=OP.mult))
            V(lambda E, s=s: E.tensor_scalar(
                xn_tm[:, s * 1024:(s + 1) * 1024], xn_tm[:, s * 1024:(s + 1) * 1024],
                lncol[:, 9:10], None, op0=OP.add), inc=True)
            tln = ("vector_t", P.cnt["vector_t"])
            xtm_free[0] = tln
            if own and "quant" not in skip:
                xnt = xn_tm[:, s * 1024:(s + 1) * 1024]
                xv = xnt.rearrange("p (nb b) -> p nb b", b=16)
                V(lambda E: E.drain())
                V(lambda E, xv=xv: E.tensor_reduce(qrow[:], xv, axis=AXX, op=OP.max, apply_absolute_value=True))
                V(lambda E: E.tensor_scalar(qrow[:], qrow[:], 1.0 / 6, 1e-12, op0=OP.mult, op1=OP.max))
                V(lambda E: E.reciprocal(misc[:, 0:64], qrow[:]))
                V(lambda E: E.drain())
                rb = bcast_ap(misc[:, 0:64], 16)
                V(lambda E, xv=xv, rb=rb: E.tensor_tensor(uq[:].rearrange("p (nb b) -> p nb b", b=16), xv, rb, op=OP.mult))
                V(lambda E: E.tensor_scalar(sgq[:], uq[:], 0.0, None, op0=OP.is_gt))
                V(lambda E: E.tensor_scalar(gq[:], uq[:], 0.0, None, op0=OP.is_lt))
                V(lambda E: E.tensor_tensor(sgq[:], sgq[:], gq[:], op=OP.subtract))
                V(lambda E: E.tensor_tensor(uq[:], uq[:], sgq[:], op=OP.mult))
                V(lambda E: E.tensor_scalar(gq[:], uq[:], 0.25, None, op0=OP.is_gt))
                for thr in (0.75, 1.25, 1.75):
                    V(lambda E, thr=thr: E.scalar_tensor_tensor(gq[:], uq[:], thr, gq[:], op0=OP.is_gt, op1=OP.add))
                V(lambda E: E.tensor_scalar(gq[:], gq[:], 0.5, None, op0=OP.mult))
                for thr in (2.5, 3.5, 5.0, 5.0):
                    V(lambda E, thr=thr: E.scalar_tensor_tensor(gq[:], uq[:], thr, gq[:], op0=OP.is_gt, op1=OP.add))
                V(lambda E: E.tensor_tensor(gq[:], gq[:], sgq[:], op=OP.mult))
                qb = bcast_ap(qrow[:], 16)
                V(lambda E, qb=qb: E.tensor_tensor(
                    xq_tm[:].rearrange("p (nb b) -> p nb b", b=16),
                    gq[:].rearrange("p (nb b) -> p nb b", b=16), qb, op=OP.mult), inc=True)
                tln = ("vector_t", P.cnt["vector_t"])
            P.wait("tensor", tln)
            if "trans" in skip:
                continue
            srcs = [(xn_tm[:, s * 1024:(s + 1) * 1024], xnT)] + ([(xq_tm[:], xqT)] if own else [])
            for srct, dstt in srcs:
                for cg in range(2):
                    g = p2rot["g"]
                    p2rot["g"] += 1
                    bank = [psT2, psT3][g % 2]
                    P.wait("tensor", p2rot["ev"].get(g - 2))
                    sname = "p5at%d" % (g % 2)
                    for cj in range(4):
                        ci = cg * 4 + cj
                        if cj == 3:
                            P.cnt[sname] += 1
                            P.ins("tensor", lambda E, ci=ci, cj=cj, srct=srct, bank=bank, sname=sname: E.transpose(
                                bank[:, cj * 128:(cj + 1) * 128],
                                srct[:, ci * 128:(ci + 1) * 128], ident[:]).then_inc(P.sem[sname], 1))
                        else:
                            TT(lambda E, ci=ci, cj=cj, srct=srct, bank=bank: E.transpose(
                                bank[:, cj * 128:(cj + 1) * 128],
                                srct[:, ci * 128:(ci + 1) * 128], ident[:]))
                    P.wait("scalar", (sname, P.cnt[sname]))
                    for cj in range(4):
                        ci = cg * 4 + cj
                        P.ins("scalar", lambda E, s=s, ci=ci, cj=cj, dstt=dstt, bank=bank: E.copy(
                            dstt[:, ci * 512 + s * 128: ci * 512 + (s + 1) * 128],
                            bank[:, cj * 128:(cj + 1) * 128]), inc=(cj == 3))
                    p2rot["ev"][g] = ("scalar_t", P.cnt["scalar_t"])
        ttr = P.token("scalar")
        if own:
            P.wait("gpsimd", ttr)
            for ci in range(8):
                st(xnT_s[(pp - 4) * 8 + ci], xnT[:, ci * 512:(ci + 1) * 512])
                st(xqT_s[(pp - 4) * 8 + ci], xqT[:, ci * 512:(ci + 1) * 512])
        if pp == 4 and "xn_dbg" in dbg:
            P.wait("gpsimd", ttr)
            P.wait("gpsimd", P.token("vector"))
            st(xn_dbg, xn_tm[:])
            st(xnT_dbg, xnT[:])
            st(ln_dbg, lncol[:])
            st(xtm_dbg, x_tm[:])
            P.wait("vector", ("gpsimd_d", P.cnt["gpsimd_d"]))
            P.wait("scalar", ("gpsimd_d", P.cnt["gpsimd_d"]))
            P.wait("sync", ("gpsimd_d", P.cnt["gpsimd_d"]))

        if "rest" in skip:
            continue
        P.wait("tensor", ttr)
        for ki in range(8):
            TT(lambda E, ki=ki: E.matmul(psR[:], wbt[:, ki * 4:(ki + 1) * 4],
                                         xnT[:, ki * 512:(ki + 1) * 512],
                                         start=(ki == 0), stop=(ki == 7)), inc=(ki == 7))
        P.wait("scalar", ("tensor_t", P.cnt["tensor_t"]))
        P.ins("scalar", lambda E: E.activation(beta_f[0:4, :], psR[:], AF.Sigmoid), inc=True)
        tbeta = ("scalar_t", P.cnt["scalar_t"])
        P.wait("gpsimd", tbeta)
        st(beta_s[pp], beta_f[0:4, :])
        P.wait("tensor", tbeta)
        P.wait("scalar", ("gpsimd_d", P.cnt["gpsimd_d"]))

        if pp == 3:
            P.wait("tensor", evtok[0])
            for mi in range(8):
                for ki in range(8):
                    TT(lambda E, ki=ki, mi=mi: E.matmul(
                        psP[0][:, 0:3], wq_t[:, ki * 1024 + mi * 128: ki * 1024 + (mi + 1) * 128],
                        xnT[:, ki * 512 + 509: ki * 512 + 512],
                        start=(ki == 0), stop=(ki == 7)), inc=(ki == 7))
                P.wait("vector", ("tensor_t", P.cnt["tensor_t"]))
                V(lambda E, mi=mi: E.tensor_copy(tails["q"][:, mi * 3:(mi + 1) * 3], psP[0][:, 0:3]), inc=True)
                P.wait("tensor", ("vector_t", P.cnt["vector_t"]))

        if pp == 4:
            for tt in "qkv":
                V(lambda E, tt=tt: E.tensor_scalar(tails[tt][:], tails[tt][:], flag[:, 0:1], None, op0=OP.mult))
            tfl = P.token("vector")
            P.wait("tensor", tfl)

        tensors = ["k", "v"] + (["q"] if own else [])
        for t in tensors:
            wt = {"q": wq_t, "k": wk_t, "v": wv_t}[t]
            buf = {"q": qbuf, "k": kbuf, "v": vbuf}[t]
            for mi in range(8):
                pb = mi % 2
                P.wait("tensor", evtok[pb])
                for ki in range(8):
                    TT(lambda E, wt=wt, ki=ki, mi=mi, pb=pb: E.matmul(
                        psP[pb][:], wt[:, ki * 1024 + mi * 128: ki * 1024 + (mi + 1) * 128],
                        xnT[:, ki * 512:(ki + 1) * 512],
                        start=(ki == 0), stop=(ki == 7)), inc=(ki == 7))
                tmm = ("tensor_t", P.cnt["tensor_t"])
                P.wait("vector", tmm)
                V(lambda E, t=t, mi=mi: E.tensor_copy(pslice[:, 0:3], tails[t][:, mi * 3:(mi + 1) * 3]))
                V(lambda E, pb=pb: E.tensor_copy(pslice[:, 3:515], psP[pb][:]), inc=True)
                evtok[pb] = ("vector_t", P.cnt["vector_t"])
                V(lambda E, t=t, mi=mi: E.tensor_copy(tails[t][:, mi * 3:(mi + 1) * 3], pslice[:, 512:515]))
                cw = convw[t]
                bo = (mi % 2) * 512 if t in "kq" else 0
                V(lambda E, buf=buf, bo=bo, cw=cw, mi=mi: E.tensor_scalar(
                    buf[:, bo:bo + 512], pslice[:, 0:512],
                    cw[:, mi * 4 + 0: mi * 4 + 1], None, op0=OP.mult))
                for j in (1, 2, 3):
                    V(lambda E, buf=buf, bo=bo, cw=cw, mi=mi, j=j: E.scalar_tensor_tensor(
                        buf[:, bo:bo + 512], pslice[:, j: 512 + j],
                        cw[:, mi * 4 + j: mi * 4 + j + 1], buf[:, bo:bo + 512],
                        op0=OP.mult, op1=OP.add), inc=(j == 3))
                P.wait("scalar", ("vector_t", P.cnt["vector_t"]))
                P.ins("scalar", lambda E, buf=buf, bo=bo: E.activation(
                    buf[:, bo:bo + 512], buf[:, bo:bo + 512], AF.Silu), inc=True)
                tsil = ("scalar_t", P.cnt["scalar_t"])
                if t == "v":
                    P.wait("gpsimd", tsil)
                    st(v_s[pp * 8 + mi], vbuf[:, 0:512])
                    P.wait("vector", ("gpsimd_d", P.cnt["gpsimd_d"]))
                    P.wait("scalar", ("gpsimd_d", P.cnt["gpsimd_d"]))
                elif mi % 2 == 1:
                    h = mi // 2
                    P.wait("vector", tsil)
                    for c2 in range(2):
                        V(lambda E, buf=buf, c2=c2: E.tensor_tensor(
                            sqr[:, c2 * 512:(c2 + 1) * 512], buf[:, c2 * 512:(c2 + 1) * 512],
                            buf[:, c2 * 512:(c2 + 1) * 512], op=OP.mult), inc=(c2 == 1))
                    P.wait("tensor", ("vector_t", P.cnt["vector_t"]))
                    for c2 in range(2):
                        TT(lambda E, c2=c2: E.matmul(psL[0:1, :], ones128[:, 0:1],
                                                     sqr[:, c2 * 512:(c2 + 1) * 512],
                                                     start=(c2 == 0), stop=(c2 == 1)), inc=(c2 == 1))
                    P.wait("vector", ("tensor_t", P.cnt["tensor_t"]))
                    V(lambda E: E.tensor_scalar(row1[0:1, 0:512], psL[0:1, :], 1e-6, None, op0=OP.add), inc=True)
                    P.wait("scalar", ("vector_t", P.cnt["vector_t"]))
                    P.ins("scalar", lambda E: E.activation(row1[0:1, 0:512], row1[0:1, 0:512], AF.Sqrt), inc=True)
                    P.wait("vector", ("scalar_t", P.cnt["scalar_t"]))
                    V(lambda E: E.reciprocal(row1[0:1, 0:512], row1[0:1, 0:512]))
                    V(lambda E: E.drain(), inc=True)
                    P.wait("tensor", ("vector_t", P.cnt["vector_t"]))
                    TT(lambda E: E.matmul(psT[:], ones128[0:1, :], row1[0:1, 0:512],
                                          start=True, stop=True), inc=True)
                    P.wait("vector", ("tensor_t", P.cnt["tensor_t"]))
                    for c2 in range(2):
                        V(lambda E, buf=buf, c2=c2: E.tensor_tensor(
                            buf[:, c2 * 512:(c2 + 1) * 512], buf[:, c2 * 512:(c2 + 1) * 512],
                            psT[:], op=OP.mult), inc=(c2 == 1))
                    tl2 = ("vector_t", P.cnt["vector_t"])
                    P.wait("tensor", tl2)
                    P.wait("gpsimd", tl2)
                    dst = khat_s if t == "k" else qhat_s
                    base = pp * 8 if t == "k" else (pp - 4) * 8
                    st(dst[base + mi - 1], buf[:, 0:512])
                    st(dst[base + mi], buf[:, 512:1024])
                    P.wait("vector", ("gpsimd_d", P.cnt["gpsimd_d"]))
                    P.wait("scalar", ("gpsimd_d", P.cnt["gpsimd_d"]))

    P.barrier()
    es2.close()
    if upto == "P2":
        P.finalize(block)
        es.close()
        return nc

    es3 = contextlib.ExitStack()
    sb3 = lambda name, p, f, dt=F32: es3.enter_context(nc.sbuf_tensor(name, [p, f], dt))
    ps3 = lambda name, p, f: es3.enter_context(nc.psum_tensor(name, [p, f], F32))

    ghmask = [[sb3(f"ghm{h}_{i}", 128, 256) for i in range(2)] for h in range(H)]
    lamPb = [sb3(f"lamPb{h}", 128, 256) for h in range(H)]
    lamNb = [sb3(f"lamNb{h}", 128, 256) for h in range(H)]
    kw = sb3("kw", 128, 8 * 256)
    vw = sb3("vw", 128, 8 * 256)
    qw = sb3("qw", 128, 8 * 256)
    kbf = sb3("kbf", 128, 8 * 256)
    knf = sb3("knf", 128, 8 * 256)
    qtf = sb3("qtf", 128, 8 * 256)
    kT = sb3("kT", 128, 2048)
    vT = sb3("vT", 128, 2048)
    kbT = sb3("kbT", 128, 2048)
    ktT = sb3("ktT", 128, 2048)
    vbT = sb3("vbT", 128, 2048)
    bw = sb3("bw", 4, 256)
    bT = sb3("bT", 128, 8)
    colsc = sb3("colsc", 128, 8)
    Lst = sb3("Lst", 128, 256)
    Ust = sb3("Ust", 128, 256)
    Dst = sb3("Dst", 128, 256)
    Lp = sb3("Lp", 128, 256)
    Up = sb3("Up", 128, 256)
    DfinPad = sb3("DfinPad", 128, 1024)
    ut = sb3("ut", 128, 2048)
    wf = sb3("wf", 128, 2048)
    WSt = sb3("WSt", 128, 2048)
    QSt = sb3("QSt", 128, 2048)
    PTt = sb3("PTt", 128, 512)
    GTt = sb3("GTt", 128, 512)
    upb = sb3("upb", 128, 2048)
    S_t = sb3("S_t", 128, 2048)
    osb = sb3("osb", 128, 256)
    psW0 = ps3("psW0", 128, 512)
    psW1 = ps3("psW1", 128, 512)
    psW2 = ps3("psW2", 128, 512)
    psW3 = ps3("psW3", 128, 1024)

    V(lambda E: E.memset(S_t[:], 0.0))
    V(lambda E: E.drain())
    for h in range(H):
        sl = sel4[:, h * 128:(h + 1) * 128]
        TT(lambda E, sl=sl: E.matmul(psW0[:, 0:256], sl, lamR[0:4, 0:256], start=True, stop=True))
        TT(lambda E, sl=sl: E.matmul(psW0[:, 256:512], sl, lamR[0:4, 256:512], start=True, stop=True), inc=True)
        P.wait("vector", ("tensor_t", P.cnt["tensor_t"]))
        V(lambda E, h=h: E.tensor_copy(lamPb[h][:], psW0[:, 0:256]))
        V(lambda E, h=h: E.tensor_copy(lamNb[h][:], psW0[:, 256:512]), inc=True)
        P.wait("tensor", ("vector_t", P.cnt["vector_t"]))
    tgl = []
    for h in range(H):
        for i in range(2):
            tgl.append(ld(ghmask[h][i][:], IN["gstrict%d" % i]))
    P.waits("vector", tgl)
    for h in range(H):
        for i in range(2):
            V(lambda E, h=h, i=i: E.scalar_tensor_tensor(
                ghmask[h][i][:], ghmask[h][i][:], lamcol[:, 4 * h + 3:4 * h + 4], gd_t[i][:],
                op0=OP.mult, op1=OP.add))
    tpre = P.token("vector")
    P.waits("tensor", [tpre])
    P.waits("sync", [tpre])

    nwin_run = 1 if upto == "P3w1" else NWIN
    wqbd = [sb3("wqbA", 128, 512), sb3("wqbB", 128, 512), sb3("wqbC", 128, 512)]
    uqad = [sb3("uqaA", 128, 512), sb3("uqaB", 128, 512)]
    gqa = sb3("gqa", 128, 512)
    sga = sb3("sga", 128, 512)
    qra = sb3("qra", 128, 32)
    rca = sb3("rca", 128, 32)
    psA5 = ps3("psA5", 128, 512)

    def emit_quant_a(buf, K):
        nb = K // 16
        bv = buf.rearrange("p (nb b) -> p nb b", b=16)
        V(lambda E: E.drain())
        V(lambda E, bv=bv, nb=nb: E.tensor_reduce(qra[:, 0:nb], bv, axis=AXX, op=OP.max, apply_absolute_value=True))
        V(lambda E, nb=nb: E.tensor_scalar(qra[:, 0:nb], qra[:, 0:nb], 1.0 / 6, 1e-12, op0=OP.mult, op1=OP.max))
        V(lambda E, nb=nb: E.reciprocal(rca[:, 0:nb], qra[:, 0:nb]))
        V(lambda E: E.drain())
        rb = bcast_ap(rca[:, 0:nb], 16)
        V(lambda E, buf=buf, bv=bv, rb=rb, K=K: E.tensor_tensor(buf.rearrange("p (nb b) -> p nb b", b=16), bv, rb, op=OP.mult))
        V(lambda E, buf=buf, K=K: E.tensor_scalar(sga[:, 0:K], buf[:, 0:K], 0.0, None, op0=OP.is_gt))
        V(lambda E, buf=buf, K=K: E.tensor_scalar(gqa[:, 0:K], buf[:, 0:K], 0.0, None, op0=OP.is_lt))
        V(lambda E, K=K: E.tensor_tensor(sga[:, 0:K], sga[:, 0:K], gqa[:, 0:K], op=OP.subtract))
        V(lambda E, buf=buf, K=K: E.tensor_tensor(buf[:, 0:K], buf[:, 0:K], sga[:, 0:K], op=OP.mult))
        V(lambda E, buf=buf, K=K: E.tensor_scalar(gqa[:, 0:K], buf[:, 0:K], 0.25, None, op0=OP.is_gt))
        for thr in (0.75, 1.25, 1.75):
            V(lambda E, buf=buf, K=K, thr=thr: E.scalar_tensor_tensor(gqa[:, 0:K], buf[:, 0:K], thr, gqa[:, 0:K], op0=OP.is_gt, op1=OP.add))
        V(lambda E, K=K: E.tensor_scalar(gqa[:, 0:K], gqa[:, 0:K], 0.5, None, op0=OP.mult))
        for thr in (2.5, 3.5, 5.0, 5.0):
            V(lambda E, buf=buf, K=K, thr=thr: E.scalar_tensor_tensor(gqa[:, 0:K], buf[:, 0:K], thr, gqa[:, 0:K], op0=OP.is_gt, op1=OP.add))
        V(lambda E, K=K: E.tensor_tensor(gqa[:, 0:K], gqa[:, 0:K], sga[:, 0:K], op=OP.mult))
        qb = bcast_ap(qra[:, 0:nb], 16)
        V(lambda E, buf=buf, K=K, qb=qb: E.tensor_tensor(
            buf.rearrange("p (nb b) -> p nb b", b=16),
            gqa[:, 0:K].rearrange("p (nb b) -> p nb b", b=16), qb, op=OP.mult), inc=True)
        return ("vector_t", P.cnt["vector_t"])

    p5a_jobs = []
    for (wname, wtgt, nrc, KT) in (("wg", wgT_s, 22, 1024), ("wu", wuT_s, 22, 1024), ("wd", wdT_s, 8, 2816)):
        for rc in range(nrc):
            c0 = 0
            while c0 < KT:
                K = min(512, KT - c0)
                p5a_jobs.append((wname, wtgt, rc, c0, K))
                c0 += K
    NJOB = len(p5a_jobs)
    p5c = {"ld": 0, "q": 0, "tr": 0}
    t_ld = {}
    t_q = {}
    t_tr = {}
    t_ev = {}
    t_sp = {}

    def p5a_slot(drain=False):
        if p5c["tr"] < p5c["q"] - (0 if drain else 1):
            n = p5c["tr"]
            (wname, wtgt, rc, c0, K) = p5a_jobs[n]
            P.wait("tensor", t_q[n])
            P.wait("tensor", t_ev.get(n - 1))
            sname = "p5at%d" % (n % 2)
            for kc in range(K // 128):
                last = kc == K // 128 - 1
                if last:
                    P.cnt[sname] += 1
                    P.ins("tensor", lambda E, kc=kc, n=n, sname=sname: E.transpose(
                        psA5[:, kc * 128:(kc + 1) * 128],
                        wqbd[n % 3][:, kc * 128:(kc + 1) * 128], ident[:]).then_inc(P.sem[sname], 1))
                else:
                    TT(lambda E, kc=kc, n=n: E.transpose(psA5[:, kc * 128:(kc + 1) * 128],
                                                         wqbd[n % 3][:, kc * 128:(kc + 1) * 128], ident[:]))
            t_tr[n] = (sname, P.cnt[sname])
            P.wait("scalar", t_tr[n])
            P.wait("scalar", t_sp.get(n - 2))
            for kc in range(K // 128):
                P.ins("scalar", lambda E, kc=kc, n=n: E.copy(
                    uqad[n % 2][:, kc * 128:(kc + 1) * 128], psA5[:, kc * 128:(kc + 1) * 128]),
                    inc=(kc == K // 128 - 1))
            t_ev[n] = ("scalar_t", P.cnt["scalar_t"])
            for kc in range(K // 128):
                t_sp[n] = st(wtgt[c0 // 128 + kc][:, rc * 128:(rc + 1) * 128],
                             uqad[n % 2][:, kc * 128:(kc + 1) * 128], eng="scalar")
            p5c["tr"] = n + 1
        if p5c["ld"] < NJOB and p5c["ld"] - p5c["tr"] < 3:
            m = p5c["ld"]
            (wname, wtgt, rc, c0, K) = p5a_jobs[m]
            t_ld[m] = ld(wqbd[m % 3][:, 0:K], IN[wname][rc * 128:(rc + 1) * 128, c0:c0 + K], eng="scalar")
            p5c["ld"] = m + 1
        if p5c["q"] < p5c["ld"]:
            p = p5c["q"]
            (wname, wtgt, rc, c0, K) = p5a_jobs[p]
            P.wait("vector", t_ld[p])
            t_q[p] = emit_quant_a(wqbd[p % 3][:, 0:K], K)
            p5c["q"] = p + 1

    rot_banks = [psB0, psB1, psW0]
    rot = {"i": 0, "ev": [None, None, None]}

    def rot_acquire():
        i = rot["i"]
        rot["i"] += 1
        P.wait("tensor", rot["ev"][i % 3])
        return rot_banks[i % 3]

    def rot_release(tok):
        rot["ev"][(rot["i"] - 1) % 3] = tok

    def rot_fence():
        P.waits("tensor", [t for t in rot["ev"] if t is not None])

    rot["ev"] = [tpre, tpre, tpre]
    win_free = [None]
    for wi in range(nwin_run):
        own = wi >= NWIN // 2
        p4 = wi // 2
        off = (wi % 2) * 256
        W0 = wi * 256

        if wi == NWIN // 2:
            V(lambda E: E.tensor_scalar(S_t[:], S_t[:], flag[:, 0:1], None, op0=OP.mult), inc=True)
            P.wait("tensor", ("vector_t", P.cnt["vector_t"]))

        P.wait("sync", win_free[0])
        tload = []
        for ci in range(8):
            tload.append(ld(kw[:, ci * 256:(ci + 1) * 256], khat_s[p4 * 8 + ci][:, off:off + 256]))
            tload.append(ld(vw[:, ci * 256:(ci + 1) * 256], v_s[p4 * 8 + ci][:, off:off + 256]))
            if own:
                tload.append(ld(qw[:, ci * 256:(ci + 1) * 256], qhat_s[(p4 - 4) * 8 + ci][:, off:off + 256]))
        tload.append(ld(bw[:], beta_s[p4][:, off:off + 256]))
        if p3stop < 1:
            continue
        P.waits("tensor", tload)
        P.waits("vector", tload)

        for tj in range(2):
            rps = rot_acquire()
            TT(lambda E, tj=tj, rps=rps: E.transpose(rps[:, 0:4], bw[0:4, tj * 128:(tj + 1) * 128], ident[0:4, 0:4]), inc=True)
            P.wait("vector", ("tensor_t", P.cnt["tensor_t"]))
            V(lambda E, tj=tj, rps=rps: E.tensor_copy(bT[:, tj * 4:tj * 4 + 4], rps[:, 0:4]), inc=True)
            rot_release(("vector_t", P.cnt["vector_t"]))

        if p3stop < 2:
            continue
        for h in range(H):
            sl = sel4[:, h * 128:(h + 1) * 128]
            rps = rot_acquire()
            TT(lambda E, sl=sl, rps=rps: E.matmul(rps[:, 0:256], sl, bw[0:4, :], start=True, stop=True), inc=True)
            P.wait("vector", ("tensor_t", P.cnt["tensor_t"]))
            for ci in (2 * h, 2 * h + 1):
                V(lambda E, ci=ci, rps=rps: E.tensor_tensor(kbf[:, ci * 256:(ci + 1) * 256],
                                                            kw[:, ci * 256:(ci + 1) * 256], rps[:, 0:256], op=OP.mult))
                V(lambda E, ci=ci, h=h: E.tensor_tensor(knf[:, ci * 256:(ci + 1) * 256],
                                                        kw[:, ci * 256:(ci + 1) * 256], lamNb[h][:], op=OP.mult))
                if own:
                    V(lambda E, ci=ci, h=h: E.tensor_tensor(qtf[:, ci * 256:(ci + 1) * 256],
                                                            qw[:, ci * 256:(ci + 1) * 256], lamPb[h][:], op=OP.mult))
            V(lambda E: E.nop(), inc=True)
            rot_release(("vector_t", P.cnt["vector_t"]))

        if p3stop < 3:
            continue
        for (srct, dstt) in ((kw, kT), (vw, vT)):
            for h in range(H):
                for tj in range(2):
                    rps = rot_acquire()
                    for c2 in range(2):
                        ci = 2 * h + c2
                        TT(lambda E, srct=srct, ci=ci, tj=tj, c2=c2, rps=rps: E.transpose(
                            rps[:, c2 * 128:c2 * 128 + 128],
                            srct[:, ci * 256 + tj * 128: ci * 256 + tj * 128 + 128], ident[:]), inc=(c2 == 1))
                    P.wait("scalar", ("tensor_t", P.cnt["tensor_t"]))
                    for c2 in range(2):
                        P.ins("scalar", lambda E, dstt=dstt, h=h, tj=tj, c2=c2, rps=rps: E.copy(
                            dstt[:, (h * 2 + tj) * 256 + c2 * 128: (h * 2 + tj) * 256 + c2 * 128 + 128],
                            rps[:, c2 * 128:c2 * 128 + 128]), inc=(c2 == 1))
                    rot_release(("scalar_t", P.cnt["scalar_t"]))
        ttr3 = P.token("scalar")
        P.wait("vector", ttr3)

        if p3stop < 4:
            continue
        for h in range(H):
            for tj in range(2):
                r = (h * 2 + tj) * 256
                V(lambda E, h=h, tj=tj: E.tensor_tensor(colsc[:, 0:1], bT[:, tj * 4 + h:tj * 4 + h + 1],
                                                        rampPT[:, tj * 4 + h:tj * 4 + h + 1], op=OP.mult))
                V(lambda E, r=r: E.tensor_scalar(kbT[:, r:r + 256], kT[:, r:r + 256], colsc[:, 0:1], None, op0=OP.mult))
                V(lambda E, r=r, h=h, tj=tj: E.tensor_scalar(ktT[:, r:r + 256], kT[:, r:r + 256],
                                                             rampNT[:, tj * 4 + h:tj * 4 + h + 1], None, op0=OP.mult))
                V(lambda E, r=r, h=h, tj=tj: E.tensor_scalar(vbT[:, r:r + 256], vT[:, r:r + 256],
                                                             bT[:, tj * 4 + h:tj * 4 + h + 1], None, op0=OP.mult))
        tsc = P.token("vector")
        P.wait("tensor", tsc)
        for _ in range(3):
            p5a_slot()

        if p3stop < 5:
            continue
        rot_fence()
        def slot(h, c):
            return (32 * (c % 4), h * 64 + (c // 4) * 32)
        for h in range(H):
            for c in range(8):
                rb, cb = slot(h, c)
                for c2 in range(2):
                    ci = 2 * h + c2
                    TT(lambda E, rb=rb, cb=cb, ci=ci, c=c, c2=c2: E.matmul(
                        psW1[rb:rb + 32, cb:cb + 32],
                        kbf[:, ci * 256 + c * 32: ci * 256 + (c + 1) * 32],
                        kw[:, ci * 256 + c * 32: ci * 256 + (c + 1) * 32],
                        start=(c2 == 0), stop=(c2 == 1), tile_position=(0, rb)))
                for c2 in range(2):
                    ci = 2 * h + c2
                    TT(lambda E, rb=rb, cb=cb, ci=ci, c=c, c2=c2: E.matmul(
                        psW1[rb:rb + 32, 256 + cb: 256 + cb + 32],
                        kw[:, ci * 256 + c * 32: ci * 256 + (c + 1) * 32],
                        kbf[:, ci * 256 + c * 32: ci * 256 + (c + 1) * 32],
                        start=(c2 == 0), stop=(c2 == 1), tile_position=(0, rb)),
                       inc=(h == H - 1 and c == 7 and c2 == 1))
        tA = ("tensor_t", P.cnt["tensor_t"])
        P.wait("vector", tA)
        V(lambda E: E.tensor_tensor(Lst[:], maskl32r[:], psW1[:, 0:256], op=OP.mult))
        V(lambda E: E.tensor_scalar(Lst[:], Lst[:], -1.0, None, op0=OP.mult))
        V(lambda E: E.tensor_tensor(Ust[:], masku32r[:], psW1[:, 256:512], op=OP.mult))
        V(lambda E: E.tensor_scalar(Ust[:], Ust[:], -1.0, None, op0=OP.mult))
        V(lambda E: E.tensor_tensor(Dst[:], Ust[:], i32r[:], op=OP.add))
        V(lambda E: E.tensor_copy(Up[:], Ust[:]))
        V(lambda E: E.tensor_copy(Lp[:], Lst[:]))
        V(lambda E: E.nop(), inc=True)
        P.wait("tensor", ("vector_t", P.cnt["vector_t"]))

        if p3stop < 6:
            continue
        tiles_by_cc = [[(h, c) for h in range(H) for c in (cc, cc + 4)] for cc in range(4)]
        for lvl in range(4):
            last = (lvl == 3)
            for cc in range(4):
                for (h, c) in tiles_by_cc[cc]:
                    rb, cb = slot(h, c)
                    L_t = Lp[rb:rb + 32, cb:cb + 32]
                    U_t = Up[rb:rb + 32, cb:cb + 32]
                    TT(lambda E, L_t=L_t, U_t=U_t, rb=rb, cb=cb: E.matmul(
                        psW2[rb:rb + 32, cb:cb + 32], U_t, L_t,
                        start=True, stop=True, tile_position=(rb, rb)))
                    TT(lambda E, L_t=L_t, U_t=U_t, rb=rb, cb=cb: E.matmul(
                        psW2[rb:rb + 32, 256 + cb:256 + cb + 32], L_t, U_t,
                        start=True, stop=True, tile_position=(rb, rb)),
                       inc=(cc == 3 and (h, c) == tiles_by_cc[cc][-1]))
            tsq = ("tensor_t", P.cnt["tensor_t"])
            P.wait("vector", tsq)
            V(lambda E: E.tensor_copy(Lp[:], psW2[:, 0:256]))
            V(lambda E: E.tensor_copy(Up[:], psW2[:, 256:512]), inc=True)
            if last:
                V(lambda E: E.memset(psW3[:, 0:1024], 0.0), inc=True)
            P.wait("tensor", ("vector_t", P.cnt["vector_t"]))
            for cc in range(4):
                for (h, c) in tiles_by_cc[cc]:
                    rb, cb = slot(h, c)
                    L_t = Lp[rb:rb + 32, cb:cb + 32]
                    D_t = Dst[rb:rb + 32, cb:cb + 32]
                    i_t = i32[rb:rb + 32, 0:32]
                    if not last:
                        outap = psW3[rb:rb + 32, cb:cb + 32]
                    else:
                        outap = psW3[rb:rb + 32, (h * 8 + c) * 32:(h * 8 + c) * 32 + 32]
                    tp = (rb, rb)
                    TT(lambda E, i_t=i_t, D_t=D_t, outap=outap, tp=tp: E.matmul(
                        outap, i_t, D_t, start=True, stop=False, tile_position=tp))
                    TT(lambda E, L_t=L_t, D_t=D_t, outap=outap, tp=tp: E.matmul(
                        outap, L_t, D_t, start=False, stop=True, tile_position=tp),
                       inc=(cc == 3 and (h, c) == tiles_by_cc[cc][-1]))
            tch = ("tensor_t", P.cnt["tensor_t"])
            P.wait("vector", tch)
            if not last:
                V(lambda E: E.tensor_copy(Dst[:], psW3[:, 0:256]), inc=True)
            else:
                V(lambda E: E.tensor_copy(DfinPad[:], psW3[:, 0:1024]), inc=True)
            P.wait("tensor", ("vector_t", P.cnt["vector_t"]))

        if p3stop < 7:
            continue
        def dfinpad(h, c):
            return DfinPad[:, (h * 8 + c) * 32:(h * 8 + c) * 32 + 32]

        for h in range(H):
            for tj in range(2):
                rps = rot_acquire()
                for cc in range(4):
                    c = tj * 4 + cc
                    TT(lambda E, h=h, c=c, cc=cc, tj=tj, rps=rps: E.matmul(
                        rps[32 * cc:32 * cc + 32, 0:256],
                        dfinpad(h, c),
                        vbT[:, (h * 2 + tj) * 256:(h * 2 + tj) * 256 + 256],
                        start=True, stop=True, tile_position=(0, 32 * cc)), inc=(cc == 3))
                P.wait("vector", ("tensor_t", P.cnt["tensor_t"]))
                V(lambda E, h=h, tj=tj, rps=rps: E.tensor_copy(ut[:, (h * 2 + tj) * 256:(h * 2 + tj) * 256 + 256],
                                                               rps[:, 0:256]), inc=True)
                rot_release(("vector_t", P.cnt["vector_t"]))
            for dkh in range(2):
                rps = rot_acquire()
                for c in range(8):
                    tj = c // 4
                    TT(lambda E, h=h, c=c, tj=tj, dkh=dkh, rps=rps: E.matmul(
                        rps[:, c * 32:(c + 1) * 32],
                        kbT[:, (h * 2 + tj) * 256 + dkh * 128: (h * 2 + tj) * 256 + dkh * 128 + 128],
                        dfinpad(h, c),
                        start=True, stop=True), inc=(c == 7))
                P.wait("vector", ("tensor_t", P.cnt["tensor_t"]))
                V(lambda E, h=h, dkh=dkh, rps=rps: E.tensor_copy(wf[:, (h * 2 + dkh) * 256:(h * 2 + dkh) * 256 + 256],
                                                                 rps[:, 0:256]), inc=True)
                rot_release(("vector_t", P.cnt["vector_t"]))

        if p3stop < 8:
            continue
        for _ in range(3):
            p5a_slot()
        for h in range(H):
            for tj in range(2):
                rps = rot_acquire()
                for dkh in range(2):
                    TT(lambda E, h=h, tj=tj, dkh=dkh, rps=rps: E.matmul(
                        rps[:, 0:256],
                        wf[:, (h * 2 + dkh) * 256 + tj * 128: (h * 2 + dkh) * 256 + tj * 128 + 128],
                        S_t[:, (h * 2 + dkh) * 256:(h * 2 + dkh) * 256 + 256],
                        start=(dkh == 0), stop=(dkh == 1)), inc=(dkh == 1))
                P.wait("vector", ("tensor_t", P.cnt["tensor_t"]))
                V(lambda E, h=h, tj=tj, rps=rps: E.tensor_copy(WSt[:, (h * 2 + tj) * 256:(h * 2 + tj) * 256 + 256],
                                                               rps[:, 0:256]), inc=True)
                rot_release(("vector_t", P.cnt["vector_t"]))
                if own:
                    rps = rot_acquire()
                    for dkh in range(2):
                        TT(lambda E, h=h, tj=tj, dkh=dkh, rps=rps: E.matmul(
                            rps[:, 0:256],
                            qtf[:, (2 * h + dkh) * 256 + tj * 128: (2 * h + dkh) * 256 + tj * 128 + 128],
                            S_t[:, (h * 2 + dkh) * 256:(h * 2 + dkh) * 256 + 256],
                            start=(dkh == 0), stop=(dkh == 1)), inc=(dkh == 1))
                    P.wait("vector", ("tensor_t", P.cnt["tensor_t"]))
                    V(lambda E, h=h, tj=tj, rps=rps: E.tensor_copy(QSt[:, (h * 2 + tj) * 256:(h * 2 + tj) * 256 + 256],
                                                                   rps[:, 0:256]), inc=True)
                    rot_release(("vector_t", P.cnt["vector_t"]))

        if p3stop < 9:
            continue
        V(lambda E: E.tensor_tensor(upb[:], ut[:], WSt[:], op=OP.subtract), inc=True)
        tup0 = ("vector_t", P.cnt["vector_t"])
        P.wait("tensor", tup0)

        if p3stop < 10:
            continue
        for h in range(H):
            for tl in range(2):
                rps = rot_acquire()
                for dkh in range(2):
                    TT(lambda E, h=h, tl=tl, dkh=dkh, rps=rps: E.matmul(
                        rps[:, 0:256],
                        knf[:, (2 * h + dkh) * 256 + tl * 128: (2 * h + dkh) * 256 + tl * 128 + 128],
                        wf[:, (h * 2 + dkh) * 256:(h * 2 + dkh) * 256 + 256],
                        start=(dkh == 0), stop=(dkh == 1)), inc=(dkh == 1))
                P.wait("vector", ("tensor_t", P.cnt["tensor_t"]))
                V(lambda E, tl=tl, rps=rps: E.tensor_copy(PTt[:, tl * 256:(tl + 1) * 256], rps[:, 0:256]), inc=True)
                rot_release(("vector_t", P.cnt["vector_t"]))
            if own:
                for tl in range(2):
                    rps = rot_acquire()
                    for dkh in range(2):
                        TT(lambda E, h=h, tl=tl, dkh=dkh, rps=rps: E.matmul(
                            rps[:, 0:256],
                            knf[:, (2 * h + dkh) * 256 + tl * 128: (2 * h + dkh) * 256 + tl * 128 + 128],
                            qtf[:, (2 * h + dkh) * 256:(2 * h + dkh) * 256 + 256],
                            start=(dkh == 0), stop=(dkh == 1)), inc=(dkh == 1))
                    P.wait("vector", ("tensor_t", P.cnt["tensor_t"]))
                    V(lambda E, h=h, tl=tl, rps=rps: E.tensor_tensor(GTt[:, tl * 256:(tl + 1) * 256],
                                                                     rps[:, 0:256], ghmask[h][tl][:], op=OP.mult), inc=True)
                    rot_release(("vector_t", P.cnt["vector_t"]))
            rot_fence()
            for j in range(1, WIN):
                nk = min(32 * j, 128)
                TT(lambda E, h=h, j=j, nk=nk: E.matmul(
                    psW1[32 * (j % 4):32 * (j % 4) + 32, 0:256],
                    PTt[0:nk, 0 * 256 + j * 32: 0 * 256 + j * 32 + 32],
                    upb[0:nk, (h * 2 + 0) * 256:(h * 2 + 0) * 256 + 256],
                    start=True, stop=(j <= 4), tile_position=(0, 32 * (j % 4))), inc=(j <= 4))
                if j > 4:
                    nk2 = 32 * j - 128
                    TT(lambda E, h=h, j=j, nk2=nk2: E.matmul(
                        psW1[32 * (j % 4):32 * (j % 4) + 32, 0:256],
                        PTt[0:nk2, 1 * 256 + j * 32: 1 * 256 + j * 32 + 32],
                        upb[0:nk2, (h * 2 + 1) * 256:(h * 2 + 1) * 256 + 256],
                        start=False, stop=True, tile_position=(0, 32 * (j % 4))), inc=True)
                P.wait("vector", ("tensor_t", P.cnt["tensor_t"]))
                V(lambda E, h=h, j=j: E.scalar_tensor_tensor(
                    upb[32 * (j % 4):32 * (j % 4) + 32, (h * 2 + j // 4) * 256 + 0:(h * 2 + j // 4) * 256 + 256],
                    psW1[32 * (j % 4):32 * (j % 4) + 32, 0:256],
                    lamcol[32 * (j % 4):32 * (j % 4) + 32, 4 * h:4 * h + 1],
                    upb[32 * (j % 4):32 * (j % 4) + 32, (h * 2 + j // 4) * 256 + 0:(h * 2 + j // 4) * 256 + 256],
                    op0=OP.mult, op1=OP.add), inc=True)
                P.wait("tensor", ("vector_t", P.cnt["vector_t"]))
            if own:
                for tjo in range(2):
                    for tl in range(2):
                        TT(lambda E, h=h, tjo=tjo, tl=tl: E.matmul(
                            psW2[:, 0:256],
                            GTt[:, tl * 256 + tjo * 128: tl * 256 + tjo * 128 + 128],
                            upb[:, (h * 2 + tl) * 256:(h * 2 + tl) * 256 + 256],
                            start=(tl == 0), stop=(tl == 1)), inc=(tl == 1))
                    P.wait("vector", ("tensor_t", P.cnt["tensor_t"]))
                    V(lambda E, h=h, tjo=tjo: E.scalar_tensor_tensor(
                        osb[:], psW2[:, 0:256], 1.0,
                        QSt[:, (h * 2 + tjo) * 256:(h * 2 + tjo) * 256 + 256],
                        op0=OP.mult, op1=OP.add), inc=True)
                    tosb = ("vector_t", P.cnt["vector_t"])
                    P.wait("tensor", tosb)
                    P.wait("gpsimd", tosb)
                    st(delta_s[W0 - 2048 + tjo * 128: W0 - 2048 + (tjo + 1) * 128, h * 256:(h + 1) * 256], osb[:])
                    P.wait("vector", ("gpsimd_d", P.cnt["gpsimd_d"]))
            for dkm in range(2):
                for tj in range(2):
                    TT(lambda E, h=h, dkm=dkm, tj=tj: E.matmul(
                        psW2[:, 256:512],
                        ktT[:, (h * 2 + tj) * 256 + dkm * 128: (h * 2 + tj) * 256 + dkm * 128 + 128],
                        upb[:, (h * 2 + tj) * 256:(h * 2 + tj) * 256 + 256],
                        start=(tj == 0), stop=(tj == 1)), inc=(tj == 1))
                P.wait("vector", ("tensor_t", P.cnt["tensor_t"]))
                V(lambda E, h=h, dkm=dkm: E.scalar_tensor_tensor(
                    S_t[:, (h * 2 + dkm) * 256:(h * 2 + dkm) * 256 + 256],
                    S_t[:, (h * 2 + dkm) * 256:(h * 2 + dkm) * 256 + 256],
                    lamcol[:, 4 * h + 1:4 * h + 2],
                    psW2[:, 256:512], op0=OP.mult, op1=OP.add))
                V(lambda E, h=h, dkm=dkm: E.tensor_scalar(
                    S_t[:, (h * 2 + dkm) * 256:(h * 2 + dkm) * 256 + 256],
                    S_t[:, (h * 2 + dkm) * 256:(h * 2 + dkm) * 256 + 256],
                    lamcol[:, 4 * h + 2:4 * h + 3], None, op0=OP.mult), inc=True)
                P.wait("tensor", ("vector_t", P.cnt["vector_t"]))
        win_free[0] = P.token("vector")
        for _ in range(3):
            p5a_slot()

    while p5c["tr"] < NJOB:
        p5a_slot(drain=True)
    P.wait("scalar", ("scalar_d", P.cnt["scalar_d"]))
    P.barrier()
    es3.close()
    if upto in ("P3", "P3w1"):
        P.finalize(block)
        es.close()
        return nc

    es4 = contextlib.ExitStack()
    sb4 = lambda name, p, f, dt=F32: es4.enter_context(nc.sbuf_tensor(name, [p, f], dt))
    ps4 = lambda name, p, f: es4.enter_context(nc.psum_tensor(name, [p, f], F32))

    w1_t = sb4("w1_t", 128, 9 * 2048)
    wo_t2 = sb4("wo_t2", 128, 8192)
    w2_t = sb4("w2_t", 128, 16 * 16)
    b1c = sb4("b1c_t", 128, 16)
    b2t = sb4("b2t_t", 128, 16)
    normw_t = sb4("normw_tt", 128, 1024)
    rtempt = sb4("rtempt", 128, 16)
    floort = sb4("floort", 128, 16)
    tmp41 = sb4("tmp41", 4, 16)
    vwin = sb4("vwin", 128, 8 * 318)
    shrt = sb4("shrt", 128, 8 * 256)
    lng = sb4("lng", 128, 8 * 256)
    xnw = sb4("xnw", 128, 8 * 256)
    dtm = sb4("dtm", 128, 1024)
    stm4 = sb4("stm4", 128, 1024)
    ltm4 = sb4("ltm4", 128, 1024)
    vtm4 = sb4("vtm4", 128, 1024)
    sq4 = sb4("sq4", 128, 1024)
    statc = sb4("statc", 128, 64)
    stF = sb4("stF", 64, 128)
    hmid2 = [sb4("hmidA", 128, 128), sb4("hmidB", 128, 128)]
    tmpP = sb4("tmpP", 128, 256)
    p4rot = {"g": 0, "ev": {}}
    p4banks_ref = []
    ptile = sb4("ptile", 128, 16)
    otm = sb4("otm", 128, 1024)
    onF = sb4("onF", 128, 8 * 128)
    xtile = sb4("xtile", 128, 1024)
    y1t = otm
    mcol = sb4("mcol", 128, 16)

    psQ0 = ps4("psQ0", 128, 512)
    psQ1 = ps4("psQ1", 128, 512)

    t4l = []
    for ki in range(8):
        t4l.append(ld(w1_t[:, ki * 2048:(ki + 1) * 2048], IN["w1"][ki * 128:(ki + 1) * 128, :]))
    t4l.append(ld(w1_t[0:64, 8 * 2048:9 * 2048], IN["w1"][1024:1088, :]))
    for ki in range(8):
        t4l.append(ld(wo_t2[:, ki * 1024:(ki + 1) * 1024], IN["wo"][ki * 128:(ki + 1) * 128, :]))
    for ki in range(16):
        t4l.append(ld(w2_t[:, ki * 16:(ki + 1) * 16], IN["w2"][ki * 128:(ki + 1) * 128, :]))
    t4l.append(ld(b1c[:], IN["b1c"]))
    t4l.append(ld(b2t[:], IN["b2t"]))
    t4l.append(ld(normw_t[:], IN["normw_t"]))
    t4l.append(ld(tmp41[0:4, 0:1], IN["ltemp"]))
    t4l.append(ld(misc[0:1, 0:16], IN["floorp"]))
    P.waits("scalar", t4l)
    P.waits("vector", t4l)
    P.waits("tensor", t4l)
    P.ins("scalar", lambda E: E.activation(tmp41[0:4, 1:2], tmp41[0:4, 0:1], AF.Exp), inc=True)
    P.wait("vector", ("scalar_t", P.cnt["scalar_t"]))
    V(lambda E: E.reciprocal(tmp41[0:4, 2:3], tmp41[0:4, 1:2]))
    V(lambda E: E.nop(), inc=True)
    P.wait("tensor", ("vector_t", P.cnt["vector_t"]))
    for h in range(H):
        TT(lambda E, h=h: E.matmul(psQ0[:, h:h + 1], sel4[:, h * 128:(h + 1) * 128], tmp41[0:4, 2:3],
                                   start=True, stop=True), inc=(h == 3))
    P.wait("vector", ("tensor_t", P.cnt["tensor_t"]))
    for h in range(H):
        V(lambda E, h=h: E.tensor_copy(rtempt[:, 4 * h:4 * h + 4], bcast_ap(psQ0[:, h:h + 1], 4)))
    V(lambda E: E.nop(), inc=True)
    P.wait("tensor", ("vector_t", P.cnt["vector_t"]))
    TT(lambda E: E.matmul(psQ0[:, 0:16], ones128[0:1, :], misc[0:1, 0:16], start=True, stop=True), inc=True)
    P.wait("scalar", ("tensor_t", P.cnt["tensor_t"]))
    P.ins("scalar", lambda E: E.activation(floort[:], psQ0[:, 0:16], AF.Sigmoid), inc=True)
    P.wait("vector", ("scalar_t", P.cnt["scalar_t"]))
    V(lambda E: E.tensor_scalar(floort[:], floort[:], 0.075, None, op0=OP.mult))
    tb4 = P.token("vector")
    P.waits("tensor", [tb4])
    P.waits("sync", [tb4])

    for ti in range(8):
        T0 = ti * 256
        tl4 = []
        for ci in range(8):
            lo = T0 - 62
            for (seg0, seg1) in ((lo, T0), (T0, T0 + 256)):
                dst0 = seg0 - lo
                t = seg0
                while t < seg1:
                    if t < 0:
                        src_row = 3 * 8 + ci
                        col = 512 + t
                        n = min(seg1, 0) - t
                    else:
                        src_row = (4 + t // 512) * 8 + ci
                        col = t % 512
                        n = min(seg1 - t, 512 - col)
                    tl4.append(ld(vwin[:, ci * 318 + (t - lo): ci * 318 + (t - lo) + n],
                                  v_s[src_row][:, col:col + n]))
                    t += n
            tl4.append(ld(xnw[:, ci * 256:(ci + 1) * 256],
                          xnT_s[(ti // 2) * 8 + ci][:, (ti % 2) * 256:(ti % 2) * 256 + 256]))
        P.waits("vector", tl4)
        if ti == 0:
            V(lambda E: E.tensor_scalar(
                vwin[:].rearrange("p (c x) -> p c x", x=318)[:, :, 0:62],
                vwin[:].rearrange("p (c x) -> p c x", x=318)[:, :, 0:62],
                flag[:, 0:1], None, op0=OP.mult))
        for ci in range(8):
            base = ci * 318
            V(lambda E, ci=ci, base=base: E.tensor_scalar(
                shrt[:, ci * 256:(ci + 1) * 256], vwin[:, base + 60: base + 316],
                firw_s[:, ci * 3 + 0: ci * 3 + 1], None, op0=OP.mult))
            for j in (1, 2):
                V(lambda E, ci=ci, base=base, j=j: E.scalar_tensor_tensor(
                    shrt[:, ci * 256:(ci + 1) * 256], vwin[:, base + 60 + j: base + 316 + j],
                    firw_s[:, ci * 3 + j: ci * 3 + j + 1], shrt[:, ci * 256:(ci + 1) * 256],
                    op0=OP.mult, op1=OP.add))
            V(lambda E, ci=ci, base=base: E.tensor_scalar(
                lng[:, ci * 256:(ci + 1) * 256], vwin[:, base + 0: base + 256],
                firw_l[:, ci * 63 + 0: ci * 63 + 1], None, op0=OP.mult))
            for j in range(1, 63):
                V(lambda E, ci=ci, base=base, j=j: E.scalar_tensor_tensor(
                    lng[:, ci * 256:(ci + 1) * 256], vwin[:, base + j: base + 256 + j],
                    firw_l[:, ci * 63 + j: ci * 63 + j + 1], lng[:, ci * 256:(ci + 1) * 256],
                    op0=OP.mult, op1=OP.add))
        tfir = P.token("vector")
        P.wait("tensor", tfir)
        P.wait("sync", tfir)

        for s in range(2):
            tdl = ld(dtm[:], delta_s[T0 + s * 128: T0 + (s + 1) * 128, :])
            jobs = []
            for (srct, dstt, srcw, hal) in ((shrt, stm4, 256, 0), (lng, ltm4, 256, 0), (vwin, vtm4, 318, 62)):
                for ci in range(8):
                    jobs.append((srct, dstt, ci * srcw + hal + s * 128, ci))
            for j0 in range(0, len(jobs), 4):
                grp = jobs[j0:j0 + 4]
                g = p4rot["g"]
                p4rot["g"] += 1
                bank = [psQ0, psB0, psB1][g % 3]
                P.wait("tensor", p4rot["ev"].get(g - 3))
                sname = "p5at%d" % (g % 2)
                for gi, (srct, dstt, off, ci) in enumerate(grp):
                    if gi == len(grp) - 1:
                        P.cnt[sname] += 1
                        P.ins("tensor", lambda E, srct=srct, off=off, gi=gi, bank=bank, sname=sname: E.transpose(
                            bank[:, gi * 128:(gi + 1) * 128], srct[:, off: off + 128], ident[:]).then_inc(P.sem[sname], 1))
                    else:
                        TT(lambda E, srct=srct, off=off, gi=gi, bank=bank: E.transpose(
                            bank[:, gi * 128:(gi + 1) * 128], srct[:, off: off + 128], ident[:]))
                P.wait("scalar", (sname, P.cnt[sname]))
                for gi, (srct, dstt, off, ci) in enumerate(grp):
                    P.ins("scalar", lambda E, dstt=dstt, ci=ci, gi=gi, bank=bank: E.copy(
                        dstt[:, ci * 128:(ci + 1) * 128], bank[:, gi * 128:(gi + 1) * 128]),
                          inc=(gi == len(grp) - 1))
                p4rot["ev"][g] = ("scalar_t", P.cnt["scalar_t"])
            ttr4 = P.token("scalar")
            P.waits("vector", [ttr4, tdl])

            for (tix, tsr) in enumerate((stm4, ltm4, dtm, vtm4)):
                tt = tsr[:]
                tt3 = tt.rearrange("p (h d) -> p h d", d=256)
                V(lambda E: E.drain())
                V(lambda E, tt3=tt3, tix=tix: E.tensor_reduce(statc[:, tix * 16 + 0:tix * 16 + 4], tt3, axis=AXX, op=OP.add))
                V(lambda E, tt=tt: E.tensor_tensor(sq4[:], tt, tt, op=OP.mult))
                V(lambda E: E.drain())
                V(lambda E, tix=tix: E.tensor_reduce(statc[:, tix * 16 + 4:tix * 16 + 8],
                                                     sq4[:].rearrange("p (h d) -> p h d", d=256), axis=AXX, op=OP.add))
                V(lambda E: E.drain())
                V(lambda E, tt3=tt3, tix=tix: E.tensor_reduce(statc[:, tix * 16 + 8:tix * 16 + 12], tt3, axis=AXX, op=OP.max))
            for tix in range(4):
                c0 = tix * 16
                V(lambda E, c0=c0: E.tensor_scalar(statc[:, c0 + 0:c0 + 4], statc[:, c0 + 0:c0 + 4], 1.0 / 256, None, op0=OP.mult))
                V(lambda E, c0=c0: E.tensor_tensor(mcol[:, 0:4], statc[:, c0 + 0:c0 + 4], statc[:, c0 + 0:c0 + 4], op=OP.mult))
                V(lambda E, c0=c0: E.tensor_scalar(statc[:, c0 + 12:c0 + 16], statc[:, c0 + 4:c0 + 8], 1.0 / 256, None, op0=OP.mult))
                V(lambda E, c0=c0: E.tensor_tensor(statc[:, c0 + 12:c0 + 16], statc[:, c0 + 12:c0 + 16], mcol[:, 0:4], op=OP.subtract))
                V(lambda E: E.nop(), inc=True)
                P.wait("scalar", ("vector_t", P.cnt["vector_t"]))
                P.ins("scalar", lambda E, c0=c0: E.activation(statc[:, c0 + 4:c0 + 8], statc[:, c0 + 4:c0 + 8], AF.Sqrt), inc=True)
                P.wait("vector", ("scalar_t", P.cnt["scalar_t"]))
            for tix in range(4):
                c0 = tix * 16
                for h in range(H):
                    V(lambda E, c0=c0, h=h: E.tensor_copy(sq4[:, c0 + 4 * h + 0:c0 + 4 * h + 1], statc[:, c0 + 0 + h:c0 + 1 + h]))
                    V(lambda E, c0=c0, h=h: E.tensor_copy(sq4[:, c0 + 4 * h + 1:c0 + 4 * h + 2], statc[:, c0 + 12 + h:c0 + 13 + h]))
                    V(lambda E, c0=c0, h=h: E.tensor_copy(sq4[:, c0 + 4 * h + 2:c0 + 4 * h + 3], statc[:, c0 + 8 + h:c0 + 9 + h]))
                    V(lambda E, c0=c0, h=h: E.tensor_copy(sq4[:, c0 + 4 * h + 3:c0 + 4 * h + 4], statc[:, c0 + 4 + h:c0 + 5 + h]))
            V(lambda E: E.nop(), inc=True)
            P.wait("tensor", ("vector_t", P.cnt["vector_t"]))
            TT(lambda E: E.transpose(psQ1[0:64, 0:128], sq4[:, 0:64], ident[:]), inc=True)
            P.wait("scalar", ("tensor_t", P.cnt["tensor_t"]))
            P.ins("scalar", lambda E: E.copy(stF[0:64, 0:128], psQ1[0:64, 0:128]), inc=True)
            P.wait("tensor", ("scalar_t", P.cnt["scalar_t"]))

            g_tok = {}
            w1tok = {}
            gbanks = [psQ0, psB0, psB1]
            P.wait("tensor", p4rot["ev"][p4rot["g"] - 1])
            for mi in range(17):
                if mi < 16:
                    bank = gbanks[mi % 3]
                    P.wait("tensor", g_tok.get(mi - 3))
                    for ki in range(8):
                        TT(lambda E, mi=mi, ki=ki, s=s, bank=bank: E.matmul(
                            bank[:, 0:128], w1_t[:, ki * 2048 + mi * 128: ki * 2048 + (mi + 1) * 128],
                            xnw[:, ki * 256 + s * 128: ki * 256 + s * 128 + 128], start=(ki == 0), stop=False))
                    TT(lambda E, mi=mi, bank=bank: E.matmul(
                        bank[:, 0:128], w1_t[0:64, 8 * 2048 + mi * 128: 8 * 2048 + (mi + 1) * 128],
                        stF[0:64, 0:128], start=False, stop=True), inc=True)
                    w1tok[mi] = ("tensor_t", P.cnt["tensor_t"])
                if mi >= 1:
                    mp = mi - 1
                    P.wait("tensor", g_tok[mp])
                    P.cnt["w2s"] += 1
                    P.ins("tensor", lambda E, mp=mp: E.matmul(
                        psQ1[:, 0:16], hmid2[mp % 2][:],
                        w2_t[:, mp * 16:(mp + 1) * 16], start=(mp == 0), stop=(mp == 15)).then_inc(P.sem["w2s"], 1))
                if mi < 16:
                    P.wait("scalar", w1tok[mi])
                    if mi >= 2:
                        P.wait("scalar", ("w2s", P.cnt["w2s"]))
                    P.ins("scalar", lambda E, mi=mi, bank=gbanks[mi % 3]: E.activation(
                        hmid2[mi % 2][:], bank[:, 0:128], AF.Gelu,
                        bias=b1c[:, mi:mi + 1], scale=1.0), inc=True)
                    g_tok[mi] = ("scalar_t", P.cnt["scalar_t"])
            P.wait("vector", ("w2s", P.cnt["w2s"]))
            pt = ptile[:]
            V(lambda E, pt=pt: E.tensor_tensor(pt, b2t[:], psQ1[:, 0:16], op=OP.add))
            V(lambda E, pt=pt: E.tensor_tensor(pt, pt, rtempt[:], op=OP.mult))
            V(lambda E: E.drain())
            V(lambda E, pt=pt: E.tensor_reduce(mcol[:, 0:4], pt.rearrange("p (h g) -> p h g", g=4), axis=AXX, op=OP.max))
            for h in range(H):
                V(lambda E, pt=pt, h=h: E.tensor_scalar(pt[:, 4 * h:4 * h + 4], pt[:, 4 * h:4 * h + 4],
                                                        mcol[:, h:h + 1], None, op0=OP.subtract))
            V(lambda E: E.nop(), inc=True)
            P.wait("scalar", ("vector_t", P.cnt["vector_t"]))
            P.ins("scalar", lambda E, pt=pt: E.activation(pt, pt, AF.Exp), inc=True)
            P.wait("vector", ("scalar_t", P.cnt["scalar_t"]))
            V(lambda E: E.drain())
            V(lambda E, pt=pt: E.tensor_reduce(mcol[:, 0:4], pt.rearrange("p (h g) -> p h g", g=4), axis=AXX, op=OP.add))
            V(lambda E: E.reciprocal(mcol[:, 4:8], mcol[:, 0:4]))
            V(lambda E: E.drain())
            for h in range(H):
                V(lambda E, pt=pt, h=h: E.tensor_scalar(pt[:, 4 * h:4 * h + 4], pt[:, 4 * h:4 * h + 4],
                                                        mcol[:, 4 + h:5 + h], None, op0=OP.mult))
            V(lambda E, pt=pt: E.tensor_tensor(pt, pt, floort[:], op=OP.max))
            V(lambda E: E.drain())
            V(lambda E, pt=pt: E.tensor_reduce(mcol[:, 0:4], pt.rearrange("p (h g) -> p h g", g=4), axis=AXX, op=OP.add))
            V(lambda E: E.reciprocal(mcol[:, 4:8], mcol[:, 0:4]))
            V(lambda E: E.drain())
            for h in range(H):
                V(lambda E, pt=pt, h=h: E.tensor_scalar(pt[:, 4 * h:4 * h + 4], pt[:, 4 * h:4 * h + 4],
                                                        mcol[:, 4 + h:5 + h], None, op0=OP.mult))

            for h in range(H):
                reg = h * 256
                V(lambda E, pt=pt, reg=reg, h=h: E.tensor_scalar(
                    otm[:, reg:reg + 256], stm4[:, reg:reg + 256], pt[:, 4 * h + 0:4 * h + 1], None, op0=OP.mult))
                for (g, tsr) in ((1, ltm4), (2, dtm), (3, vtm4)):
                    V(lambda E, pt=pt, reg=reg, h=h, g=g, tsr=tsr: E.scalar_tensor_tensor(
                        otm[:, reg:reg + 256], tsr[:, reg:reg + 256], pt[:, 4 * h + g:4 * h + g + 1],
                        otm[:, reg:reg + 256], op0=OP.mult, op1=OP.add))
            V(lambda E: E.tensor_tensor(sq4[:], otm[:], otm[:], op=OP.mult))
            V(lambda E: E.drain())
            V(lambda E: E.tensor_reduce(mcol[:, 0:4], sq4[:].rearrange("p (h d) -> p h d", d=256), axis=AXX, op=OP.add))
            V(lambda E: E.tensor_scalar(mcol[:, 0:4], mcol[:, 0:4], 1.0 / 256, 1e-5, op0=OP.mult, op1=OP.add))
            V(lambda E: E.nop(), inc=True)
            P.wait("scalar", ("vector_t", P.cnt["vector_t"]))
            P.ins("scalar", lambda E: E.activation(mcol[:, 4:8], mcol[:, 0:4], AF.Sqrt), inc=True)
            P.wait("vector", ("scalar_t", P.cnt["scalar_t"]))
            V(lambda E: E.reciprocal(mcol[:, 8:12], mcol[:, 4:8]))
            V(lambda E: E.drain())
            for h in range(H):
                reg = h * 256
                V(lambda E, reg=reg, h=h: E.tensor_scalar(
                    otm[:, reg:reg + 256], otm[:, reg:reg + 256], mcol[:, 8 + h:9 + h], None, op0=OP.mult))
            V(lambda E: E.tensor_tensor(otm[:], otm[:], normw_t[:], op=OP.mult))
            tcomb = P.token("vector")
            P.wait("tensor", tcomb)

            for cg in range(2):
                g = p4rot["g"]
                p4rot["g"] += 1
                bank = [psQ0, psB0, psB1][g % 3]
                P.wait("tensor", p4rot["ev"].get(g - 3))
                sname = "p5at%d" % (g % 2)
                for cj in range(4):
                    ci = cg * 4 + cj
                    if cj == 3:
                        P.cnt[sname] += 1
                        P.ins("tensor", lambda E, ci=ci, cj=cj, bank=bank, sname=sname: E.transpose(
                            bank[:, cj * 128:(cj + 1) * 128], otm[:, ci * 128:(ci + 1) * 128], ident[:]).then_inc(P.sem[sname], 1))
                    else:
                        TT(lambda E, ci=ci, cj=cj, bank=bank: E.transpose(
                            bank[:, cj * 128:(cj + 1) * 128], otm[:, ci * 128:(ci + 1) * 128], ident[:]))
                P.wait("scalar", (sname, P.cnt[sname]))
                for cj in range(4):
                    ci = cg * 4 + cj
                    P.ins("scalar", lambda E, ci=ci, cj=cj, bank=bank: E.copy(
                        onF[:, ci * 128:(ci + 1) * 128], bank[:, cj * 128:(cj + 1) * 128]), inc=(cj == 3))
                p4rot["ev"][g] = ("scalar_t", P.cnt["scalar_t"])
            P.wait("tensor", p4rot["ev"][p4rot["g"] - 1])

            tx4 = ld(xtile[:], IN["x12"][2048 + T0 + s * 128: 2048 + T0 + (s + 1) * 128, :])
            for nn in range(2):
                for ki in range(8):
                    TT(lambda E, nn=nn, ki=ki: E.matmul(
                        psQ1[:, 0:512], onF[:, ki * 128:(ki + 1) * 128],
                        wo_t2[:, ki * 1024 + nn * 512: ki * 1024 + nn * 512 + 512],
                        start=(ki == 0), stop=(ki == 7)), inc=(ki == 7))
                P.waits("vector", [("tensor_t", P.cnt["tensor_t"]), tx4])
                V(lambda E, nn=nn: E.tensor_tensor(y1t[:, nn * 512:(nn + 1) * 512],
                                                   xtile[:, nn * 512:(nn + 1) * 512], psQ1[:, 0:512], op=OP.add), inc=True)
                P.wait("tensor", ("vector_t", P.cnt["vector_t"]))
            ty1 = P.token("vector")
            P.wait("scalar", ty1)
            tspill = st(y1_s[T0 + s * 128: T0 + (s + 1) * 128, :], y1t[:], eng="scalar")
            P.wait("vector", tspill)
            P.wait("sync", tspill)
        tend = P.token("vector")
        P.wait("sync", tend)

    P.barrier()
    es4.close()
    if upto == "P4":
        P.finalize(block)
        es.close()
        return nc

    es5 = contextlib.ExitStack()
    sb5 = lambda name, p, f, dt=F32: es5.enter_context(nc.sbuf_tensor(name, [p, f], dt))
    ps5 = lambda name, p, f: es5.enter_context(nc.psum_tensor(name, [p, f], F32))

    es5b = contextlib.ExitStack()
    sb5b = lambda name, p, f, dt=F32: es5b.enter_context(nc.sbuf_tensor(name, [p, f], dt))
    wbs = [sb5b("wbuf", 128, 512), sb5b("wbuf2", 128, 512), sb5b("wbuf3", 128, 512)]
    uq5 = sb5b("uq5", 128, 512)
    gq5 = sb5b("gq5", 128, 512)
    sg5 = sb5b("sg5", 128, 512)
    qr5 = sb5b("qr5", 128, 32)
    rc5 = sb5b("rc5", 128, 32)
    psF = [psB0, psB1] + [ps5("psF%d" % i, 128, 512) for i in range(6)]

    def emit_quant(buf, K):
        nb = K // 16
        bv = buf.rearrange("p (nb b) -> p nb b", b=16)
        V(lambda E: E.drain())
        V(lambda E, bv=bv, nb=nb: E.tensor_reduce(qr5[:, 0:nb], bv, axis=AXX, op=OP.max, apply_absolute_value=True))
        V(lambda E, nb=nb: E.tensor_scalar(qr5[:, 0:nb], qr5[:, 0:nb], 1.0 / 6, 1e-12, op0=OP.mult, op1=OP.max))
        V(lambda E, nb=nb: E.reciprocal(rc5[:, 0:nb], qr5[:, 0:nb]))
        V(lambda E: E.drain())
        rb = bcast_ap(rc5[:, 0:nb], 16)
        V(lambda E, bv=bv, rb=rb, K=K: E.tensor_tensor(uq5[:, 0:K].rearrange("p (nb b) -> p nb b", b=16), bv, rb, op=OP.mult))
        V(lambda E, K=K: E.tensor_scalar(sg5[:, 0:K], uq5[:, 0:K], 0.0, None, op0=OP.is_gt))
        V(lambda E, K=K: E.tensor_scalar(gq5[:, 0:K], uq5[:, 0:K], 0.0, None, op0=OP.is_lt))
        V(lambda E, K=K: E.tensor_tensor(sg5[:, 0:K], sg5[:, 0:K], gq5[:, 0:K], op=OP.subtract))
        V(lambda E, K=K: E.tensor_tensor(uq5[:, 0:K], uq5[:, 0:K], sg5[:, 0:K], op=OP.mult))
        V(lambda E, K=K: E.tensor_scalar(gq5[:, 0:K], uq5[:, 0:K], 0.25, None, op0=OP.is_gt))
        for thr in (0.75, 1.25, 1.75):
            V(lambda E, K=K, thr=thr: E.scalar_tensor_tensor(gq5[:, 0:K], uq5[:, 0:K], thr, gq5[:, 0:K], op0=OP.is_gt, op1=OP.add))
        V(lambda E, K=K: E.tensor_scalar(gq5[:, 0:K], gq5[:, 0:K], 0.5, None, op0=OP.mult))
        for thr in (2.5, 3.5, 5.0, 5.0):
            V(lambda E, K=K, thr=thr: E.scalar_tensor_tensor(gq5[:, 0:K], uq5[:, 0:K], thr, gq5[:, 0:K], op0=OP.is_gt, op1=OP.add))
        V(lambda E, K=K: E.tensor_tensor(gq5[:, 0:K], gq5[:, 0:K], sg5[:, 0:K], op=OP.mult))
        qb = bcast_ap(qr5[:, 0:nb], 16)
        V(lambda E, buf=buf, K=K, qb=qb, nb=nb: E.tensor_tensor(
            buf.rearrange("p (nb b) -> p nb b", b=16),
            gq5[:, 0:K].rearrange("p (nb b) -> p nb b", b=16), qb, op=OP.mult))

    xq5 = sb5b("xq5", 128, 8 * 2048)
    wgk = sb5b("wgk", 128, 8 * 512)
    wuk = sb5b("wuk", 128, 8 * 512)
    gsil = sb5b("gsil", 128, 512)
    hqb = [sb5b("hqbA", 128, 512), sb5b("hqbB", 128, 512)]
    t5x = []
    for ci in range(8):
        for tt in range(4):
            t5x.append(ld(xq5[:, ci * 2048 + tt * 512: ci * 2048 + (tt + 1) * 512], xqT_s[tt * 8 + ci]))
    P.waits("tensor", t5x)
    ncols = [(0, 512), (512, 512), (1024, 512), (1536, 512), (2048, 512), (2560, 256)]
    stages = [(n0, nsz, m) for (n0, nsz) in ncols for m in range(16)]
    NST = len(stages)
    pairs5 = [(0, 1), (2, 3), (5, 6)]
    ps_cons = [None, None, None]
    mmtok = {}
    qtok = {}
    trtok = {}
    for g in range(NST + 2):
        if g < NST:
            n0, nsz, m = stages[g]
            if m == 0:
                P.wait("sync", mmtok.get(g - 1))
                tlw = []
                for kc in range(8):
                    tlw.append(ld(wgk[:, kc * 512: kc * 512 + nsz], wgT_s[kc][:, n0:n0 + nsz]))
                    tlw.append(ld(wuk[:, kc * 512: kc * 512 + nsz], wuT_s[kc][:, n0:n0 + nsz]))
                P.waits("tensor", tlw)
            pr = g % 3
            a5, b5 = pairs5[pr]
            P.wait("tensor", ps_cons[pr])
            for kc in range(8):
                TT(lambda E, m=m, kc=kc, nsz=nsz, a5=a5: E.matmul(
                    psF[a5][:, 0:nsz], xq5[:, kc * 2048 + m * 128: kc * 2048 + (m + 1) * 128],
                    wgk[:, kc * 512: kc * 512 + nsz], start=(kc == 0), stop=(kc == 7)))
            for kc in range(8):
                TT(lambda E, m=m, kc=kc, nsz=nsz, b5=b5: E.matmul(
                    psF[b5][:, 0:nsz], xq5[:, kc * 2048 + m * 128: kc * 2048 + (m + 1) * 128],
                    wuk[:, kc * 512: kc * 512 + nsz], start=(kc == 0), stop=(kc == 7)), inc=(kc == 7))
            mmtok[g] = ("tensor_t", P.cnt["tensor_t"])
        if g >= 1 and g - 1 < NST:
            gp = g - 1
            n0p, nszp, mp = stages[gp]
            pr = gp % 3
            a5, b5 = pairs5[pr]
            wb = wbs[gp % 3]
            P.wait("scalar", mmtok[gp])
            P.wait("scalar", ps_cons[(gp + 2) % 3])
            P.ins("scalar", lambda E, nszp=nszp, a5=a5: E.activation(gsil[:, 0:nszp], psF[a5][:, 0:nszp], AF.Silu), inc=True)
            P.wait("vector", ("scalar_t", P.cnt["scalar_t"]))
            P.wait("vector", trtok.get(gp - 3))
            V(lambda E, nszp=nszp, b5=b5, wb=wb: E.tensor_tensor(wb[:, 0:nszp], gsil[:, 0:nszp], psF[b5][:, 0:nszp], op=OP.mult), inc=True)
            ps_cons[pr] = ("vector_t", P.cnt["vector_t"])
            emit_quant(wb[:, 0:nszp], nszp)
            qtok[gp] = P.token("vector")
        if g >= 2:
            gt = g - 2
            n0t, nszt, mt = stages[gt]
            wb = wbs[gt % 3]
            bank = psF[4] if gt % 2 == 0 else psF[7]
            P.wait("tensor", qtok[gt])
            sname = "p5at%d" % (gt % 2)
            for kc in range(nszt // 128):
                if kc == nszt // 128 - 1:
                    P.cnt[sname] += 1
                    P.ins("tensor", lambda E, kc=kc, wb=wb, bank=bank, sname=sname: E.transpose(
                        bank[:, kc * 128:(kc + 1) * 128], wb[:, kc * 128:(kc + 1) * 128], ident[:]).then_inc(P.sem[sname], 1))
                else:
                    TT(lambda E, kc=kc, wb=wb, bank=bank: E.transpose(
                        bank[:, kc * 128:(kc + 1) * 128], wb[:, kc * 128:(kc + 1) * 128], ident[:]))
            trtok[gt] = (sname, P.cnt[sname])
            P.wait("scalar", trtok[gt])
            P.wait("scalar", ("gpsimd_d", P.cnt["gpsimd_d"]))
            for kc in range(nszt // 128):
                P.ins("scalar", lambda E, kc=kc, gt=gt, bank=bank: E.copy(
                    hqb[gt % 2][:, kc * 128:(kc + 1) * 128], bank[:, kc * 128:(kc + 1) * 128]), inc=(kc == nszt // 128 - 1))
            tsp = ("scalar_t", P.cnt["scalar_t"])
            P.wait("gpsimd", tsp)
            for kc in range(nszt // 128):
                st(hqT_s[n0t // 128 + kc][:, mt * 128:(mt + 1) * 128], hqb[gt % 2][:, kc * 128:(kc + 1) * 128])
    P.wait("scalar", ("gpsimd_d", P.cnt["gpsimd_d"]))
    P.barrier()
    es5b.close()

    wdr = sb5("wdr", 128, 22 * 1024)
    hqk = [sb5("hqk0", 128, 4 * 128), sb5("hqk1", 128, 4 * 128), sb5("hqk2", 128, 4 * 128)]
    y5 = sb5("y5", 128, 1024)
    twd = []
    for kc in range(22):
        twd.append(ld(wdr[:, kc * 1024:(kc + 1) * 1024], wdT_s[kc]))
    mmt = {}
    ldt = {}

    def p5c_loads(li):
        mg, kc = li // 22, li % 22
        pb = li % 3
        P.wait("gpsimd", mmt.get(li - 3))
        t = []
        for mm_ in range(4):
            m = mg * 4 + mm_
            t.append(ld(hqk[pb][:, mm_ * 128:(mm_ + 1) * 128], hqT_s[kc][:, m * 128:(m + 1) * 128], eng="gpsimd"))
        ldt[li] = t

    p5c_loads(0)
    p5c_loads(1)
    p5c_loads(2)
    P.waits("tensor", twd)
    for mg in range(4):
        for kc in range(22):
            li = mg * 22 + kc
            pb = li % 3
            P.waits("tensor", ldt[li])
            for mm_ in range(4):
                for nn in range(2):
                    TT(lambda E, mm_=mm_, nn=nn, kc=kc, pb=pb: E.matmul(
                        psF[mm_ * 2 + nn][:, 0:512],
                        hqk[pb][:, mm_ * 128:(mm_ + 1) * 128],
                        wdr[:, kc * 1024 + nn * 512: kc * 1024 + (nn + 1) * 512],
                        start=(kc == 0), stop=(kc == 21)), inc=(mm_ == 3 and nn == 1))
            mmt[li] = ("tensor_t", P.cnt["tensor_t"])
            if li + 3 < 88:
                p5c_loads(li + 3)
        tdone = ("tensor_t", P.cnt["tensor_t"])
        for mm_ in range(4):
            m = mg * 4 + mm_
            ty = ld(y5[:], y1_s[m * 128:(m + 1) * 128, :], eng="gpsimd")
            P.waits("vector", [tdone, ty])
            for nn in range(2):
                V(lambda E, mm_=mm_, nn=nn: E.tensor_tensor(
                    y5[:, nn * 512:(nn + 1) * 512], y5[:, nn * 512:(nn + 1) * 512],
                    psF[mm_ * 2 + nn][:, 0:512], op=OP.add))
            V(lambda E: E.nop(), inc=True)
            tout = ("vector_t", P.cnt["vector_t"])
            P.wait("gpsimd", tout)
            st(out_h[m * 128:(m + 1) * 128, :], y5[:])
            P.wait("vector", ("gpsimd_d", P.cnt["gpsimd_d"]))
            P.wait("tensor", ("gpsimd_d", P.cnt["gpsimd_d"]))

    P.wait("sync", ("gpsimd_d", P.cnt["gpsimd_d"]))
    P.barrier()
    es5.close()

    P.finalize(block)
    es.close()
    return nc



def kernel(**inputs):
    maps = shard_inputs(inputs)
    nc = build_nc(upto="ALL")
    res = run_bass_kernel_spmd(nc, maps, core_ids=list(range(NCHIP)))
    out = np.zeros((4, 4096, 1024), np.float32)
    for c in range(NCHIP):
        b, half = c // 2, c % 2
        out[b, 2048 * half: 2048 * half + 2048] = res.results[c]["out"]
    return out
